# version 06
# baseline (speedup 1.0000x reference)
import contextlib
import numpy as np
import concourse.bass as bass
import concourse.mybir as mybir
from concourse.bass_utils import run_bass_kernel_spmd

F32 = mybir.dt.float32
I32 = mybir.dt.int32
AF = mybir.ActivationFunctionType
OP = mybir.AluOpType

B, S, D = 4, 4096, 1024
T = 2048
NCH = 8
TT = 512
NTT = T // TT
EPS = 1e-5
SHIFT_A = 24.0
SHIFT_B = 10.0
QSC_A = 0.125
QSC_B = float(1.0 / np.sqrt(np.float32(128.0)))
INV127 = float(np.float32(1.0) / np.float32(127.0))
PAIR_RG = [[0, 1], [2, 3], [4, 5], [6, 7]]
NEG = -1.0e30
NSCAN = S

SCAN_INTERLEAVE = True
DBG_BARRIER = False
DBG_ABAR = False
DBG_NONORM = False
TAPS = []
STOP_AFTER = None

BP = {}
_c = 0
for _name, _n in [("bout_a", 8), ("gam_a", 8), ("bet_a", 8), ("lng", 8),
                  ("lnb", 8), ("ln1g", 8), ("ln1b", 8), ("ln2g", 8),
                  ("ln2b", 8), ("lnfg", 8), ("lnfb", 8), ("bgate", 8),
                  ("bq", 8), ("bk", 8), ("bo", 8), ("b1", 32), ("b2", 8),
                  ("bf", 8), ("bdelta", 1), ("b_in", 1), ("selE", 1),
                  ("selO", 1), ("c_eps", 1), ("c_nsha", 1), ("c_nshb", 1),
                  ("c_inv127", 1), ("c_i1024", 1), ("c_neg1", 1)]:
    BP[_name] = _c
    _c += _n
NBP = _c

ENGS = ["sync", "tensor", "vector", "scalar", "gpsimd"]


class Em:
    def __init__(self, nc, stack):
        self.nc = nc
        self.q = {e: [] for e in ENGS}
        self.sem = {}
        self.cnt = {}
        for s in ["dW", "dG", "sT", "sV", "sS", "sC", "zS", "zT", "st"]:
            self.sem[s] = stack.enter_context(nc.semaphore(s))
            self.cnt[s] = 0

    def v(self, s):
        return self.cnt[s]

    def wait(self, eng, s, val=None):
        v = self.cnt[s] if val is None else val
        if v <= 0:
            return
        h = self.sem[s]
        self.q[eng].append(lambda e: e.wait_ge(h, v))

    def op(self, eng, builder, inc=None, n=1):
        if inc is None:
            self.q[eng].append(lambda e: builder(e))
            return None
        h = self.sem[inc]
        self.q[eng].append(lambda e: builder(e).then_inc(h, n))
        self.cnt[inc] += n
        return self.cnt[inc]

    def flush(self, block):
        for name in ENGS:
            ops = self.q[name]
            if not ops:
                continue

            def runner(e, ops=ops):
                for f in ops:
                    f(e)
            getattr(block, name)(runner)


def build_nc():
    nc = bass.Bass("TRN2", target_bir_lowering=False)
    di = lambda n, s, dt=F32: nc.dram_tensor(n, s, dt, kind="ExternalInput").ap()
    do = lambda n, s, dt=F32: nc.dram_tensor(n, s, dt, kind="ExternalOutput").ap()
    dd = lambda n, s, dt=F32: nc.dram_tensor(n, s, dt).ap()

    x0 = di("x0", [D, T])
    wqT = di("wqT", [D, D]); wkT = di("wkT", [D, D]); wvT = di("wvT", [D, D])
    woT = di("woT", [D, D])
    wgT = di("wgT", [D, 3072], I32); wuT = di("wuT", [D, 3072], I32)
    wdT = di("wdT", [3072, D], I32)
    Wdl = di("Wdl", [D, 16]); Win = di("Win", [D, 16])
    Wgt = di("Wgt", [D, D])
    Amat = di("Amat", [16, 16]); Cmat = di("Cmat", [16, D])
    Wq = di("Wq", [D, D]); Wk = di("Wk", [D, D]); Wv = di("Wv", [D, D])
    Wo = di("Wo", [D, D])
    W1 = di("W1", [D, 4096]); W2 = di("W2", [4096, D]); Wf = di("Wf", [2 * D, D])
    bp_in = di("bpv", [128, NBP])
    rp_in = di("rpv", [1, D])
    ident_in = di("identv", [128, 128])
    m0_in = di("m0v", [128, 128])
    m1_in = di("m1v", [128, 128])
    ones_in = di("onesv", [128, 128])

    out = do("out", [D, T])
    _tap_shapes = {"q_scr": [D, T], "ka_i": [D, T], "va_i": [T, 1040],
                   "ctxa": [T, D], "qb_scr": [D, T], "kb_i": [D, T],
                   "vb_i": [T, 1032], "ctxb": [T, D], "gate": [D, T],
                   "m_i": [48, T], "x2p": [D, T], "hall": [16, S + 1],
                   "kdbg": [128, 4096], "vdbg": [128, 4160]}
    taps = {t: do("tap_" + t, _tap_shapes.get(t, [D, T])) for t in TAPS}

    q_scr = dd("q_scr", [D, T])
    qb_scr = dd("qb_scr", [D, T])
    gate_scr = dd("gate_scr", [D, T])
    x2_scr = dd("x2_scr", [D, T])
    x2p_scr = dd("x2p_scr", [D, T])
    ctxa_scr = dd("ctxa_scr", [T, D])
    ctxb_scr = dd("ctxb_scr", [T, D])
    ka_i = dd("ka_i", [D, T]); ka_o = dd("ka_o", [2, D, T])
    va_i = dd("va_i", [T, 1040]); va_o = dd("va_o", [2, T, 1040])
    m_i = dd("m_i", [48, T]); m_o = dd("m_o", [2, 48, T])
    kb_i = dd("kb_i", [D, T]); kb_o = dd("kb_o", [2, D, T])
    vb_i = dd("vb_i", [T, 1032]); vb_o = dd("vb_o", [2, T, 1032])

    stack = contextlib.ExitStack()
    with stack:
        work = stack.enter_context(nc.sbuf_tensor("work", [128, 21536], F32))
        strm = stack.enter_context(nc.sbuf_tensor("strm", [128, NCH * T], F32))
        bp = stack.enter_context(nc.sbuf_tensor("bps", [128, NBP], F32))
        ident = stack.enter_context(nc.sbuf_tensor("identt", [128, 128], F32))
        m0 = stack.enter_context(nc.sbuf_tensor("m0t", [128, 128], F32))
        m1 = stack.enter_context(nc.sbuf_tensor("m1t", [128, 128], F32))
        ones = stack.enter_context(nc.sbuf_tensor("ones_t", [128, 128], F32))
        strow = stack.enter_context(nc.sbuf_tensor("strow", [128, 2048], F32))
        A_sb = stack.enter_context(nc.sbuf_tensor("A_sb", [16, 16], F32))
        dfull = stack.enter_context(nc.sbuf_tensor("dfull", [16, S], F32))
        btfull = stack.enter_context(nc.sbuf_tensor("btfull", [16, S], F32))
        hall = stack.enter_context(nc.sbuf_tensor("hallt", [16, S + 1], F32))
        hd = stack.enter_context(nc.sbuf_tensor("hdt", [16, 1], F32))

        pb = [stack.enter_context(nc.psum_tensor(f"pb{k}", [128, 512], F32))
              for k in range(8)]

        for cname, cval in [("c_eps", EPS), ("c_nsha", -SHIFT_A),
                            ("c_nshb", -SHIFT_B), ("c_inv127", INV127),
                            ("c_i1024", 1.0 / 1024.0), ("c_neg1", -1.0)]:
            nc.const_aps.register(F32, cval, bp[:, BP[cname]:BP[cname] + 1])

        em = Em(nc, stack)
        block = stack.enter_context(nc.Block())

        ST_MU2X, ST_R2X = (0, 0), (32, 0)
        ST_MUT, ST_RT = (64, 0), (64, 512)
        ST_BV = (64, 1024)
        ST_A, ST_B = (64, 1024), (64, 1536)

        def srow(slot, off, n):
            r, c = slot
            return strow[r:r + 1, c + off:c + off + n]

        def sl(slot, off):
            return (slot[0], slot[1] + off)

        def bpc(name, k=0):
            return bp[:, BP[name] + k:BP[name] + k + 1]

        def bar():
            for _e in ENGS:
                for _s in ["sT", "sV", "sS", "dW", "dG"]:
                    em.wait(_e, _s)

        scan_state = {"next": 0, "ready": False}

        def scan_pump(nsteps):
            if not scan_state["ready"] or not SCAN_INTERLEAVE:
                return
            _scan_emit(nsteps)

        def _scan_emit(nsteps):
            t0 = scan_state["next"]
            t1 = min(t0 + nsteps, NSCAN)
            for t in range(t0, t1):
                em.op("scalar", lambda e, t=t: e.activation(
                    out=hd[:, :], in_=hall[:, t:t + 1], func=AF.Identity,
                    scale=dfull[:, t:t + 1]), inc="zS")
                em.wait("tensor", "zS", 2 * t + 1)
                em.op("tensor", lambda e: e.matmul(
                    pb[5][0:16, 0:1], A_sb[:, :], hd[:, :], start=True,
                    stop=True), inc="zT")
                em.wait("scalar", "zT", t + 1)
                em.op("scalar", lambda e, t=t: e.activation(
                    out=hall[:, t + 1:t + 2], in_=pb[5][0:16, 0:1],
                    func=AF.Tanh, bias=btfull[:, t:t + 1]), inc="zS")
            scan_state["next"] = t1

        def ln_stats(sbuf, soff, sstride, n, mu_slot, r_slot, sq_off,
                     bank_s=6, bank_q=7):
            em.wait("tensor", "sS")
            em.wait("tensor", "sV")
            em.wait("scalar", "sT")
            em.wait("scalar", "sV")
            em.wait("scalar", "dG")
            for k in range(NCH):
                cw = soff + k * sstride
                last = (k == NCH - 1)
                em.op("tensor", lambda e, cw=cw, k=k, last=last: e.matmul(
                    pb[bank_s][0:1, 0:n], ones[:, 0:1], sbuf[:, cw:cw + n],
                    start=(k == 0), stop=last), inc=("sT" if last else None))
            mm_sum = em.v("sT")
            sq_mm = []
            for k in range(NCH):
                cw = soff + k * sstride
                po = sq_off + (k % 2) * n
                if k >= 2:
                    em.wait("scalar", "sT", sq_mm[k - 2])
                em.op("scalar", lambda e, cw=cw, po=po: e.activation(
                    out=work[:, po:po + n], in_=sbuf[:, cw:cw + n],
                    func=AF.Square), inc="sS")
                em.wait("tensor", "sS")
                last = (k == NCH - 1)
                em.op("tensor", lambda e, po=po, k=k, last=last: e.matmul(
                    pb[bank_q][0:1, 0:n], ones[:, 0:1], work[:, po:po + n],
                    start=(k == 0), stop=last), inc="sT")
                sq_mm.append(em.v("sT"))
            mm_sq = em.v("sT")
            em.wait("scalar", "sT", mm_sum)
            em.op("scalar", lambda e: e.mul(
                out=srow(mu_slot, 0, n), in_=pb[bank_s][0:1, 0:n],
                mul=1.0 / 1024.0), inc="sS")
            em.wait("scalar", "sT", mm_sq)
            em.op("scalar", lambda e: e.mul(
                out=srow(ST_A, 0, n), in_=pb[bank_q][0:1, 0:n],
                mul=1.0 / 1024.0), inc="sS")
            em.op("scalar", lambda e: e.activation(
                out=srow(ST_B, 0, n),
                in_=srow(mu_slot, 0, n), func=AF.Square), inc="sS")
            em.wait("vector", "sS")
            em.op("vector", lambda e: e.tensor_sub(
                srow(ST_A, 0, n), srow(ST_A, 0, n), srow(ST_B, 0, n)),
                inc="sV")
            em.wait("scalar", "sV")
            em.op("scalar", lambda e: e.activation(
                out=srow(ST_B, 0, n), in_=srow(ST_A, 0, n),
                func=AF.Sqrt, bias=EPS), inc="sS")
            em.wait("vector", "sS")
            em.op("vector", lambda e: e.reciprocal(
                srow(r_slot, 0, n), srow(ST_B, 0, n)), inc="sV")

        def ln_apply(sbuf, soff, sstride, n, mu_slot, r_slot,
                     dbuf, doff, dstride, gam=None, bet=None,
                     bank_mu=4, bank_r=6):
            em.wait("tensor", "sS")
            em.wait("tensor", "sV")
            rmu, rr = mu_slot[0], r_slot[0]
            em.op("tensor", lambda e: e.matmul(
                pb[bank_mu][:, 0:n], ones[rmu:rmu + 1, :],
                srow(mu_slot, 0, n), start=True, stop=True), inc="sT")
            em.op("tensor", lambda e: e.matmul(
                pb[bank_r][:, 0:n], ones[rr:rr + 1, :],
                srow(r_slot, 0, n), start=True, stop=True), inc="sT")
            bc = em.v("sT")
            em.wait("vector", "sT", bc)
            for k in range(NCH):
                cw = soff + k * sstride
                dw = doff + k * dstride
                em.op("vector", lambda e, cw=cw, dw=dw: e.tensor_sub(
                    dbuf[:, dw:dw + n], sbuf[:, cw:cw + n],
                    pb[bank_mu][:, 0:n]), inc="sV")
                em.op("vector", lambda e, dw=dw: e.tensor_mul(
                    dbuf[:, dw:dw + n], dbuf[:, dw:dw + n], pb[bank_r][:, 0:n]),
                    inc="sV")
                if gam is not None:
                    em.op("vector", lambda e, dw=dw, k=k: e.tensor_scalar(
                        dbuf[:, dw:dw + n], dbuf[:, dw:dw + n], bpc(gam, k),
                        bpc(bet, k), op0=OP.mult, op1=OP.add), inc="sV")

        def proj(wsrc, n_out, xn_off, n, slab_off, kch, evac, cast=False,
                 scan_steps=0, xn_stride=None):
            xst = xn_stride if xn_stride is not None else n
            slab_cols = kch * 128
            queue = "gpsimd" if cast else "sync"
            dsem = "dG" if cast else "dW"
            slab_ready, mm_done, evac_done = [], [], []

            def dma_slab(oc):
                so = slab_off + (oc % 2) * slab_cols
                if oc >= 2:
                    em.wait(queue, "sT", mm_done[oc - 2])
                em.op(queue, lambda e, so=so, oc=oc: e.dma_start(
                    out=work[:, so:so + slab_cols].rearrange(
                        "p (k c) -> p k c", c=128),
                    in_=wsrc[:, oc * 128:(oc + 1) * 128].rearrange(
                        "(k p) c -> p k c", p=128)), inc=dsem, n=16)
                slab_ready.append(em.v(dsem))

            em.wait(queue, "sT")
            em.wait("tensor", "sV")
            em.wait("tensor", "sS")
            dma_slab(0)
            if n_out > 1:
                dma_slab(1)
            for oc in range(n_out):
                so = slab_off + (oc % 2) * slab_cols
                bank = pb[oc % 2]
                em.wait("tensor", dsem, slab_ready[oc])
                if oc >= 2:
                    sem_n, sem_v = evac_done[oc - 2]
                    em.wait("tensor", sem_n, sem_v)
                for k in range(kch):
                    last = (k == kch - 1)
                    em.op("tensor", lambda e, so=so, k=k, last=last, bank=bank:
                          e.matmul(
                        bank[:, 0:n],
                        work[:, so + k * 128:so + (k + 1) * 128],
                        work[:, xn_off + k * xst:xn_off + k * xst + n],
                        start=(k == 0), stop=last),
                        inc=("sT" if last else None))
                mm_done.append(em.v("sT"))
                if oc + 2 < n_out:
                    dma_slab(oc + 2)
                evac_done.append(evac(oc, bank, mm_done[oc]))
                if scan_steps:
                    scan_pump(scan_steps)

        for dst, src in [(bp, bp_in), (ident, ident_in), (m0, m0_in),
                         (m1, m1_in), (ones, ones_in), (A_sb, Amat)]:
            em.op("sync", lambda e, dst=dst, src=src: e.dma_start(
                out=dst[:, :], in_=src[:, :]), inc="dW", n=16)
        LD_CONST = em.v("dW")
        em.op("gpsimd", lambda e: e.dma_start(
            out=strm[:, :].rearrange("p (ch t) -> p ch t", t=T),
            in_=x0[:, :].rearrange("(ch p) t -> p ch t", p=128)), inc="dG", n=16)
        LD_X0 = em.v("dG")
        for eng in ["tensor", "vector", "scalar"]:
            em.wait(eng, "dW", LD_CONST)
            em.wait(eng, "dG", LD_X0)

        def tap_stream(name):
            if name in TAPS:
                em.wait("gpsimd", "sV")
                em.op("gpsimd", lambda e, name=name: e.dma_start(
                    out=taps[name][:, :].rearrange("(ch p) t -> p ch t", p=128),
                    in_=strm[:, :].rearrange("p (ch t) -> p ch t", t=T)),
                    inc="dG", n=16)

        def finish():
            _scr = {"q_scr": q_scr, "ka_i": ka_i, "va_i": va_i,
                    "ctxa": ctxa_scr, "qb_scr": qb_scr, "kb_i": kb_i,
                    "vb_i": vb_i, "ctxb": ctxb_scr, "gate": gate_scr,
                    "m_i": m_i, "x2p": x2p_scr}
            em.wait("gpsimd", "sV")
            em.wait("gpsimd", "sS")
            em.wait("gpsimd", "dW")
            for tname in TAPS:
                if tname in _scr:
                    em.op("gpsimd", lambda e, tname=tname: e.dma_start(
                        out=taps[tname][:, :], in_=_scr[tname][:, :]),
                        inc="dG", n=16)
                elif tname == "hall":
                    em.op("gpsimd", lambda e: e.dma_start(
                        out=taps["hall"][:, :], in_=hall[:, :]), inc="dG", n=16)
            em.op("gpsimd", lambda e: e.dma_start(
                out=out[:, :].rearrange("(ch p) t -> p ch t", p=128),
                in_=strm[:, :].rearrange("p (ch t) -> p ch t", t=T)),
                inc="st", n=16)
            em.wait("gpsimd", "st")
            em.flush(block)

        W_N0, W_XN, W_SLAB = 0, 4096, 8192
        W_QTMP, W_SQ = 10240, 11264
        W_VSLAB, W_VTMP = 11264, 19456

        for tt in range(NTT):
            tw = tt * TT
            ln_stats(strm, tw, T, TT, ST_MUT, ST_RT, W_SQ)
            ln_apply(strm, tw, T, TT, ST_MUT, ST_RT, work, W_N0, TT)
            ln_stats(work, W_N0, TT, TT, ST_MUT, ST_RT, W_SQ)
            ln_apply(work, W_N0, TT, TT, ST_MUT, ST_RT, work, W_XN, TT,
                     gam="gam_a", bet="bet_a")
            if "xn_dbg" in TAPS:
                em.wait("gpsimd", "sV")
                em.op("gpsimd", lambda e, tt=tt: e.dma_start(
                    out=taps["xn_dbg"][:, tt * TT:(tt + 1) * TT].rearrange(
                        "(ch p) t -> p ch t", p=128),
                    in_=work[:, W_XN:W_XN + 4096].rearrange(
                        "p (ch t) -> p ch t", t=TT)), inc="dG", n=16)
                em.wait("tensor", "dG")
            em.wait("tensor", "sV")

            def make_ev(scr, scale=None, copy=False, tt=tt):
                dmas = []

                def ev(oc, bank, mmv, tt=tt):
                    po = W_QTMP + (oc % 2) * TT
                    em.wait("vector", "sT", mmv)
                    if len(dmas) >= 2:
                        em.wait("vector", "dG", dmas[-2])
                    else:
                        em.wait("vector", "dG")
                    if copy:
                        em.op("vector", lambda e, po=po, bank=bank: e.tensor_copy(
                            work[:, po:po + TT], bank[:, 0:TT]), inc="sV")
                    else:
                        em.op("vector", lambda e, po=po, bank=bank:
                              e.tensor_scalar_mul(
                            work[:, po:po + TT], bank[:, 0:TT], scale), inc="sV")
                    sv = em.v("sV")
                    em.wait("gpsimd", "sV", sv)
                    em.op("gpsimd", lambda e, po=po, oc=oc, tt=tt: e.dma_start(
                        out=scr[oc * 128:(oc + 1) * 128, tt * TT:(tt + 1) * TT],
                        in_=work[:, po:po + TT]), inc="dG", n=16)
                    dmas.append(em.v("dG"))
                    return ("sV", sv)
                return ev

            proj(wqT, NCH, W_XN, TT, W_SLAB, NCH, make_ev(q_scr, scale=QSC_A))
            if DBG_BARRIER:
                for _e in ["tensor", "vector", "scalar", "sync", "gpsimd"]:
                    for _s in ["sT", "sV", "sS", "dW", "dG"]:
                        em.wait(_e, _s)
            if STOP_AFTER == "qonly":
                continue
            proj(wkT, NCH, W_XN, TT, W_SLAB, NCH, make_ev(ka_i, copy=True))

            vs_ready = []
            for dvg in range(2):
                so = W_VSLAB + dvg * 4096
                em.wait("sync", "sT")
                em.op("sync", lambda e, so=so, dvg=dvg: e.dma_start(
                    out=work[:, so:so + 4096].rearrange(
                        "p (k c) -> p k c", c=512),
                    in_=wvT[:, dvg * 512:(dvg + 1) * 512].rearrange(
                        "(k p) c -> p k c", p=128)), inc="dW", n=16)
                vs_ready.append(em.v("dW"))
            vdma = []
            vev = {}
            em.wait("tensor", "sV")
            for sub in range(4):
                vo = W_VTMP + (sub % 2) * 1040
                if sub >= 2:
                    em.wait("vector", "dG", vdma[sub - 2])
                else:
                    em.wait("vector", "dG")
                for dvg in range(2):
                    so = W_VSLAB + dvg * 4096
                    bank = pb[2 + dvg]
                    em.wait("tensor", "dW", vs_ready[dvg])
                    if (sub - 1, dvg) in vev:
                        em.wait("tensor", "sV", vev[(sub - 1, dvg)])
                    for k in range(NCH):
                        em.op("tensor", lambda e, so=so, k=k, sub=sub, bank=bank:
                              e.matmul(
                            bank[:, 0:512],
                            work[:, W_XN + k * TT + sub * 128:
                                  W_XN + k * TT + sub * 128 + 128],
                            work[:, so + k * 512:so + (k + 1) * 512],
                            start=(k == 0), stop=(k == NCH - 1)),
                            inc=("sT" if k == NCH - 1 else None))
                    mmv = em.v("sT")
                    em.wait("vector", "sT", mmv)
                    em.op("vector", lambda e, vo=vo, dvg=dvg, bank=bank:
                          e.tensor_copy(
                        work[:, vo + dvg * 520:vo + dvg * 520 + 520].rearrange(
                            "p (h c) -> p h c", c=65)[:, :, 0:64],
                        bank[:, 0:512].rearrange("p (h c) -> p h c", c=64)),
                        inc="sV")
                    vev[(sub, dvg)] = em.v("sV")
                em.op("vector", lambda e, vo=vo: e.memset(
                    work[:, vo:vo + 1040].rearrange("p (h c) -> p h c", c=65)
                    [:, :, 64:65], 1.0), inc="sV")
                sv = em.v("sV")
                em.wait("gpsimd", "sV", sv)
                em.op("gpsimd", lambda e, vo=vo, tt=tt, sub=sub: e.dma_start(
                    out=va_i[tt * TT + sub * 128:tt * TT + sub * 128 + 128, :],
                    in_=work[:, vo:vo + 1040]), inc="dG", n=16)
                vdma.append(em.v("dG"))

        if STOP_AFTER == "qonly":
            finish()
            return nc
        em.wait("gpsimd", "dG")
        em.op("gpsimd", lambda e: e.collective_compute(
            "AllGather", OP.bypass, ins=[ka_i[:, :]], outs=[ka_o[:, :, :]],
            replica_groups=PAIR_RG), inc="sC")
        em.op("gpsimd", lambda e: e.collective_compute(
            "AllGather", OP.bypass, ins=[va_i[:, :]], outs=[va_o[:, :, :]],
            replica_groups=PAIR_RG), inc="sC")
        AG_A = em.v("sC")

        WA_K, WA_V, WA_QB, WA_ES, WA_RCP, WA_CTH = 0, 4096, 8256, 8512, 9024, 9216
        em.wait("sync", "sC", AG_A)
        qdma, qused = [], []
        av_evac = {}
        ctx_dmas = []
        for hg in range(NCH):
            em.wait("sync", "sT")
            for r in range(2):
                em.op("sync", lambda e, r=r, hg=hg: e.dma_start(
                    out=work[:, WA_K:WA_K + 4096].rearrange(
                        "p (g c) -> p g c", c=128)[:, r::2, :],
                    in_=ka_o[r, hg * 128:(hg + 1) * 128, :].rearrange(
                        "p (g c) -> p g c", c=128)), inc="dW", n=16)
                em.op("sync", lambda e, r=r, hg=hg: e.dma_start(
                    out=work[:, WA_V:WA_V + 4160].rearrange(
                        "p (g c) -> p g c", c=130)[:, r::2, :],
                    in_=va_o[r, :, hg * 130:(hg + 1) * 130].rearrange(
                        "(g p) c -> p g c", p=128)), inc="dW", n=16)
            kv_ready = em.v("dW")
            if hg == 0 and "kdbg" in TAPS:
                em.wait("gpsimd", "dW", kv_ready)
                em.op("gpsimd", lambda e: e.dma_start(
                    out=taps["kdbg"][:, :], in_=work[:, WA_K:WA_K + 4096]),
                    inc="dG", n=16)
                em.op("gpsimd", lambda e: e.dma_start(
                    out=taps["vdbg"][:, :], in_=work[:, WA_V:WA_V + 4160]),
                    inc="dG", n=16)
                em.wait("tensor", "dG")
            em.wait("tensor", "dW", kv_ready)
            for i in range(16):
                qo = WA_QB + (i % 2) * 128
                if len(qdma) >= 2:
                    em.wait("sync", "sT", qused[-2])
                em.op("sync", lambda e, qo=qo, hg=hg, i=i: e.dma_start(
                    out=work[:, qo:qo + 128],
                    in_=q_scr[hg * 128:(hg + 1) * 128, i * 128:(i + 1) * 128]),
                    inc="dW", n=16)
                qdma.append(em.v("dW"))
                em.wait("tensor", "dW", qdma[-1])
                em.wait("tensor", "sS")
                avb = pb[4 if i % 2 == 0 else 6]
                if (i % 2) in av_evac:
                    em.wait("tensor", "sV", av_evac[i % 2])
                ng = 2 * i + 2
                exp_done = []
                for g in range(ng):
                    par = g % 2
                    sb0, sb1 = pb[0 + par], pb[2 + par]
                    if g >= 2:
                        em.wait("tensor", "sS", exp_done[g - 2])
                    for h, sb in ((0, sb0), (1, sb1)):
                        em.op("tensor", lambda e, g=g, h=h, sb=sb, qo=qo:
                              e.matmul(
                            sb[:, 0:128],
                            work[:, WA_K + g * 128:WA_K + (g + 1) * 128]
                            [h * 64:(h + 1) * 64, :],
                            work[:, qo:qo + 128][h * 64:(h + 1) * 64, :],
                            start=True, stop=True, tile_position=(h * 64, 0)),
                            inc="sT")
                    smm = em.v("sT")
                    if g >= 2 * i:
                        mt = m0 if g == 2 * i else m1
                        em.wait("vector", "sT", smm)
                        for sb in (sb0, sb1):
                            em.op("vector", lambda e, sb=sb, mt=mt: e.tensor_add(
                                sb[:, 0:128], sb[:, 0:128], mt[:, :]), inc="sV")
                        em.wait("scalar", "sV")
                    em.wait("scalar", "sT", smm)
                    for h, sb in ((0, sb0), (1, sb1)):
                        eo = WA_ES + (par * 2 + h) * 128
                        em.op("scalar", lambda e, sb=sb, eo=eo: e.activation(
                            out=work[:, eo:eo + 128], in_=sb[:, 0:128],
                            func=AF.Exp, bias=-SHIFT_A), inc="sS")
                    exp_done.append(em.v("sS"))
                    em.wait("tensor", "sS", exp_done[-1])
                    for h in range(2):
                        eo = WA_ES + (par * 2 + h) * 128
                        em.op("tensor", lambda e, eo=eo, g=g, h=h, avb=avb,
                              ng=ng: e.matmul(
                            avb[:, h * 65:h * 65 + 65], work[:, eo:eo + 128],
                            work[:, WA_V + g * 130 + h * 65:
                                  WA_V + g * 130 + h * 65 + 65],
                            start=(g == 0 and h == 0),
                            stop=(g == ng - 1 and h == 1)),
                            inc=("sT" if (g == ng - 1 and h == 1) else None))
                av_mm = em.v("sT")
                scan_pump(2)
                em.wait("vector", "sT", av_mm)
                if i == 0 and ctx_dmas:
                    em.wait("vector", "dG", ctx_dmas[-1])
                for h in range(2):
                    if DBG_NONORM:
                        em.op("vector", lambda e, h=h, avb=avb, i=i:
                              e.tensor_copy(
                            work[:, WA_CTH + i * 128 + h * 64:
                                  WA_CTH + i * 128 + h * 64 + 63],
                            avb[:, h * 65:h * 65 + 63]), inc="sV")
                        em.op("vector", lambda e, h=h, avb=avb, i=i:
                              e.tensor_copy(
                            work[:, WA_CTH + i * 128 + h * 64 + 63:
                                  WA_CTH + i * 128 + h * 64 + 64],
                            avb[:, h * 65 + 64:h * 65 + 65]), inc="sV")
                        continue
                    rb = WA_RCP + (i % 2) * 8
                    em.wait("scalar", "sT", av_mm)
                    if (i % 2) in av_evac:
                        em.wait("scalar", "sV", av_evac[i % 2])
                    em.op("scalar", lambda e, h=h, avb=avb, rb=rb: e.activation(
                        out=work[:, rb + 4 + h:rb + 5 + h],
                        in_=avb[:, h * 65 + 64:h * 65 + 65], func=AF.Ln),
                        inc="sS")
                    em.op("scalar", lambda e, h=h, rb=rb: e.activation(
                        out=work[:, rb + h:rb + h + 1],
                        in_=work[:, rb + 4 + h:rb + 5 + h],
                        func=AF.Exp, scale=-1.0), inc="sS")
                    em.wait("vector", "sS")
                    em.op("vector", lambda e, h=h, avb=avb, i=i, rb=rb:
                          e.tensor_scalar_mul(
                        work[:, WA_CTH + i * 128 + h * 64:
                              WA_CTH + i * 128 + h * 64 + 64],
                        avb[:, h * 65:h * 65 + 64],
                        work[:, rb + h:rb + h + 1]), inc="sV")
                av_evac[i % 2] = em.v("sV")
                qused.append(em.v("sT"))
                bar()
            em.wait("gpsimd", "sV")
            em.op("gpsimd", lambda e, hg=hg: e.dma_start(
                out=ctxa_scr[:, hg * 128:(hg + 1) * 128].rearrange(
                    "(i p) c -> p i c", p=128),
                in_=work[:, WA_CTH:WA_CTH + 2048].rearrange(
                    "p (i c) -> p i c", c=128)), inc="dG", n=16)
            ctx_dmas.append(em.v("dG"))
        CTXA_SAVED = em.v("dG")

        WD_WO, WD_CTI, WD_FM = 0, 8192, 10240
        em.wait("sync", "sT")
        for oc in range(NCH):
            em.op("sync", lambda e, oc=oc: e.dma_start(
                out=work[:, WD_WO + oc * 1024:WD_WO + (oc + 1) * 1024].rearrange(
                    "p (k c) -> p k c", c=128),
                in_=woT[:, oc * 128:(oc + 1) * 128].rearrange(
                    "(k p) c -> p k c", p=128)), inc="dW", n=16)
        WO_READY = em.v("dW")
        em.wait("sync", "dG", CTXA_SAVED)
        cti_dma, cti_used = [], []
        em.wait("tensor", "dW", WO_READY)
        for i in range(16):
            co = WD_CTI + (i % 2) * 1024
            if len(cti_dma) >= 2:
                em.wait("sync", "sT", cti_used[-2])
            em.op("sync", lambda e, i=i, co=co: e.dma_start(
                out=work[:, co:co + 1024],
                in_=ctxa_scr[i * 128:(i + 1) * 128, :]), inc="dW", n=16)
            cti_dma.append(em.v("dW"))
            em.wait("tensor", "dW", cti_dma[-1])
            tr_done = []
            for ec in range(NCH):
                bank = pb[ec % 2]
                if ec >= 2:
                    em.wait("tensor", "sV", tr_done[ec - 2])
                em.op("tensor", lambda e, co=co, ec=ec, bank=bank: e.transpose(
                    bank[:, 0:128],
                    work[:, co + ec * 128:co + (ec + 1) * 128], ident[:, :]),
                    inc="sT")
                em.wait("vector", "sT")
                em.op("vector", lambda e, ec=ec, bank=bank: e.tensor_copy(
                    work[:, WD_FM + ec * 128:WD_FM + (ec + 1) * 128],
                    bank[:, 0:128]), inc="sV")
                tr_done.append(em.v("sV"))
            em.wait("tensor", "sV", tr_done[-1])
            opv = []
            for oc in range(NCH):
                bank = pb[2 + oc % 2]
                if oc >= 2:
                    em.wait("tensor", "sV", opv[oc - 2])
                for k in range(NCH):
                    em.op("tensor", lambda e, oc=oc, k=k, bank=bank: e.matmul(
                        bank[:, 0:128],
                        work[:, WD_WO + oc * 1024 + k * 128:
                              WD_WO + oc * 1024 + (k + 1) * 128],
                        work[:, WD_FM + k * 128:WD_FM + (k + 1) * 128],
                        start=(k == 0), stop=(k == NCH - 1)),
                        inc=("sT" if k == NCH - 1 else None))
                mmv = em.v("sT")
                em.wait("vector", "sT", mmv)
                em.op("vector", lambda e, oc=oc, i=i, bank=bank:
                      e.scalar_tensor_tensor(
                    strm[:, oc * T + i * 128:oc * T + (i + 1) * 128],
                    bank[:, 0:128], bpc("bout_a", oc),
                    strm[:, oc * T + i * 128:oc * T + (i + 1) * 128],
                    op0=OP.add, op1=OP.add), inc="sV")
                opv.append(em.v("sV"))
            cti_used.append(em.v("sT"))
            for _e in ["tensor", "vector", "scalar", "sync", "gpsimd"]:
                for _s in ["sT", "sV", "sS", "dW", "dG"]:
                    em.wait(_e, _s)
            scan_pump(2)
        tap_stream("x1")
        if STOP_AFTER == "s1":
            finish()
            return nc

        def ffn_phase():
            F_XN, F_SQ, F_GSL, F_USL = 0, 4096, 5120, 7168
            F_GS, F_H2, F_DSL = 4096, 9216, 0
            for tt in range(NTT):
                tw = tt * TT
                ln_stats(strm, tw, T, TT, ST_MUT, ST_RT, F_SQ)
                ln_apply(strm, tw, T, TT, ST_MUT, ST_RT, work, F_XN, TT)
                em.wait("tensor", "sV")
                gsl_ready, usl_ready, gmm, umm, h2v = [], [], [], [], []

                def dma_g(hc):
                    so = F_GSL + (hc % 2) * 1024
                    uo = F_USL + (hc % 2) * 1024
                    if hc >= 2:
                        em.wait("gpsimd", "sT", umm[hc - 2])
                    em.op("gpsimd", lambda e, so=so, hc=hc: e.dma_start(
                        out=work[:, so:so + 1024].rearrange(
                            "p (k c) -> p k c", c=128),
                        in_=wgT[:, hc * 128:(hc + 1) * 128].rearrange(
                            "(k p) c -> p k c", p=128)), inc="dG", n=16)
                    gsl_ready.append(em.v("dG"))
                    em.op("gpsimd", lambda e, uo=uo, hc=hc: e.dma_start(
                        out=work[:, uo:uo + 1024].rearrange(
                            "p (k c) -> p k c", c=128),
                        in_=wuT[:, hc * 128:(hc + 1) * 128].rearrange(
                            "(k p) c -> p k c", p=128)), inc="dG", n=16)
                    usl_ready.append(em.v("dG"))

                em.wait("gpsimd", "sT")
                dma_g(0); dma_g(1)
                for hc in range(24):
                    so = F_GSL + (hc % 2) * 1024
                    uo = F_USL + (hc % 2) * 1024
                    gb, ub = pb[hc % 2], pb[2 + hc % 2]
                    em.wait("tensor", "dG", usl_ready[hc])
                    if hc >= 2:
                        em.wait("tensor", "sV", h2v[hc - 2])
                        em.wait("tensor", "sS")
                    for k in range(NCH):
                        em.op("tensor", lambda e, so=so, k=k, gb=gb: e.matmul(
                            gb[:, 0:TT],
                            work[:, so + k * 128:so + (k + 1) * 128],
                            work[:, F_XN + k * TT:F_XN + k * TT + TT],
                            start=(k == 0), stop=(k == NCH - 1)),
                            inc=("sT" if k == NCH - 1 else None))
                    gmm.append(em.v("sT"))
                    for k in range(NCH):
                        em.op("tensor", lambda e, uo=uo, k=k, ub=ub: e.matmul(
                            ub[:, 0:TT],
                            work[:, uo + k * 128:uo + (k + 1) * 128],
                            work[:, F_XN + k * TT:F_XN + k * TT + TT],
                            start=(k == 0), stop=(k == NCH - 1)),
                            inc=("sT" if k == NCH - 1 else None))
                    umm.append(em.v("sT"))
                    if hc + 2 < 24:
                        dma_g(hc + 2)
                    go = F_GS + (hc % 2) * TT
                    em.wait("scalar", "sT", gmm[-1])
                    em.op("scalar", lambda e, go=go, gb=gb: e.activation(
                        out=work[:, go:go + TT], in_=gb[:, 0:TT], func=AF.Silu,
                        scale=INV127), inc="sS")
                    ssv = em.v("sS")
                    em.wait("vector", "sT", umm[-1])
                    em.wait("vector", "sS", ssv)
                    em.op("vector", lambda e, go=go, hc=hc, ub=ub:
                          e.scalar_tensor_tensor(
                        work[:, F_H2 + hc * TT:F_H2 + (hc + 1) * TT],
                        ub[:, 0:TT], INV127, work[:, go:go + TT],
                        op0=OP.mult, op1=OP.mult), inc="sV")
                    h2v.append(em.v("sV"))
                    scan_pump(5)
                dsl_ready, dmm, resv = [], [], []

                def dma_d(oc):
                    so = F_DSL + (oc % 2) * 3072
                    if oc >= 2:
                        em.wait("gpsimd", "sT", dmm[oc - 2])
                    em.op("gpsimd", lambda e, so=so, oc=oc: e.dma_start(
                        out=work[:, so:so + 3072].rearrange(
                            "p (k c) -> p k c", c=128),
                        in_=wdT[:, oc * 128:(oc + 1) * 128].rearrange(
                            "(k p) c -> p k c", p=128)), inc="dG", n=16)
                    dsl_ready.append(em.v("dG"))

                em.wait("gpsimd", "sT")
                dma_d(0); dma_d(1)
                em.wait("tensor", "sV", h2v[-1])
                for oc in range(NCH):
                    so = F_DSL + (oc % 2) * 3072
                    bank = pb[4 if oc % 2 == 0 else 6]
                    em.wait("tensor", "dG", dsl_ready[oc])
                    if oc >= 2:
                        em.wait("tensor", "sV", resv[oc - 2])
                    for k in range(24):
                        em.op("tensor", lambda e, so=so, k=k, bank=bank:
                              e.matmul(
                            bank[:, 0:TT],
                            work[:, so + k * 128:so + (k + 1) * 128],
                            work[:, F_H2 + k * TT:F_H2 + k * TT + TT],
                            start=(k == 0), stop=(k == 23)),
                            inc=("sT" if k == 23 else None))
                    dmm.append(em.v("sT"))
                    if oc + 2 < NCH:
                        dma_d(oc + 2)
                    em.wait("vector", "sT", dmm[-1])
                    em.op("vector", lambda e, oc=oc, tw=tw, bank=bank:
                          e.scalar_tensor_tensor(
                        strm[:, oc * T + tw:oc * T + tw + TT], bank[:, 0:TT],
                        INV127, strm[:, oc * T + tw:oc * T + tw + TT],
                        op0=OP.mult, op1=OP.add), inc="sV")
                    resv.append(em.v("sV"))
                    scan_pump(5)

        ffn_phase()
        tap_stream("x2")
        if STOP_AFTER == "s2":
            finish()
            return nc

        em.wait("gpsimd", "sV")
        em.op("gpsimd", lambda e: e.dma_start(
            out=x2_scr[:, :].rearrange("(ch p) t -> p ch t", p=128),
            in_=strm[:, :].rearrange("p (ch t) -> p ch t", t=T)), inc="dG", n=16)
        X2_SAVED = em.v("dG")

        C_XN, C_SQ, C_SLAB, C_QTMP = 0, 4096, 5120, 4096
        C_VSL, C_VTMP = 7168, 15360
        C_WDL, C_WIN, C_BVB = C_VTMP + 512, C_VTMP + 640, 17424
        em.wait("sync", "sT")
        em.wait("sync", "sV")
        em.op("sync", lambda e: e.dma_start(
            out=work[:, C_WDL:C_WDL + 128].rearrange("p (k c) -> p k c", c=16),
            in_=Wdl[:, :].rearrange("(k p) c -> p k c", p=128)), inc="dW", n=16)
        em.op("sync", lambda e: e.dma_start(
            out=work[:, C_WIN:C_WIN + 128].rearrange("p (k c) -> p k c", c=16),
            in_=Win[:, :].rearrange("(k p) c -> p k c", p=128)), inc="dW", n=16)
        WDLWIN = em.v("dW")
        em.wait("tensor", "dW", WDLWIN)
        gate_dmas = []
        for tt in range(NTT):
            tw = tt * TT
            ln_stats(strm, tw, T, TT, ST_MUT, ST_RT, C_SQ)
            if tt == 0:
                em.wait("vector", "dG", X2_SAVED)
            ln_apply(strm, tw, T, TT, ST_MUT, ST_RT, strm, tw, T)
            ln_stats(strm, tw, T, TT, ST_MUT, ST_RT, C_SQ)
            ln_apply(strm, tw, T, TT, ST_MUT, ST_RT, work, C_XN, TT,
                     gam="lng", bet="lnb")
            em.wait("tensor", "sV")
            for woff, bname, func, r0, r1 in [
                    (C_WDL, "bdelta", AF.Sigmoid, 0, 16),
                    (C_WIN, "b_in", AF.Identity, 32, 48)]:
                em.wait("tensor", "sS")
                for k in range(NCH):
                    em.op("tensor", lambda e, woff=woff, k=k: e.matmul(
                        pb[6][0:16, 0:TT],
                        work[:, woff + k * 16:woff + (k + 1) * 16],
                        work[:, C_XN + k * TT:C_XN + k * TT + TT],
                        start=(k == 0), stop=(k == NCH - 1)),
                        inc=("sT" if k == NCH - 1 else None))
                mmv = em.v("sT")
                em.wait("scalar", "sT", mmv)
                em.op("scalar", lambda e, bname=bname, func=func, r0=r0, r1=r1:
                      e.activation(
                    out=work[r0:r1, C_VTMP:C_VTMP + TT],
                    in_=pb[6][0:16, 0:TT], func=func,
                    bias=bp[0:16, BP[bname]:BP[bname] + 1]), inc="sS")
            dm_done = em.v("sS")
            em.wait("gpsimd", "sS", dm_done)
            em.op("gpsimd", lambda e, tt=tt: e.dma_start(
                out=m_i[:, tt * TT:(tt + 1) * TT],
                in_=work[0:48, C_VTMP:C_VTMP + TT]), inc="dG", n=16)

            def evg(oc, bank, mmv):
                po = C_QTMP + (oc % 2) * TT
                em.wait("scalar", "sT", mmv)
                if len(gate_dmas) >= 2:
                    em.wait("scalar", "dG", gate_dmas[-2])
                else:
                    em.wait("scalar", "dG")
                em.op("scalar", lambda e, po=po, oc=oc, bank=bank: e.activation(
                    out=work[:, po:po + TT], in_=bank[:, 0:TT], func=AF.Sigmoid,
                    bias=bpc("bgate", oc)), inc="sS")
                sv = em.v("sS")
                em.wait("gpsimd", "sS", sv)
                em.op("gpsimd", lambda e, po=po, oc=oc, tt=tt: e.dma_start(
                    out=gate_scr[oc * 128:(oc + 1) * 128, tt * TT:(tt + 1) * TT],
                    in_=work[:, po:po + TT]), inc="dG", n=16)
                gate_dmas.append(em.v("dG"))
                return ("sS", sv)
            proj(Wgt, NCH, C_XN, TT, C_SLAB, NCH, evg)

        em.wait("gpsimd", "dG")
        em.op("gpsimd", lambda e: e.collective_compute(
            "AllGather", OP.bypass, ins=[m_i[:, :]], outs=[m_o[:, :, :]],
            replica_groups=PAIR_RG), inc="sC")
        AG_M = em.v("sC")
        em.wait("sync", "sC", AG_M)
        for r in range(2):
            em.op("sync", lambda e, r=r: e.dma_start(
                out=dfull[:, :].rearrange("p (g c) -> p g c", c=128)[:, r::2, :],
                in_=m_o[r, 0:16, :].rearrange("p (g c) -> p g c", c=128)),
                inc="dW", n=16)
            em.op("sync", lambda e, r=r: e.dma_start(
                out=btfull[:, :].rearrange("p (g c) -> p g c", c=128)[:, r::2, :],
                in_=m_o[r, 32:48, :].rearrange("p (g c) -> p g c", c=128)),
                inc="dW", n=16)
        MB_READY = em.v("dW")
        em.wait("vector", "dW", MB_READY)
        em.op("vector", lambda e: e.memset(hall[:, 0:1], 0.0), inc="sV")
        HALL0 = em.v("sV")
        em.wait("scalar", "sV", HALL0)
        em.wait("scalar", "dW", MB_READY)
        em.wait("tensor", "dW", MB_READY)
        scan_state["ready"] = True
        if not SCAN_INTERLEAVE:
            _scan_emit(NSCAN)

        em.wait("sync", "sS")
        em.wait("sync", "sV")
        em.op("sync", lambda e: e.dma_start(out=srow(ST_BV, 0, 1024),
              in_=rp_in[0:1, 0:1024]), inc="dW", n=16)
        RP_LD = em.v("dW")
        em.wait("tensor", "dW", RP_LD)
        em.wait("tensor", "sV")
        em.wait("tensor", "sS")
        em.op("tensor", lambda e: e.matmul(
            pb[6][:, 0:512], ones[64:65, :], srow(ST_BV, 0, 512), start=True,
            stop=True), inc="sT")
        em.op("tensor", lambda e: e.matmul(
            pb[7][:, 0:512], ones[64:65, :], srow(ST_BV, 512, 512), start=True,
            stop=True), inc="sT")
        BVB_MM = em.v("sT")
        em.wait("vector", "sT", BVB_MM)
        em.op("vector", lambda e: e.tensor_copy(
            work[:, C_BVB:C_BVB + 512], pb[6][:, 0:512]), inc="sV")
        em.op("vector", lambda e: e.tensor_copy(
            work[:, C_BVB + 512:C_BVB + 1024], pb[7][:, 0:512]), inc="sV")
        BVB_DONE = em.v("sV")

        qbdmas, kbdmas = [], []
        for tt in range(NTT):
            tw = tt * TT
            ln_stats(strm, tw, T, TT, ST_MUT, ST_RT, C_SQ)
            ln_apply(strm, tw, T, TT, ST_MUT, ST_RT, work, C_XN, TT,
                     gam="ln1g", bet="ln1b")
            em.wait("tensor", "sV")

            def evqb(oc, bank, mmv):
                po = C_QTMP + (oc % 2) * TT
                em.wait("vector", "sT", mmv)
                if len(qbdmas) >= 2:
                    em.wait("vector", "dG", qbdmas[-2])
                else:
                    em.wait("vector", "dG")
                em.op("vector", lambda e, po=po, oc=oc, bank=bank:
                      e.tensor_scalar(
                    work[:, po:po + TT], bank[:, 0:TT], bpc("bq", oc), QSC_B,
                    op0=OP.add, op1=OP.mult), inc="sV")
                sv = em.v("sV")
                em.wait("gpsimd", "sV", sv)
                em.op("gpsimd", lambda e, po=po, oc=oc, tt=tt: e.dma_start(
                    out=qb_scr[oc * 128:(oc + 1) * 128, tt * TT:(tt + 1) * TT],
                    in_=work[:, po:po + TT]), inc="dG", n=16)
                qbdmas.append(em.v("dG"))
                return ("sV", sv)
            proj(Wq, NCH, C_XN, TT, C_SLAB, NCH, evqb, scan_steps=8)

            def evkb(oc, bank, mmv):
                po = C_QTMP + (oc % 2) * TT
                em.wait("vector", "sT", mmv)
                if len(kbdmas) >= 2:
                    em.wait("vector", "dG", kbdmas[-2])
                else:
                    em.wait("vector", "dG")
                em.op("vector", lambda e, po=po, oc=oc, bank=bank:
                      e.tensor_scalar_add(
                    work[:, po:po + TT], bank[:, 0:TT], bpc("bk", oc)),
                    inc="sV")
                sv = em.v("sV")
                em.wait("gpsimd", "sV", sv)
                em.op("gpsimd", lambda e, po=po, oc=oc, tt=tt: e.dma_start(
                    out=kb_i[oc * 128:(oc + 1) * 128, tt * TT:(tt + 1) * TT],
                    in_=work[:, po:po + TT]), inc="dG", n=16)
                kbdmas.append(em.v("dG"))
                return ("sV", sv)
            proj(Wk, NCH, C_XN, TT, C_SLAB, NCH, evkb, scan_steps=8)

            em.wait("vector", "sV", BVB_DONE)
            vs_ready = []
            for dvg in range(2):
                so = C_VSL + dvg * 4096
                em.wait("sync", "sT")
                em.op("sync", lambda e, so=so, dvg=dvg: e.dma_start(
                    out=work[:, so:so + 4096].rearrange(
                        "p (k c) -> p k c", c=512),
                    in_=Wv[:, dvg * 512:(dvg + 1) * 512].rearrange(
                        "(k p) c -> p k c", p=128)), inc="dW", n=16)
                vs_ready.append(em.v("dW"))
            vdma = []
            vev = {}
            em.wait("tensor", "sV")
            for sub in range(4):
                vo = C_VTMP + (sub % 2) * 1032
                if sub >= 2:
                    em.wait("vector", "dG", vdma[sub - 2])
                else:
                    em.wait("vector", "dG")
                for dvg in range(2):
                    so = C_VSL + dvg * 4096
                    bank = pb[2 + dvg]
                    em.wait("tensor", "dW", vs_ready[dvg])
                    if (sub - 1, dvg) in vev:
                        em.wait("tensor", "sV", vev[(sub - 1, dvg)])
                    for k in range(NCH):
                        em.op("tensor", lambda e, so=so, k=k, sub=sub, bank=bank:
                              e.matmul(
                            bank[:, 0:512],
                            work[:, C_XN + k * TT + sub * 128:
                                  C_XN + k * TT + sub * 128 + 128],
                            work[:, so + k * 512:so + (k + 1) * 512],
                            start=(k == 0), stop=(k == NCH - 1)),
                            inc=("sT" if k == NCH - 1 else None))
                    mmv = em.v("sT")
                    em.wait("vector", "sT", mmv)
                    em.op("vector", lambda e, vo=vo, dvg=dvg, bank=bank:
                          e.tensor_add(
                        work[:, vo + dvg * 516:vo + dvg * 516 + 516].rearrange(
                            "p (h c) -> p h c", c=129)[:, :, 0:128],
                        bank[:, 0:512].rearrange("p (h c) -> p h c", c=128),
                        work[:, C_BVB + dvg * 512:C_BVB + (dvg + 1) * 512]
                        .rearrange("p (h c) -> p h c", c=128)), inc="sV")
                    vev[(sub, dvg)] = em.v("sV")
                em.op("vector", lambda e, vo=vo: e.memset(
                    work[:, vo:vo + 1032].rearrange("p (h c) -> p h c", c=129)
                    [:, :, 128:129], 1.0), inc="sV")
                sv = em.v("sV")
                em.wait("gpsimd", "sV", sv)
                em.op("gpsimd", lambda e, vo=vo, tt=tt, sub=sub: e.dma_start(
                    out=vb_i[tt * TT + sub * 128:tt * TT + sub * 128 + 128, :],
                    in_=work[:, vo:vo + 1032]), inc="dG", n=16)
                vdma.append(em.v("dG"))
                scan_pump(8)

        em.wait("gpsimd", "dG")
        em.op("gpsimd", lambda e: e.collective_compute(
            "AllGather", OP.bypass, ins=[kb_i[:, :]], outs=[kb_o[:, :, :]],
            replica_groups=PAIR_RG), inc="sC")
        em.op("gpsimd", lambda e: e.collective_compute(
            "AllGather", OP.bypass, ins=[vb_i[:, :]], outs=[vb_o[:, :, :]],
            replica_groups=PAIR_RG), inc="sC")
        AG_B = em.v("sC")

        WB_K, WB_V, WB_QB, WB_ES, WB_RCP, WB_CTH = 0, 4096, 8224, 9248, 11296, 11392
        em.wait("sync", "sC", AG_B)
        qbt_dma, qbt_used = [], []
        avb_evac = {}
        ctxb_dmas = []
        for h in range(8):
            em.wait("sync", "sT")
            for r in range(2):
                em.op("sync", lambda e, r=r, h=h: e.dma_start(
                    out=work[:, WB_K + r * 2048:WB_K + r * 2048 + 2048],
                    in_=kb_o[r, h * 128:(h + 1) * 128, :]), inc="dW", n=16)
                em.op("sync", lambda e, r=r, h=h: e.dma_start(
                    out=work[:, WB_V + r * 2064:WB_V + r * 2064 + 2064]
                    .rearrange("p (g c) -> p g c", c=129),
                    in_=vb_o[r, :, h * 129:(h + 1) * 129].rearrange(
                        "(g p) c -> p g c", p=128)), inc="dW", n=16)
            kv_ready = em.v("dW")
            em.wait("tensor", "dW", kv_ready)
            for qt in range(NTT):
                qo = WB_QB + (qt % 2) * TT
                if len(qbt_dma) >= 2:
                    em.wait("sync", "sT", qbt_used[-2])
                em.op("sync", lambda e, qo=qo, h=h, qt=qt: e.dma_start(
                    out=work[:, qo:qo + TT],
                    in_=qb_scr[h * 128:(h + 1) * 128, qt * TT:(qt + 1) * TT]),
                    inc="dW", n=16)
                qbt_dma.append(em.v("dW"))
                em.wait("tensor", "dW", qbt_dma[-1])
                em.wait("tensor", "sS")
                if avb_evac:
                    em.wait("tensor", "sV", max(avb_evac.values()))
                av0, av1 = pb[4], pb[6]
                exp_done = []
                for g in range(32):
                    par = g % 2
                    sb = pb[par]
                    if g >= 2:
                        em.wait("tensor", "sS", exp_done[g - 2])
                    em.op("tensor", lambda e, g=g, sb=sb, qo=qo: e.matmul(
                        sb[:, 0:TT],
                        work[:, WB_K + g * 128:WB_K + (g + 1) * 128],
                        work[:, qo:qo + TT], start=True, stop=True), inc="sT")
                    smm = em.v("sT")
                    em.wait("scalar", "sT", smm)
                    eo = WB_ES + par * TT
                    em.op("scalar", lambda e, sb=sb, eo=eo: e.activation(
                        out=work[:, eo:eo + TT], in_=sb[:, 0:TT], func=AF.Exp,
                        bias=-SHIFT_B), inc="sS")
                    exp_done.append(em.v("sS"))
                    em.wait("tensor", "sS", exp_done[-1])
                    for sq in range(4):
                        avb = av0 if sq < 2 else av1
                        em.op("tensor", lambda e, eo=eo, g=g, sq=sq, avb=avb:
                              e.matmul(
                            avb[:, (sq % 2) * 129:(sq % 2) * 129 + 129],
                            work[:, eo + sq * 128:eo + (sq + 1) * 128],
                            work[:, WB_V + g * 129:WB_V + (g + 1) * 129],
                            start=(g == 0 and sq % 2 == 0),
                            stop=(g == 31 and sq % 2 == 1)),
                            inc=("sT" if (g == 31 and sq == 3) else None))
                    scan_pump(2)
                av_mm = em.v("sT")
                em.wait("vector", "sT", av_mm)
                if qt == 0 and ctxb_dmas:
                    em.wait("vector", "dG", ctxb_dmas[-1])
                rb = WB_RCP + (qt % 2) * 16
                em.wait("scalar", "sT", av_mm)
                if (qt % 2) in avb_evac:
                    em.wait("scalar", "sV", avb_evac[qt % 2])
                for sq in range(4):
                    avb = av0 if sq < 2 else av1
                    em.op("scalar", lambda e, sq=sq, avb=avb, rb=rb:
                          e.activation(
                        out=work[:, rb + 8 + sq:rb + 9 + sq],
                        in_=avb[:, (sq % 2) * 129 + 128:(sq % 2) * 129 + 129],
                        func=AF.Ln), inc="sS")
                    em.op("scalar", lambda e, sq=sq, rb=rb: e.activation(
                        out=work[:, rb + sq:rb + sq + 1],
                        in_=work[:, rb + 8 + sq:rb + 9 + sq],
                        func=AF.Exp, scale=-1.0), inc="sS")
                em.wait("vector", "sS")
                for sq in range(4):
                    avb = av0 if sq < 2 else av1
                    em.op("vector", lambda e, sq=sq, avb=avb, qt=qt, rb=rb:
                          e.tensor_scalar_mul(
                        work[:, WB_CTH + (qt * 4 + sq) * 128:
                              WB_CTH + (qt * 4 + sq) * 128 + 128],
                        avb[:, (sq % 2) * 129:(sq % 2) * 129 + 128],
                        work[:, rb + sq:rb + sq + 1]), inc="sV")
                avb_evac[qt % 2] = em.v("sV")
                qbt_used.append(em.v("sT"))
                bar()
            em.wait("gpsimd", "sV")
            em.op("gpsimd", lambda e, h=h: e.dma_start(
                out=ctxb_scr[:, h * 128:(h + 1) * 128].rearrange(
                    "(i p) c -> p i c", p=128),
                in_=work[:, WB_CTH:WB_CTH + 2048].rearrange(
                    "p (i c) -> p i c", c=128)), inc="dG", n=16)
            ctxb_dmas.append(em.v("dG"))
        CTXB_SAVED = em.v("dG")

        WB_WO, WB_CTI, WB_FM, WB_X2P, WB_SQ2 = 0, 8192, 12288, 16384, 20480
        em.wait("sync", "sT")
        for oc in range(NCH):
            em.op("sync", lambda e, oc=oc: e.dma_start(
                out=work[:, WB_WO + oc * 1024:WB_WO + (oc + 1) * 1024]
                .rearrange("p (k c) -> p k c", c=128),
                in_=Wo[:, oc * 128:(oc + 1) * 128].rearrange(
                    "(k p) c -> p k c", p=128)), inc="dW", n=16)
        WOB_READY = em.v("dW")
        em.wait("sync", "dG", CTXB_SAVED)
        em.wait("tensor", "dW", WOB_READY)
        x2p_dmas = []
        for qt in range(NTT):
            em.wait("sync", "sT")
            em.op("sync", lambda e, qt=qt: e.dma_start(
                out=work[:, WB_CTI:WB_CTI + 4096].rearrange(
                    "p (s c) -> p s c", c=1024),
                in_=ctxb_scr[qt * 512:(qt + 1) * 512, :].rearrange(
                    "(s p) c -> p s c", p=128)), inc="dW", n=16)
            cti_ld = em.v("dW")
            em.wait("tensor", "dW", cti_ld)
            em.wait("vector", "sS")
            tr_done = []
            for blk in range(4 * NCH):
                sq, ec = blk // NCH, blk % NCH
                bank = pb[blk % 2]
                if blk >= 2:
                    em.wait("tensor", "sV", tr_done[blk - 2])
                em.op("tensor", lambda e, sq=sq, ec=ec, bank=bank:
                      e.transpose(
                    bank[:, 0:128],
                    work[:, WB_CTI + sq * 1024 + ec * 128:
                          WB_CTI + sq * 1024 + (ec + 1) * 128],
                    ident[:, :]), inc="sT")
                em.wait("vector", "sT")
                em.op("vector", lambda e, sq=sq, ec=ec, bank=bank:
                      e.tensor_copy(
                    work[:, WB_FM + ec * TT + sq * 128:
                          WB_FM + ec * TT + (sq + 1) * 128], bank[:, 0:128]),
                    inc="sV")
                tr_done.append(em.v("sV"))
            em.wait("tensor", "sV", tr_done[-1])
            x2p_off = WB_X2P
            opv = []
            for oc in range(NCH):
                bank = pb[2 + oc % 2]
                if oc >= 2:
                    em.wait("tensor", "sV", opv[oc - 2])
                for k in range(NCH):
                    em.op("tensor", lambda e, oc=oc, k=k, bank=bank: e.matmul(
                        bank[:, 0:TT],
                        work[:, WB_WO + oc * 1024 + k * 128:
                              WB_WO + oc * 1024 + (k + 1) * 128],
                        work[:, WB_FM + k * TT:WB_FM + (k + 1) * TT],
                        start=(k == 0), stop=(k == NCH - 1)),
                        inc=("sT" if k == NCH - 1 else None))
                mmv = em.v("sT")
                em.wait("vector", "sT", mmv)
                em.op("vector", lambda e, oc=oc, qt=qt, bank=bank,
                      x2p_off=x2p_off: e.scalar_tensor_tensor(
                    work[:, x2p_off + oc * TT:x2p_off + (oc + 1) * TT],
                    bank[:, 0:TT], bpc("bo", oc),
                    strm[:, oc * T + qt * TT:oc * T + (qt + 1) * TT],
                    op0=OP.add, op1=OP.add), inc="sV")
                opv.append(em.v("sV"))
            scan_pump(4)
            ln_stats(work, x2p_off, TT, TT, sl(ST_MU2X, qt * TT),
                     sl(ST_R2X, qt * TT), WB_SQ2)
            em.wait("gpsimd", "sV")
            em.wait("gpsimd", "sS")
            em.op("gpsimd", lambda e, x2p_off=x2p_off, qt=qt: e.dma_start(
                out=x2p_scr[:, qt * TT:(qt + 1) * TT].rearrange(
                    "(ch p) t -> p ch t", p=128),
                in_=work[:, x2p_off:x2p_off + 4096].rearrange(
                    "p (ch t) -> p ch t", t=TT)), inc="dG", n=16)
            x2p_dmas.append(em.v("dG"))
            bar()
        X2P_SAVED = em.v("dG")


        H_X2P, H_XN2, H_HB, H_W1S = 0, 2048, 4096, 12288
        H_W2S, H_TOUT = 14336, 18432
        H_MB, H_GT, H_WFS, H_FP, H_SQF = 4096, 6144, 8192, 12288, 14336
        TB = 256
        for t2 in range(8):
            tw2 = t2 * TB
            qt_of = t2 // 2
            em.wait("sync", "dG", X2P_SAVED)
            em.wait("sync", "sV")
            em.op("sync", lambda e, t2=t2: e.dma_start(
                out=work[:, H_X2P:H_X2P + 2048].rearrange(
                    "p (ch t) -> p ch t", t=TB),
                in_=x2p_scr[:, t2 * TB:(t2 + 1) * TB].rearrange(
                    "(ch p) t -> p ch t", p=128)), inc="dW", n=16)
            x2p_ld = em.v("dW")
            em.wait("vector", "dW", x2p_ld)
            ln_apply(work, H_X2P, TB, TB, sl(ST_MU2X, tw2), sl(ST_R2X, tw2),
                     work, H_XN2, TB, gam="ln2g", bet="ln2b")
            em.wait("tensor", "sV")

            def evw1(hc, bank, mmv):
                em.wait("scalar", "sT", mmv)
                em.op("scalar", lambda e, hc=hc, bank=bank: e.activation(
                    out=work[:, H_HB + hc * TB:H_HB + (hc + 1) * TB],
                    in_=bank[:, 0:TB], func=AF.Relu, bias=bpc("b1", hc)),
                    inc="sS")
                return ("sS", em.v("sS"))
            proj(W1, 32, H_XN2, TB, H_W1S, NCH, evw1, scan_steps=6)
            em.wait("tensor", "sS")

            w2_ready, w2mm, toutv = [], [], []

            def dma_w2(idx):
                so = H_W2S + (idx % 2) * 2048
                oc, hh = idx // 2, idx % 2
                if idx >= 2:
                    em.wait("sync", "sT", w2mm[idx - 2])
                em.op("sync", lambda e, so=so, oc=oc, hh=hh: e.dma_start(
                    out=work[:, so:so + 2048].rearrange(
                        "p (k c) -> p k c", c=128),
                    in_=W2[hh * 2048:(hh + 1) * 2048,
                           oc * 128:(oc + 1) * 128].rearrange(
                        "(k p) c -> p k c", p=128)), inc="dW", n=16)
                w2_ready.append(em.v("dW"))

            em.wait("sync", "sT")
            dma_w2(0); dma_w2(1)
            for oc in range(NCH):
                bank = pb[4 if oc % 2 == 0 else 6]
                if oc >= 2:
                    em.wait("tensor", "sV", toutv[oc - 2])
                for hh in range(2):
                    idx = oc * 2 + hh
                    so = H_W2S + (idx % 2) * 2048
                    em.wait("tensor", "dW", w2_ready[idx])
                    for k in range(16):
                        kk = hh * 16 + k
                        em.op("tensor", lambda e, so=so, k=k, kk=kk, bank=bank:
                              e.matmul(
                            bank[:, 0:TB],
                            work[:, so + k * 128:so + (k + 1) * 128],
                            work[:, H_HB + kk * TB:H_HB + kk * TB + TB],
                            start=(kk == 0), stop=(kk == 31)),
                            inc=("sT" if k == 15 else None))
                    w2mm.append(em.v("sT"))
                    if idx + 2 < 16:
                        dma_w2(idx + 2)
                mmv = em.cnt["sT"]
                em.wait("vector", "sT", mmv)
                em.op("vector", lambda e, oc=oc, bank=bank:
                      e.scalar_tensor_tensor(
                    work[:, H_TOUT + oc * TB:H_TOUT + (oc + 1) * TB],
                    bank[:, 0:TB], bpc("b2", oc),
                    work[:, H_X2P + oc * TB:H_X2P + (oc + 1) * TB],
                    op0=OP.add, op1=OP.add), inc="sV")
                toutv.append(em.v("sV"))
                scan_pump(6)
            em.wait("sync", "sT")
            em.op("sync", lambda e, t2=t2: e.dma_start(
                out=work[:, H_GT:H_GT + 2048].rearrange(
                    "p (ch t) -> p ch t", t=TB),
                in_=gate_scr[:, t2 * TB:(t2 + 1) * TB].rearrange(
                    "(ch p) t -> p ch t", p=128)), inc="dW", n=16)
            em.op("sync", lambda e: e.dma_start(
                out=work[0:16, H_SQF + 512:H_SQF + 1536], in_=Cmat[:, :]),
                inc="dW", n=16)
            gt_ld = em.v("dW")
            need = (4 * t2 + 4) * 128
            if scan_state["next"] < need:
                _scan_emit(need - scan_state["next"])
            em.wait("tensor", "zS", 2 * need)
            em.wait("tensor", "dW", gt_ld)
            em.wait("tensor", "sV")
            mbv = []
            for oc in range(NCH):
                bE, bO = pb[7], pb[4]
                if oc >= 1:
                    em.wait("tensor", "sV", mbv[-1])
                for par, bank in ((0, bE), (1, bO)):
                    for blk in range(2):
                        g = 2 * (2 * t2 + blk) + par
                        em.op("tensor", lambda e, oc=oc, g=g, blk=blk,
                              bank=bank: e.matmul(
                            bank[:, blk * 128:(blk + 1) * 128],
                            work[0:16, H_SQF + 512 + oc * 128:
                                 H_SQF + 512 + (oc + 1) * 128],
                            hall[:, 1 + g * 128:1 + (g + 1) * 128],
                            start=(blk == 0), stop=(blk == 1)),
                            inc=("sT" if (par == 1 and blk == 1) else None))
                mmv = em.v("sT")
                em.wait("vector", "sT", mmv)
                em.op("vector", lambda e, oc=oc: e.tensor_scalar_mul(
                    work[:, H_MB + oc * TB:H_MB + (oc + 1) * TB],
                    pb[7][:, 0:TB], bpc("selE")), inc="sV")
                em.op("vector", lambda e, oc=oc: e.scalar_tensor_tensor(
                    work[:, H_MB + oc * TB:H_MB + (oc + 1) * TB],
                    pb[4][:, 0:TB], bpc("selO"),
                    work[:, H_MB + oc * TB:H_MB + (oc + 1) * TB],
                    op0=OP.mult, op1=OP.add), inc="sV")
                em.op("vector", lambda e, oc=oc: e.tensor_mul(
                    work[:, H_MB + oc * TB:H_MB + (oc + 1) * TB],
                    work[:, H_MB + oc * TB:H_MB + (oc + 1) * TB],
                    work[:, H_GT + oc * TB:H_GT + (oc + 1) * TB]), inc="sV")
                em.op("vector", lambda e, oc=oc, t2=t2: e.tensor_add(
                    work[:, H_MB + oc * TB:H_MB + (oc + 1) * TB],
                    work[:, H_MB + oc * TB:H_MB + (oc + 1) * TB],
                    strm[:, oc * T + t2 * TB:oc * T + (t2 + 1) * TB]),
                    inc="sV")
                mbv.append(em.v("sV"))
            MB_DONE = em.v("sV")

            wf_ready, wfmm, fpv = [], [], []

            def dma_wf(oc):
                so = H_WFS + (oc % 2) * 2048
                if oc >= 2:
                    em.wait("sync", "sT", wfmm[oc - 2])
                em.op("sync", lambda e, so=so, oc=oc: e.dma_start(
                    out=work[:, so:so + 2048].rearrange(
                        "p (k c) -> p k c", c=128),
                    in_=Wf[:, oc * 128:(oc + 1) * 128].rearrange(
                        "(k p) c -> p k c", p=128)), inc="dW", n=16)
                wf_ready.append(em.v("dW"))

            em.wait("sync", "sT")
            dma_wf(0); dma_wf(1)
            em.wait("tensor", "sV", MB_DONE)
            for oc in range(NCH):
                so = H_WFS + (oc % 2) * 2048
                bank = pb[oc % 2]
                em.wait("tensor", "dW", wf_ready[oc])
                if oc >= 2:
                    em.wait("tensor", "sV", fpv[oc - 2])
                for k in range(16):
                    src = H_MB if k < 8 else H_TOUT
                    kk = k if k < 8 else k - 8
                    em.op("tensor", lambda e, so=so, k=k, kk=kk, src=src,
                          bank=bank: e.matmul(
                        bank[:, 0:TB],
                        work[:, so + k * 128:so + (k + 1) * 128],
                        work[:, src + kk * TB:src + kk * TB + TB],
                        start=(k == 0), stop=(k == 15)),
                        inc=("sT" if k == 15 else None))
                mmv = em.v("sT")
                wfmm.append(mmv)
                if oc + 2 < NCH:
                    dma_wf(oc + 2)
                em.wait("vector", "sT", mmv)
                em.op("vector", lambda e, oc=oc, bank=bank: e.tensor_scalar_add(
                    work[:, H_FP + oc * TB:H_FP + (oc + 1) * TB],
                    bank[:, 0:TB], bpc("bf", oc)), inc="sV")
                fpv.append(em.v("sV"))
            ln_stats(work, H_FP, TB, TB, ST_MUT, ST_RT, H_SQF)
            ln_apply(work, H_FP, TB, TB, ST_MUT, ST_RT, work, H_FP, TB,
                     gam="lnfg", bet="lnfb")
            FUSED_DONE = em.v("sV")
            em.wait("sync", "sV", FUSED_DONE)
            em.op("sync", lambda e, t2=t2: e.dma_start(
                out=work[:, H_XN2:H_XN2 + 2048].rearrange(
                    "p (ch t) -> p ch t", t=TB),
                in_=x2_scr[:, t2 * TB:(t2 + 1) * TB].rearrange(
                    "(ch p) t -> p ch t", p=128)), inc="dW", n=16)
            x2_ld = em.v("dW")
            em.wait("vector", "dW", x2_ld)
            for oc in range(NCH):
                em.op("vector", lambda e, oc=oc, t2=t2: e.tensor_add(
                    strm[:, oc * T + t2 * TB:oc * T + (t2 + 1) * TB],
                    work[:, H_FP + oc * TB:H_FP + (oc + 1) * TB],
                    work[:, H_XN2 + oc * TB:H_XN2 + (oc + 1) * TB]), inc="sV")
            bar()

        tap_stream("x3")
        if STOP_AFTER == "s3":
            finish()
            return nc

        ffn_phase()
        finish()
    return nc


def _pack_col(vec):
    return np.ascontiguousarray(vec.reshape(-1, 128).T.astype(np.float32))


def kernel(L0_attn_a_x, L0_attn_a_ln_gamma, L0_attn_a_ln_beta, L0_attn_a_w_qkv,
           L0_attn_a_w_out, L0_attn_a_b_out, L0_attn_b_ln_g, L0_attn_b_ln_b,
           L0_attn_b_Wdelta, L0_attn_b_bdelta, L0_attn_b_Win, L0_attn_b_b_in,
           L0_attn_b_Wgate, L0_attn_b_bgate, L0_attn_b_A, L0_attn_b_C,
           L0_attn_b_ln1_g, L0_attn_b_ln1_b, L0_attn_b_Wq, L0_attn_b_bq,
           L0_attn_b_Wk, L0_attn_b_bk, L0_attn_b_Wv, L0_attn_b_bv,
           L0_attn_b_Wo, L0_attn_b_bo, L0_attn_b_ln2_g, L0_attn_b_ln2_b,
           L0_attn_b_W1, L0_attn_b_b1, L0_attn_b_W2, L0_attn_b_b2,
           L0_attn_b_Wf, L0_attn_b_bf, L0_attn_b_lnf_g, L0_attn_b_lnf_b,
           L0_ffn_w_gate, L0_ffn_w_up, L0_ffn_w_down):
    x = np.asarray(L0_attn_a_x)
    w_qkv = np.asarray(L0_attn_a_w_qkv)
    wqT = np.ascontiguousarray(w_qkv[0:1024].T)
    wkT = np.ascontiguousarray(w_qkv[1024:2048].T)
    wvT = np.ascontiguousarray(w_qkv[2048:3072].T)
    woT = np.ascontiguousarray(np.asarray(L0_attn_a_w_out).T)
    wgT = np.ascontiguousarray(np.asarray(L0_ffn_w_gate).T)
    wuT = np.ascontiguousarray(np.asarray(L0_ffn_w_up).T)
    wdT = np.ascontiguousarray(np.asarray(L0_ffn_w_down).T)

    shared = {
        "wqT": wqT, "wkT": wkT, "wvT": wvT, "woT": woT,
        "wgT": wgT, "wuT": wuT, "wdT": wdT,
        "Wdl": np.ascontiguousarray(L0_attn_b_Wdelta),
        "Win": np.ascontiguousarray(L0_attn_b_Win),
        "Wgt": np.ascontiguousarray(L0_attn_b_Wgate),
        "Amat": np.ascontiguousarray(L0_attn_b_A),
        "Cmat": np.ascontiguousarray(L0_attn_b_C),
        "Wq": np.ascontiguousarray(L0_attn_b_Wq),
        "Wk": np.ascontiguousarray(L0_attn_b_Wk),
        "Wv": np.ascontiguousarray(L0_attn_b_Wv),
        "Wo": np.ascontiguousarray(L0_attn_b_Wo),
        "W1": np.ascontiguousarray(L0_attn_b_W1),
        "W2": np.ascontiguousarray(L0_attn_b_W2),
        "Wf": np.ascontiguousarray(L0_attn_b_Wf),
        "rpv": np.ascontiguousarray(np.asarray(L0_attn_b_bv)[None, :]),
        "identv": np.eye(128, dtype=np.float32),
        "onesv": np.ones((128, 128), np.float32),
    }

    diag = np.where(np.arange(128)[:, None] > np.arange(128)[None, :],
                    np.float32(NEG), np.float32(0.0)).astype(np.float32)
    full = np.full((128, 128), np.float32(NEG), np.float32)
    zero = np.zeros((128, 128), np.float32)

    in_maps = []
    for c in range(8):
        b, p = c // 2, c % 2
        bp_arr = np.zeros((128, NBP), np.float32)

        def put(name, vec):
            v = np.asarray(vec).astype(np.float32)
            n = v.size // 128
            bp_arr[:, BP[name]:BP[name] + n] = v.reshape(n, 128).T

        put("bout_a", L0_attn_a_b_out)
        put("gam_a", L0_attn_a_ln_gamma)
        put("bet_a", L0_attn_a_ln_beta)
        put("lng", L0_attn_b_ln_g); put("lnb", L0_attn_b_ln_b)
        put("ln1g", L0_attn_b_ln1_g); put("ln1b", L0_attn_b_ln1_b)
        put("ln2g", L0_attn_b_ln2_g); put("ln2b", L0_attn_b_ln2_b)
        put("lnfg", L0_attn_b_lnf_g); put("lnfb", L0_attn_b_lnf_b)
        put("bgate", L0_attn_b_bgate)
        put("bq", L0_attn_b_bq); put("bk", L0_attn_b_bk)
        put("bo", L0_attn_b_bo)
        put("b1", L0_attn_b_b1); put("b2", L0_attn_b_b2)
        put("bf", L0_attn_b_bf)
        bp_arr[0:16, BP["bdelta"]] = np.asarray(L0_attn_b_bdelta)
        bp_arr[0:16, BP["b_in"]] = np.asarray(L0_attn_b_b_in)
        bp_arr[:, BP["selE"]] = 1.0 if p == 0 else 0.0
        bp_arr[:, BP["selO"]] = 1.0 if p == 1 else 0.0
        bp_arr[:, BP["c_eps"]] = EPS
        bp_arr[:, BP["c_nsha"]] = -SHIFT_A
        bp_arr[:, BP["c_nshb"]] = -SHIFT_B
        bp_arr[:, BP["c_inv127"]] = INV127
        bp_arr[:, BP["c_i1024"]] = 1.0 / 1024.0
        bp_arr[:, BP["c_neg1"]] = -1.0

        xb = x[b].reshape(32, 128, D)[p::2].reshape(T, D)
        m = dict(shared)
        m["x0"] = np.ascontiguousarray(xb.T)
        m["bpv"] = bp_arr
        m["m0v"] = diag if p == 0 else zero
        m["m1v"] = full if p == 0 else diag
        in_maps.append(m)

    nc = build_nc()
    res = run_bass_kernel_spmd(nc, in_maps, core_ids=list(range(8)))

    out_full = np.empty((B, S, D), np.float32)
    for c in range(8):
        b, p = c // 2, c % 2
        sh = res.results[c]["out"]
        out_full[b].reshape(32, 128, D)[p::2] = sh.T.reshape(16, 128, D)
    if TAPS:
        kernel.tap_results = []
        for c in range(8):
            kernel.tap_results.append(
                {t: res.results[c]["tap_" + t] for t in TAPS})
    return out_full


# version 07
# speedup vs baseline: 1.0254x; 1.0254x over previous
import contextlib
import numpy as np
import concourse.bass as bass
import concourse.mybir as mybir
from concourse.bass_utils import run_bass_kernel_spmd

F32 = mybir.dt.float32
I32 = mybir.dt.int32
AF = mybir.ActivationFunctionType
OP = mybir.AluOpType

B, S, D = 4, 4096, 1024
T = 2048
NCH = 8
TT = 512
NTT = T // TT
EPS = 1e-5
SHIFT_A = 24.0
SHIFT_B = 10.0
QSC_A = 0.125
QSC_B = float(1.0 / np.sqrt(np.float32(128.0)))
INV127 = float(np.float32(1.0) / np.float32(127.0))
PAIR_RG = [[0, 1], [2, 3], [4, 5], [6, 7]]
NEG = -1.0e30
NSCAN = S

SCAN_INTERLEAVE = True
DBG_BARRIER = False
DBG_ABAR = False
DBG_NONORM = False
TAPS = []
STOP_AFTER = None

BP = {}
_c = 0
for _name, _n in [("bout_a", 8), ("gam_a", 8), ("bet_a", 8), ("lng", 8),
                  ("lnb", 8), ("ln1g", 8), ("ln1b", 8), ("ln2g", 8),
                  ("ln2b", 8), ("lnfg", 8), ("lnfb", 8), ("bgate", 8),
                  ("bq", 8), ("bk", 8), ("bo", 8), ("b1", 32), ("b2", 8),
                  ("bf", 8), ("bdelta", 1), ("b_in", 1), ("selE", 1),
                  ("selO", 1), ("c_eps", 1), ("c_nsha", 1), ("c_nshb", 1),
                  ("c_inv127", 1), ("c_i1024", 1), ("c_neg1", 1)]:
    BP[_name] = _c
    _c += _n
NBP = _c

ENGS = ["sync", "tensor", "vector", "scalar", "gpsimd"]


class Em:
    def __init__(self, nc, stack):
        self.nc = nc
        self.q = {e: [] for e in ENGS}
        self.sem = {}
        self.cnt = {}
        for s in ["dW", "dG", "sT", "sV", "sS", "sC", "zS", "zT", "st"]:
            self.sem[s] = stack.enter_context(nc.semaphore(s))
            self.cnt[s] = 0

    def v(self, s):
        return self.cnt[s]

    def wait(self, eng, s, val=None):
        v = self.cnt[s] if val is None else val
        if v <= 0:
            return
        h = self.sem[s]
        self.q[eng].append(lambda e: e.wait_ge(h, v))

    def op(self, eng, builder, inc=None, n=1):
        if inc is None:
            self.q[eng].append(lambda e: builder(e))
            return None
        h = self.sem[inc]
        self.q[eng].append(lambda e: builder(e).then_inc(h, n))
        self.cnt[inc] += n
        return self.cnt[inc]

    def flush(self, block):
        for name in ENGS:
            ops = self.q[name]
            if not ops:
                continue

            def runner(e, ops=ops):
                for f in ops:
                    f(e)
            getattr(block, name)(runner)


def build_nc():
    nc = bass.Bass("TRN2", target_bir_lowering=False)
    di = lambda n, s, dt=F32: nc.dram_tensor(n, s, dt, kind="ExternalInput").ap()
    do = lambda n, s, dt=F32: nc.dram_tensor(n, s, dt, kind="ExternalOutput").ap()
    dd = lambda n, s, dt=F32: nc.dram_tensor(n, s, dt).ap()

    x0 = di("x0", [D, T])
    wqT = di("wqT", [D, D]); wkT = di("wkT", [D, D]); wvT = di("wvT", [D, D])
    woT = di("woT", [D, D])
    wgT = di("wgT", [D, 3072], I32); wuT = di("wuT", [D, 3072], I32)
    wdT = di("wdT", [3072, D], I32)
    Wdl = di("Wdl", [D, 16]); Win = di("Win", [D, 16])
    Wgt = di("Wgt", [D, D])
    Amat = di("Amat", [16, 16]); Cmat = di("Cmat", [16, D])
    Wq = di("Wq", [D, D]); Wk = di("Wk", [D, D]); Wv = di("Wv", [D, D])
    Wo = di("Wo", [D, D])
    W1 = di("W1", [D, 4096]); W2 = di("W2", [4096, D]); Wf = di("Wf", [2 * D, D])
    bp_in = di("bpv", [128, NBP])
    rp_in = di("rpv", [1, D])
    ident_in = di("identv", [128, 128])
    m0_in = di("m0v", [128, 128])
    m1_in = di("m1v", [128, 128])
    ones_in = di("onesv", [128, 128])

    out = do("out", [D, T])
    _tap_shapes = {"q_scr": [D, T], "ka_i": [D, T], "va_i": [T, 1040],
                   "ctxa": [T, D], "qb_scr": [D, T], "kb_i": [D, T],
                   "vb_i": [T, 1032], "ctxb": [T, D], "gate": [D, T],
                   "m_i": [48, T], "x2p": [D, T], "hall": [16, S + 1],
                   "kdbg": [128, 4096], "vdbg": [128, 4160]}
    taps = {t: do("tap_" + t, _tap_shapes.get(t, [D, T])) for t in TAPS}

    q_scr = dd("q_scr", [D, T])
    qb_scr = dd("qb_scr", [D, T])
    gate_scr = dd("gate_scr", [D, T])
    x2_scr = dd("x2_scr", [D, T])
    x2p_scr = dd("x2p_scr", [D, T])
    ctxa_scr = dd("ctxa_scr", [T, D])
    ctxb_scr = dd("ctxb_scr", [T, D])
    ka_i = dd("ka_i", [D, T]); ka_o = dd("ka_o", [2, D, T])
    va_i = dd("va_i", [T, 1040]); va_o = dd("va_o", [2, T, 1040])
    m_i = dd("m_i", [48, T]); m_o = dd("m_o", [2, 48, T])
    kb_i = dd("kb_i", [D, T]); kb_o = dd("kb_o", [2, D, T])
    vb_i = dd("vb_i", [T, 1032]); vb_o = dd("vb_o", [2, T, 1032])

    stack = contextlib.ExitStack()
    with stack:
        work = stack.enter_context(nc.sbuf_tensor("work", [128, 21536], F32))
        strm = stack.enter_context(nc.sbuf_tensor("strm", [128, NCH * T], F32))
        bp = stack.enter_context(nc.sbuf_tensor("bps", [128, NBP], F32))
        ident = stack.enter_context(nc.sbuf_tensor("identt", [128, 128], F32))
        m0 = stack.enter_context(nc.sbuf_tensor("m0t", [128, 128], F32))
        m1 = stack.enter_context(nc.sbuf_tensor("m1t", [128, 128], F32))
        ones = stack.enter_context(nc.sbuf_tensor("ones_t", [128, 128], F32))
        strow = stack.enter_context(nc.sbuf_tensor("strow", [128, 2048], F32))
        A_sb = stack.enter_context(nc.sbuf_tensor("A_sb", [16, 16], F32))
        dfull = stack.enter_context(nc.sbuf_tensor("dfull", [16, S], F32))
        btfull = stack.enter_context(nc.sbuf_tensor("btfull", [16, S], F32))
        hall = stack.enter_context(nc.sbuf_tensor("hallt", [16, S + 1], F32))
        hd = stack.enter_context(nc.sbuf_tensor("hdt", [16, 1], F32))

        pb = [stack.enter_context(nc.psum_tensor(f"pb{k}", [128, 512], F32))
              for k in range(8)]

        for cname, cval in [("c_eps", EPS), ("c_nsha", -SHIFT_A),
                            ("c_nshb", -SHIFT_B), ("c_inv127", INV127),
                            ("c_i1024", 1.0 / 1024.0), ("c_neg1", -1.0)]:
            nc.const_aps.register(F32, cval, bp[:, BP[cname]:BP[cname] + 1])

        em = Em(nc, stack)
        block = stack.enter_context(nc.Block())

        ST_MU2X, ST_R2X = (0, 0), (32, 0)
        ST_MUT, ST_RT = (64, 0), (64, 512)
        ST_BV = (64, 1024)
        ST_A, ST_B = (64, 1024), (64, 1536)

        def srow(slot, off, n):
            r, c = slot
            return strow[r:r + 1, c + off:c + off + n]

        def sl(slot, off):
            return (slot[0], slot[1] + off)

        def bpc(name, k=0):
            return bp[:, BP[name] + k:BP[name] + k + 1]

        def bar():
            for _e in ENGS:
                for _s in ["sT", "sV", "sS", "dW", "dG"]:
                    em.wait(_e, _s)

        scan_state = {"next": 0, "ready": False}

        def scan_pump(nsteps):
            if not scan_state["ready"] or not SCAN_INTERLEAVE:
                return
            _scan_emit(nsteps)

        def _scan_emit(nsteps):
            t0 = scan_state["next"]
            t1 = min(t0 + nsteps, NSCAN)
            for t in range(t0, t1):
                em.op("scalar", lambda e, t=t: e.activation(
                    out=hd[:, :], in_=hall[:, t:t + 1], func=AF.Identity,
                    scale=dfull[:, t:t + 1]), inc="zS")
                em.wait("tensor", "zS", 2 * t + 1)
                em.op("tensor", lambda e: e.matmul(
                    pb[5][0:16, 0:1], A_sb[:, :], hd[:, :], start=True,
                    stop=True), inc="zT")
                em.wait("scalar", "zT", t + 1)
                em.op("scalar", lambda e, t=t: e.activation(
                    out=hall[:, t + 1:t + 2], in_=pb[5][0:16, 0:1],
                    func=AF.Tanh, bias=btfull[:, t:t + 1]), inc="zS")
            scan_state["next"] = t1

        def ln_stats(sbuf, soff, sstride, n, mu_slot, r_slot, sq_off,
                     bank_s=6, bank_q=7):
            em.wait("tensor", "sS")
            em.wait("tensor", "sV")
            em.wait("scalar", "sT")
            em.wait("scalar", "sV")
            em.wait("scalar", "dG")
            for k in range(NCH):
                cw = soff + k * sstride
                last = (k == NCH - 1)
                em.op("tensor", lambda e, cw=cw, k=k, last=last: e.matmul(
                    pb[bank_s][0:1, 0:n], ones[:, 0:1], sbuf[:, cw:cw + n],
                    start=(k == 0), stop=last), inc=("sT" if last else None))
            mm_sum = em.v("sT")
            sq_mm = []
            for k in range(NCH):
                cw = soff + k * sstride
                po = sq_off + (k % 2) * n
                if k >= 2:
                    em.wait("scalar", "sT", sq_mm[k - 2])
                em.op("scalar", lambda e, cw=cw, po=po: e.activation(
                    out=work[:, po:po + n], in_=sbuf[:, cw:cw + n],
                    func=AF.Square), inc="sS")
                em.wait("tensor", "sS")
                last = (k == NCH - 1)
                em.op("tensor", lambda e, po=po, k=k, last=last: e.matmul(
                    pb[bank_q][0:1, 0:n], ones[:, 0:1], work[:, po:po + n],
                    start=(k == 0), stop=last), inc="sT")
                sq_mm.append(em.v("sT"))
            mm_sq = em.v("sT")
            em.wait("scalar", "sT", mm_sum)
            em.op("scalar", lambda e: e.mul(
                out=srow(mu_slot, 0, n), in_=pb[bank_s][0:1, 0:n],
                mul=1.0 / 1024.0), inc="sS")
            em.wait("scalar", "sT", mm_sq)
            em.op("scalar", lambda e: e.mul(
                out=srow(ST_A, 0, n), in_=pb[bank_q][0:1, 0:n],
                mul=1.0 / 1024.0), inc="sS")
            em.op("scalar", lambda e: e.activation(
                out=srow(ST_B, 0, n),
                in_=srow(mu_slot, 0, n), func=AF.Square), inc="sS")
            em.wait("vector", "sS")
            em.op("vector", lambda e: e.tensor_sub(
                srow(ST_A, 0, n), srow(ST_A, 0, n), srow(ST_B, 0, n)),
                inc="sV")
            em.wait("scalar", "sV")
            em.op("scalar", lambda e: e.activation(
                out=srow(ST_B, 0, n), in_=srow(ST_A, 0, n),
                func=AF.Sqrt, bias=EPS), inc="sS")
            em.wait("vector", "sS")
            em.op("vector", lambda e: e.reciprocal(
                srow(r_slot, 0, n), srow(ST_B, 0, n)), inc="sV")

        def ln_apply(sbuf, soff, sstride, n, mu_slot, r_slot,
                     dbuf, doff, dstride, gam=None, bet=None,
                     bank_mu=4, bank_r=6):
            em.wait("tensor", "sS")
            em.wait("tensor", "sV")
            rmu, rr = mu_slot[0], r_slot[0]
            em.op("tensor", lambda e: e.matmul(
                pb[bank_mu][:, 0:n], ones[rmu:rmu + 1, :],
                srow(mu_slot, 0, n), start=True, stop=True), inc="sT")
            em.op("tensor", lambda e: e.matmul(
                pb[bank_r][:, 0:n], ones[rr:rr + 1, :],
                srow(r_slot, 0, n), start=True, stop=True), inc="sT")
            bc = em.v("sT")
            em.wait("vector", "sT", bc)
            for k in range(NCH):
                cw = soff + k * sstride
                dw = doff + k * dstride
                em.op("vector", lambda e, cw=cw, dw=dw: e.tensor_sub(
                    dbuf[:, dw:dw + n], sbuf[:, cw:cw + n],
                    pb[bank_mu][:, 0:n]), inc="sV")
                em.op("vector", lambda e, dw=dw: e.tensor_mul(
                    dbuf[:, dw:dw + n], dbuf[:, dw:dw + n], pb[bank_r][:, 0:n]),
                    inc="sV")
                if gam is not None:
                    em.op("vector", lambda e, dw=dw, k=k: e.tensor_scalar(
                        dbuf[:, dw:dw + n], dbuf[:, dw:dw + n], bpc(gam, k),
                        bpc(bet, k), op0=OP.mult, op1=OP.add), inc="sV")

        def proj(wsrc, n_out, xn_off, n, slab_off, kch, evac, cast=False,
                 scan_steps=0, xn_stride=None):
            xst = xn_stride if xn_stride is not None else n
            slab_cols = kch * 128
            queue = "gpsimd" if cast else "sync"
            dsem = "dG" if cast else "dW"
            slab_ready, mm_done, evac_done = [], [], []

            def dma_slab(oc):
                so = slab_off + (oc % 2) * slab_cols
                if oc >= 2:
                    em.wait(queue, "sT", mm_done[oc - 2])
                em.op(queue, lambda e, so=so, oc=oc: e.dma_start(
                    out=work[:, so:so + slab_cols].rearrange(
                        "p (k c) -> p k c", c=128),
                    in_=wsrc[:, oc * 128:(oc + 1) * 128].rearrange(
                        "(k p) c -> p k c", p=128)), inc=dsem, n=16)
                slab_ready.append(em.v(dsem))

            em.wait(queue, "sT")
            em.wait("tensor", "sV")
            em.wait("tensor", "sS")
            dma_slab(0)
            if n_out > 1:
                dma_slab(1)
            for oc in range(n_out):
                so = slab_off + (oc % 2) * slab_cols
                bank = pb[oc % 2]
                em.wait("tensor", dsem, slab_ready[oc])
                if oc >= 2:
                    sem_n, sem_v = evac_done[oc - 2]
                    em.wait("tensor", sem_n, sem_v)
                for k in range(kch):
                    last = (k == kch - 1)
                    em.op("tensor", lambda e, so=so, k=k, last=last, bank=bank:
                          e.matmul(
                        bank[:, 0:n],
                        work[:, so + k * 128:so + (k + 1) * 128],
                        work[:, xn_off + k * xst:xn_off + k * xst + n],
                        start=(k == 0), stop=last),
                        inc=("sT" if last else None))
                mm_done.append(em.v("sT"))
                if oc + 2 < n_out:
                    dma_slab(oc + 2)
                evac_done.append(evac(oc, bank, mm_done[oc]))
                if scan_steps:
                    scan_pump(scan_steps)

        for dst, src in [(bp, bp_in), (ident, ident_in), (m0, m0_in),
                         (m1, m1_in), (ones, ones_in), (A_sb, Amat)]:
            em.op("sync", lambda e, dst=dst, src=src: e.dma_start(
                out=dst[:, :], in_=src[:, :]), inc="dW", n=16)
        LD_CONST = em.v("dW")
        em.op("gpsimd", lambda e: e.dma_start(
            out=strm[:, :].rearrange("p (ch t) -> p ch t", t=T),
            in_=x0[:, :].rearrange("(ch p) t -> p ch t", p=128)), inc="dG", n=16)
        LD_X0 = em.v("dG")
        for eng in ["tensor", "vector", "scalar"]:
            em.wait(eng, "dW", LD_CONST)
            em.wait(eng, "dG", LD_X0)

        def tap_stream(name):
            if name in TAPS:
                em.wait("gpsimd", "sV")
                em.op("gpsimd", lambda e, name=name: e.dma_start(
                    out=taps[name][:, :].rearrange("(ch p) t -> p ch t", p=128),
                    in_=strm[:, :].rearrange("p (ch t) -> p ch t", t=T)),
                    inc="dG", n=16)

        def finish():
            _scr = {"q_scr": q_scr, "ka_i": ka_i, "va_i": va_i,
                    "ctxa": ctxa_scr, "qb_scr": qb_scr, "kb_i": kb_i,
                    "vb_i": vb_i, "ctxb": ctxb_scr, "gate": gate_scr,
                    "m_i": m_i, "x2p": x2p_scr}
            em.wait("gpsimd", "sV")
            em.wait("gpsimd", "sS")
            em.wait("gpsimd", "dW")
            for tname in TAPS:
                if tname in _scr:
                    em.op("gpsimd", lambda e, tname=tname: e.dma_start(
                        out=taps[tname][:, :], in_=_scr[tname][:, :]),
                        inc="dG", n=16)
                elif tname == "hall":
                    em.op("gpsimd", lambda e: e.dma_start(
                        out=taps["hall"][:, :], in_=hall[:, :]), inc="dG", n=16)
            em.op("gpsimd", lambda e: e.dma_start(
                out=out[:, :].rearrange("(ch p) t -> p ch t", p=128),
                in_=strm[:, :].rearrange("p (ch t) -> p ch t", t=T)),
                inc="st", n=16)
            em.wait("gpsimd", "st")
            em.flush(block)

        W_N0, W_XN, W_SLAB = 0, 4096, 8192
        W_QTMP, W_SQ = 10240, 11264
        W_VSLAB, W_VTMP = 11264, 19456

        for tt in range(NTT):
            tw = tt * TT
            ln_stats(strm, tw, T, TT, ST_MUT, ST_RT, W_SQ)
            ln_apply(strm, tw, T, TT, ST_MUT, ST_RT, work, W_N0, TT)
            ln_stats(work, W_N0, TT, TT, ST_MUT, ST_RT, W_SQ)
            ln_apply(work, W_N0, TT, TT, ST_MUT, ST_RT, work, W_XN, TT,
                     gam="gam_a", bet="bet_a")
            if "xn_dbg" in TAPS:
                em.wait("gpsimd", "sV")
                em.op("gpsimd", lambda e, tt=tt: e.dma_start(
                    out=taps["xn_dbg"][:, tt * TT:(tt + 1) * TT].rearrange(
                        "(ch p) t -> p ch t", p=128),
                    in_=work[:, W_XN:W_XN + 4096].rearrange(
                        "p (ch t) -> p ch t", t=TT)), inc="dG", n=16)
                em.wait("tensor", "dG")
            em.wait("tensor", "sV")

            def make_ev(scr, scale=None, copy=False, tt=tt):
                dmas = []

                def ev(oc, bank, mmv, tt=tt):
                    po = W_QTMP + (oc % 2) * TT
                    em.wait("vector", "sT", mmv)
                    if len(dmas) >= 2:
                        em.wait("vector", "dG", dmas[-2])
                    else:
                        em.wait("vector", "dG")
                    if copy:
                        em.op("vector", lambda e, po=po, bank=bank: e.tensor_copy(
                            work[:, po:po + TT], bank[:, 0:TT]), inc="sV")
                    else:
                        em.op("vector", lambda e, po=po, bank=bank:
                              e.tensor_scalar_mul(
                            work[:, po:po + TT], bank[:, 0:TT], scale), inc="sV")
                    sv = em.v("sV")
                    em.wait("gpsimd", "sV", sv)
                    em.op("gpsimd", lambda e, po=po, oc=oc, tt=tt: e.dma_start(
                        out=scr[oc * 128:(oc + 1) * 128, tt * TT:(tt + 1) * TT],
                        in_=work[:, po:po + TT]), inc="dG", n=16)
                    dmas.append(em.v("dG"))
                    return ("sV", sv)
                return ev

            proj(wqT, NCH, W_XN, TT, W_SLAB, NCH, make_ev(q_scr, scale=QSC_A))
            if DBG_BARRIER:
                for _e in ["tensor", "vector", "scalar", "sync", "gpsimd"]:
                    for _s in ["sT", "sV", "sS", "dW", "dG"]:
                        em.wait(_e, _s)
            if STOP_AFTER == "qonly":
                continue
            proj(wkT, NCH, W_XN, TT, W_SLAB, NCH, make_ev(ka_i, copy=True))

            vs_ready = []
            for dvg in range(2):
                so = W_VSLAB + dvg * 4096
                em.wait("sync", "sT")
                em.op("sync", lambda e, so=so, dvg=dvg: e.dma_start(
                    out=work[:, so:so + 4096].rearrange(
                        "p (k c) -> p k c", c=512),
                    in_=wvT[:, dvg * 512:(dvg + 1) * 512].rearrange(
                        "(k p) c -> p k c", p=128)), inc="dW", n=16)
                vs_ready.append(em.v("dW"))
            vdma = []
            vev = {}
            em.wait("tensor", "sV")
            for sub in range(4):
                vo = W_VTMP + (sub % 2) * 1040
                if sub >= 2:
                    em.wait("vector", "dG", vdma[sub - 2])
                else:
                    em.wait("vector", "dG")
                for dvg in range(2):
                    so = W_VSLAB + dvg * 4096
                    bank = pb[2 + dvg]
                    em.wait("tensor", "dW", vs_ready[dvg])
                    if (sub - 1, dvg) in vev:
                        em.wait("tensor", "sV", vev[(sub - 1, dvg)])
                    for k in range(NCH):
                        em.op("tensor", lambda e, so=so, k=k, sub=sub, bank=bank:
                              e.matmul(
                            bank[:, 0:512],
                            work[:, W_XN + k * TT + sub * 128:
                                  W_XN + k * TT + sub * 128 + 128],
                            work[:, so + k * 512:so + (k + 1) * 512],
                            start=(k == 0), stop=(k == NCH - 1)),
                            inc=("sT" if k == NCH - 1 else None))
                    mmv = em.v("sT")
                    em.wait("vector", "sT", mmv)
                    em.op("vector", lambda e, vo=vo, dvg=dvg, bank=bank:
                          e.tensor_copy(
                        work[:, vo + dvg * 520:vo + dvg * 520 + 520].rearrange(
                            "p (h c) -> p h c", c=65)[:, :, 0:64],
                        bank[:, 0:512].rearrange("p (h c) -> p h c", c=64)),
                        inc="sV")
                    vev[(sub, dvg)] = em.v("sV")
                em.op("vector", lambda e, vo=vo: e.memset(
                    work[:, vo:vo + 1040].rearrange("p (h c) -> p h c", c=65)
                    [:, :, 64:65], 1.0), inc="sV")
                sv = em.v("sV")
                em.wait("gpsimd", "sV", sv)
                em.op("gpsimd", lambda e, vo=vo, tt=tt, sub=sub: e.dma_start(
                    out=va_i[tt * TT + sub * 128:tt * TT + sub * 128 + 128, :],
                    in_=work[:, vo:vo + 1040]), inc="dG", n=16)
                vdma.append(em.v("dG"))

        if STOP_AFTER == "qonly":
            finish()
            return nc
        em.wait("gpsimd", "dG")
        em.op("gpsimd", lambda e: e.collective_compute(
            "AllGather", OP.bypass, ins=[ka_i[:, :]], outs=[ka_o[:, :, :]],
            replica_groups=PAIR_RG), inc="sC")
        em.op("gpsimd", lambda e: e.collective_compute(
            "AllGather", OP.bypass, ins=[va_i[:, :]], outs=[va_o[:, :, :]],
            replica_groups=PAIR_RG), inc="sC")
        AG_A = em.v("sC")

        WA_K, WA_V, WA_QB, WA_ES, WA_RCP, WA_CTH = 0, 4096, 8256, 8512, 9024, 9216
        em.wait("sync", "sC", AG_A)
        qdma, qused = [], []
        av_evac = {}
        ctx_dmas = []
        for hg in range(NCH):
            em.wait("sync", "sT")
            for r in range(2):
                em.op("sync", lambda e, r=r, hg=hg: e.dma_start(
                    out=work[:, WA_K:WA_K + 4096].rearrange(
                        "p (g c) -> p g c", c=128)[:, r::2, :],
                    in_=ka_o[r, hg * 128:(hg + 1) * 128, :].rearrange(
                        "p (g c) -> p g c", c=128)), inc="dW", n=16)
                em.op("sync", lambda e, r=r, hg=hg: e.dma_start(
                    out=work[:, WA_V:WA_V + 4160].rearrange(
                        "p (g c) -> p g c", c=130)[:, r::2, :],
                    in_=va_o[r, :, hg * 130:(hg + 1) * 130].rearrange(
                        "(g p) c -> p g c", p=128)), inc="dW", n=16)
            kv_ready = em.v("dW")
            if hg == 0 and "kdbg" in TAPS:
                em.wait("gpsimd", "dW", kv_ready)
                em.op("gpsimd", lambda e: e.dma_start(
                    out=taps["kdbg"][:, :], in_=work[:, WA_K:WA_K + 4096]),
                    inc="dG", n=16)
                em.op("gpsimd", lambda e: e.dma_start(
                    out=taps["vdbg"][:, :], in_=work[:, WA_V:WA_V + 4160]),
                    inc="dG", n=16)
                em.wait("tensor", "dG")
            em.wait("tensor", "dW", kv_ready)
            for i in range(16):
                qo = WA_QB + (i % 2) * 128
                if len(qdma) >= 2:
                    em.wait("sync", "sT", qused[-2])
                em.op("sync", lambda e, qo=qo, hg=hg, i=i: e.dma_start(
                    out=work[:, qo:qo + 128],
                    in_=q_scr[hg * 128:(hg + 1) * 128, i * 128:(i + 1) * 128]),
                    inc="dW", n=16)
                qdma.append(em.v("dW"))
                em.wait("tensor", "dW", qdma[-1])
                em.wait("tensor", "sS")
                avb = pb[4 if i % 2 == 0 else 6]
                if (i % 2) in av_evac:
                    em.wait("tensor", "sV", av_evac[i % 2])
                ng = 2 * i + 2
                exp_done = []
                for g in range(ng):
                    par = g % 2
                    sb0, sb1 = pb[0 + par], pb[2 + par]
                    if g >= 2:
                        em.wait("tensor", "sS", exp_done[g - 2])
                    for h, sb in ((0, sb0), (1, sb1)):
                        em.op("tensor", lambda e, g=g, h=h, sb=sb, qo=qo:
                              e.matmul(
                            sb[:, 0:128],
                            work[:, WA_K + g * 128:WA_K + (g + 1) * 128]
                            [h * 64:(h + 1) * 64, :],
                            work[:, qo:qo + 128][h * 64:(h + 1) * 64, :],
                            start=True, stop=True, tile_position=(h * 64, 0)),
                            inc="sT")
                    smm = em.v("sT")
                    if g >= 2 * i:
                        mt = m0 if g == 2 * i else m1
                        em.wait("vector", "sT", smm)
                        for sb in (sb0, sb1):
                            em.op("vector", lambda e, sb=sb, mt=mt: e.tensor_add(
                                sb[:, 0:128], sb[:, 0:128], mt[:, :]), inc="sV")
                        em.wait("scalar", "sV")
                    em.wait("scalar", "sT", smm)
                    for h, sb in ((0, sb0), (1, sb1)):
                        eo = WA_ES + (par * 2 + h) * 128
                        em.op("scalar", lambda e, sb=sb, eo=eo: e.activation(
                            out=work[:, eo:eo + 128], in_=sb[:, 0:128],
                            func=AF.Exp, bias=-SHIFT_A), inc="sS")
                    exp_done.append(em.v("sS"))
                    em.wait("tensor", "sS", exp_done[-1])
                    for h in range(2):
                        eo = WA_ES + (par * 2 + h) * 128
                        em.op("tensor", lambda e, eo=eo, g=g, h=h, avb=avb,
                              ng=ng: e.matmul(
                            avb[:, h * 65:h * 65 + 65], work[:, eo:eo + 128],
                            work[:, WA_V + g * 130 + h * 65:
                                  WA_V + g * 130 + h * 65 + 65],
                            start=(g == 0 and h == 0),
                            stop=(g == ng - 1 and h == 1)),
                            inc=("sT" if (g == ng - 1 and h == 1) else None))
                av_mm = em.v("sT")
                scan_pump(2)
                em.wait("vector", "sT", av_mm)
                if i == 0 and ctx_dmas:
                    em.wait("vector", "dG", ctx_dmas[-1])
                for h in range(2):
                    if DBG_NONORM:
                        em.op("vector", lambda e, h=h, avb=avb, i=i:
                              e.tensor_copy(
                            work[:, WA_CTH + i * 128 + h * 64:
                                  WA_CTH + i * 128 + h * 64 + 63],
                            avb[:, h * 65:h * 65 + 63]), inc="sV")
                        em.op("vector", lambda e, h=h, avb=avb, i=i:
                              e.tensor_copy(
                            work[:, WA_CTH + i * 128 + h * 64 + 63:
                                  WA_CTH + i * 128 + h * 64 + 64],
                            avb[:, h * 65 + 64:h * 65 + 65]), inc="sV")
                        continue
                    rb = WA_RCP + (i % 2) * 8
                    em.wait("scalar", "sT", av_mm)
                    if (i % 2) in av_evac:
                        em.wait("scalar", "sV", av_evac[i % 2])
                    em.op("scalar", lambda e, h=h, avb=avb, rb=rb: e.activation(
                        out=work[:, rb + 4 + h:rb + 5 + h],
                        in_=avb[:, h * 65 + 64:h * 65 + 65], func=AF.Ln),
                        inc="sS")
                    em.op("scalar", lambda e, h=h, rb=rb: e.activation(
                        out=work[:, rb + h:rb + h + 1],
                        in_=work[:, rb + 4 + h:rb + 5 + h],
                        func=AF.Exp, scale=-1.0), inc="sS")
                    em.wait("vector", "sS")
                    em.op("vector", lambda e, h=h, avb=avb, i=i, rb=rb:
                          e.tensor_scalar_mul(
                        work[:, WA_CTH + i * 128 + h * 64:
                              WA_CTH + i * 128 + h * 64 + 64],
                        avb[:, h * 65:h * 65 + 64],
                        work[:, rb + h:rb + h + 1]), inc="sV")
                av_evac[i % 2] = em.v("sV")
                qused.append(em.v("sT"))
                bar()
            em.wait("gpsimd", "sV")
            em.op("gpsimd", lambda e, hg=hg: e.dma_start(
                out=ctxa_scr[:, hg * 128:(hg + 1) * 128].rearrange(
                    "(i p) c -> p i c", p=128),
                in_=work[:, WA_CTH:WA_CTH + 2048].rearrange(
                    "p (i c) -> p i c", c=128)), inc="dG", n=16)
            ctx_dmas.append(em.v("dG"))
        CTXA_SAVED = em.v("dG")

        WD_WO, WD_CTI, WD_FM = 0, 8192, 10240
        em.wait("sync", "sT")
        for oc in range(NCH):
            em.op("sync", lambda e, oc=oc: e.dma_start(
                out=work[:, WD_WO + oc * 1024:WD_WO + (oc + 1) * 1024].rearrange(
                    "p (k c) -> p k c", c=128),
                in_=woT[:, oc * 128:(oc + 1) * 128].rearrange(
                    "(k p) c -> p k c", p=128)), inc="dW", n=16)
        WO_READY = em.v("dW")
        em.wait("sync", "dG", CTXA_SAVED)
        cti_dma, cti_used = [], []
        em.wait("tensor", "dW", WO_READY)
        for i in range(16):
            co = WD_CTI + (i % 2) * 1024
            if len(cti_dma) >= 2:
                em.wait("sync", "sT", cti_used[-2])
            em.op("sync", lambda e, i=i, co=co: e.dma_start(
                out=work[:, co:co + 1024],
                in_=ctxa_scr[i * 128:(i + 1) * 128, :]), inc="dW", n=16)
            cti_dma.append(em.v("dW"))
            em.wait("tensor", "dW", cti_dma[-1])
            tr_done = []
            for ec in range(NCH):
                bank = pb[ec % 2]
                if ec >= 2:
                    em.wait("tensor", "sV", tr_done[ec - 2])
                em.op("tensor", lambda e, co=co, ec=ec, bank=bank: e.transpose(
                    bank[:, 0:128],
                    work[:, co + ec * 128:co + (ec + 1) * 128], ident[:, :]),
                    inc="sT")
                em.wait("vector", "sT")
                em.op("vector", lambda e, ec=ec, bank=bank: e.tensor_copy(
                    work[:, WD_FM + ec * 128:WD_FM + (ec + 1) * 128],
                    bank[:, 0:128]), inc="sV")
                tr_done.append(em.v("sV"))
            em.wait("tensor", "sV", tr_done[-1])
            opv = []
            for oc in range(NCH):
                bank = pb[2 + oc % 2]
                if oc >= 2:
                    em.wait("tensor", "sV", opv[oc - 2])
                for k in range(NCH):
                    em.op("tensor", lambda e, oc=oc, k=k, bank=bank: e.matmul(
                        bank[:, 0:128],
                        work[:, WD_WO + oc * 1024 + k * 128:
                              WD_WO + oc * 1024 + (k + 1) * 128],
                        work[:, WD_FM + k * 128:WD_FM + (k + 1) * 128],
                        start=(k == 0), stop=(k == NCH - 1)),
                        inc=("sT" if k == NCH - 1 else None))
                mmv = em.v("sT")
                em.wait("vector", "sT", mmv)
                em.op("vector", lambda e, oc=oc, i=i, bank=bank:
                      e.scalar_tensor_tensor(
                    strm[:, oc * T + i * 128:oc * T + (i + 1) * 128],
                    bank[:, 0:128], bpc("bout_a", oc),
                    strm[:, oc * T + i * 128:oc * T + (i + 1) * 128],
                    op0=OP.add, op1=OP.add), inc="sV")
                opv.append(em.v("sV"))
            cti_used.append(em.v("sT"))
            for _e in ["tensor", "vector", "scalar", "sync", "gpsimd"]:
                for _s in ["sT", "sV", "sS", "dW", "dG"]:
                    em.wait(_e, _s)
            scan_pump(2)
        tap_stream("x1")
        if STOP_AFTER == "s1":
            finish()
            return nc

        def ffn_phase():
            F_XN, F_SQ, F_GSL, F_USL = 0, 4096, 5120, 7168
            F_GS, F_H2, F_DSL = 4096, 9216, 0
            for tt in range(NTT):
                tw = tt * TT
                ln_stats(strm, tw, T, TT, ST_MUT, ST_RT, F_SQ)
                ln_apply(strm, tw, T, TT, ST_MUT, ST_RT, work, F_XN, TT)
                em.wait("tensor", "sV")
                gsl_ready, usl_ready, gmm, umm, h2v = [], [], [], [], []

                def dma_g(hc):
                    so = F_GSL + (hc % 2) * 1024
                    uo = F_USL + (hc % 2) * 1024
                    if hc >= 2:
                        em.wait("gpsimd", "sT", umm[hc - 2])
                    em.op("gpsimd", lambda e, so=so, hc=hc: e.dma_start(
                        out=work[:, so:so + 1024].rearrange(
                            "p (k c) -> p k c", c=128),
                        in_=wgT[:, hc * 128:(hc + 1) * 128].rearrange(
                            "(k p) c -> p k c", p=128)), inc="dG", n=16)
                    gsl_ready.append(em.v("dG"))
                    em.op("gpsimd", lambda e, uo=uo, hc=hc: e.dma_start(
                        out=work[:, uo:uo + 1024].rearrange(
                            "p (k c) -> p k c", c=128),
                        in_=wuT[:, hc * 128:(hc + 1) * 128].rearrange(
                            "(k p) c -> p k c", p=128)), inc="dG", n=16)
                    usl_ready.append(em.v("dG"))

                em.wait("gpsimd", "sT")
                dma_g(0); dma_g(1)
                for hc in range(24):
                    so = F_GSL + (hc % 2) * 1024
                    uo = F_USL + (hc % 2) * 1024
                    gb, ub = pb[hc % 2], pb[2 + hc % 2]
                    em.wait("tensor", "dG", usl_ready[hc])
                    if hc >= 2:
                        em.wait("tensor", "sV", h2v[hc - 2])
                        em.wait("tensor", "sS")
                    for k in range(NCH):
                        em.op("tensor", lambda e, so=so, k=k, gb=gb: e.matmul(
                            gb[:, 0:TT],
                            work[:, so + k * 128:so + (k + 1) * 128],
                            work[:, F_XN + k * TT:F_XN + k * TT + TT],
                            start=(k == 0), stop=(k == NCH - 1)),
                            inc=("sT" if k == NCH - 1 else None))
                    gmm.append(em.v("sT"))
                    for k in range(NCH):
                        em.op("tensor", lambda e, uo=uo, k=k, ub=ub: e.matmul(
                            ub[:, 0:TT],
                            work[:, uo + k * 128:uo + (k + 1) * 128],
                            work[:, F_XN + k * TT:F_XN + k * TT + TT],
                            start=(k == 0), stop=(k == NCH - 1)),
                            inc=("sT" if k == NCH - 1 else None))
                    umm.append(em.v("sT"))
                    if hc + 2 < 24:
                        dma_g(hc + 2)
                    go = F_GS + (hc % 2) * TT
                    em.wait("scalar", "sT", gmm[-1])
                    em.op("scalar", lambda e, go=go, gb=gb: e.activation(
                        out=work[:, go:go + TT], in_=gb[:, 0:TT], func=AF.Silu,
                        scale=INV127), inc="sS")
                    ssv = em.v("sS")
                    em.wait("vector", "sT", umm[-1])
                    em.wait("vector", "sS", ssv)
                    em.op("vector", lambda e, go=go, hc=hc, ub=ub:
                          e.scalar_tensor_tensor(
                        work[:, F_H2 + hc * TT:F_H2 + (hc + 1) * TT],
                        ub[:, 0:TT], INV127, work[:, go:go + TT],
                        op0=OP.mult, op1=OP.mult), inc="sV")
                    h2v.append(em.v("sV"))
                    scan_pump(5)
                dsl_ready, dmm, resv = [], [], []

                def dma_d(oc):
                    so = F_DSL + (oc % 2) * 3072
                    if oc >= 2:
                        em.wait("gpsimd", "sT", dmm[oc - 2])
                    em.op("gpsimd", lambda e, so=so, oc=oc: e.dma_start(
                        out=work[:, so:so + 3072].rearrange(
                            "p (k c) -> p k c", c=128),
                        in_=wdT[:, oc * 128:(oc + 1) * 128].rearrange(
                            "(k p) c -> p k c", p=128)), inc="dG", n=16)
                    dsl_ready.append(em.v("dG"))

                em.wait("gpsimd", "sT")
                dma_d(0); dma_d(1)
                em.wait("tensor", "sV", h2v[-1])
                for oc in range(NCH):
                    so = F_DSL + (oc % 2) * 3072
                    bank = pb[4 if oc % 2 == 0 else 6]
                    em.wait("tensor", "dG", dsl_ready[oc])
                    if oc >= 2:
                        em.wait("tensor", "sV", resv[oc - 2])
                    for k in range(24):
                        em.op("tensor", lambda e, so=so, k=k, bank=bank:
                              e.matmul(
                            bank[:, 0:TT],
                            work[:, so + k * 128:so + (k + 1) * 128],
                            work[:, F_H2 + k * TT:F_H2 + k * TT + TT],
                            start=(k == 0), stop=(k == 23)),
                            inc=("sT" if k == 23 else None))
                    dmm.append(em.v("sT"))
                    if oc + 2 < NCH:
                        dma_d(oc + 2)
                    em.wait("vector", "sT", dmm[-1])
                    em.op("vector", lambda e, oc=oc, tw=tw, bank=bank:
                          e.scalar_tensor_tensor(
                        strm[:, oc * T + tw:oc * T + tw + TT], bank[:, 0:TT],
                        INV127, strm[:, oc * T + tw:oc * T + tw + TT],
                        op0=OP.mult, op1=OP.add), inc="sV")
                    resv.append(em.v("sV"))
                    scan_pump(5)

        ffn_phase()
        tap_stream("x2")
        if STOP_AFTER == "s2":
            finish()
            return nc

        em.wait("gpsimd", "sV")
        em.op("gpsimd", lambda e: e.dma_start(
            out=x2_scr[:, :].rearrange("(ch p) t -> p ch t", p=128),
            in_=strm[:, :].rearrange("p (ch t) -> p ch t", t=T)), inc="dG", n=16)
        X2_SAVED = em.v("dG")

        C_XN, C_SQ, C_SLAB, C_QTMP = 0, 4096, 5120, 4096
        C_VSL, C_VTMP = 7168, 15360
        C_WDL, C_WIN, C_BVB = C_VTMP + 512, C_VTMP + 640, 17424
        em.wait("sync", "sT")
        em.wait("sync", "sV")
        em.op("sync", lambda e: e.dma_start(
            out=work[:, C_WDL:C_WDL + 128].rearrange("p (k c) -> p k c", c=16),
            in_=Wdl[:, :].rearrange("(k p) c -> p k c", p=128)), inc="dW", n=16)
        em.op("sync", lambda e: e.dma_start(
            out=work[:, C_WIN:C_WIN + 128].rearrange("p (k c) -> p k c", c=16),
            in_=Win[:, :].rearrange("(k p) c -> p k c", p=128)), inc="dW", n=16)
        WDLWIN = em.v("dW")
        em.wait("tensor", "dW", WDLWIN)
        gate_dmas = []
        for tt in range(NTT):
            tw = tt * TT
            ln_stats(strm, tw, T, TT, ST_MUT, ST_RT, C_SQ)
            if tt == 0:
                em.wait("vector", "dG", X2_SAVED)
            ln_apply(strm, tw, T, TT, ST_MUT, ST_RT, strm, tw, T)
            ln_stats(strm, tw, T, TT, ST_MUT, ST_RT, C_SQ)
            ln_apply(strm, tw, T, TT, ST_MUT, ST_RT, work, C_XN, TT,
                     gam="lng", bet="lnb")
            em.wait("tensor", "sV")
            for woff, bname, func, r0, r1 in [
                    (C_WDL, "bdelta", AF.Sigmoid, 0, 16),
                    (C_WIN, "b_in", AF.Identity, 32, 48)]:
                em.wait("tensor", "sS")
                for k in range(NCH):
                    em.op("tensor", lambda e, woff=woff, k=k: e.matmul(
                        pb[6][0:16, 0:TT],
                        work[:, woff + k * 16:woff + (k + 1) * 16],
                        work[:, C_XN + k * TT:C_XN + k * TT + TT],
                        start=(k == 0), stop=(k == NCH - 1)),
                        inc=("sT" if k == NCH - 1 else None))
                mmv = em.v("sT")
                em.wait("scalar", "sT", mmv)
                em.op("scalar", lambda e, bname=bname, func=func, r0=r0, r1=r1:
                      e.activation(
                    out=work[r0:r1, C_VTMP:C_VTMP + TT],
                    in_=pb[6][0:16, 0:TT], func=func,
                    bias=bp[0:16, BP[bname]:BP[bname] + 1]), inc="sS")
            dm_done = em.v("sS")
            em.wait("gpsimd", "sS", dm_done)
            em.op("gpsimd", lambda e, tt=tt: e.dma_start(
                out=m_i[:, tt * TT:(tt + 1) * TT],
                in_=work[0:48, C_VTMP:C_VTMP + TT]), inc="dG", n=16)

            def evg(oc, bank, mmv):
                po = C_QTMP + (oc % 2) * TT
                em.wait("scalar", "sT", mmv)
                if len(gate_dmas) >= 2:
                    em.wait("scalar", "dG", gate_dmas[-2])
                else:
                    em.wait("scalar", "dG")
                em.op("scalar", lambda e, po=po, oc=oc, bank=bank: e.activation(
                    out=work[:, po:po + TT], in_=bank[:, 0:TT], func=AF.Sigmoid,
                    bias=bpc("bgate", oc)), inc="sS")
                sv = em.v("sS")
                em.wait("gpsimd", "sS", sv)
                em.op("gpsimd", lambda e, po=po, oc=oc, tt=tt: e.dma_start(
                    out=gate_scr[oc * 128:(oc + 1) * 128, tt * TT:(tt + 1) * TT],
                    in_=work[:, po:po + TT]), inc="dG", n=16)
                gate_dmas.append(em.v("dG"))
                return ("sS", sv)
            proj(Wgt, NCH, C_XN, TT, C_SLAB, NCH, evg)

        em.wait("gpsimd", "dG")
        em.op("gpsimd", lambda e: e.collective_compute(
            "AllGather", OP.bypass, ins=[m_i[:, :]], outs=[m_o[:, :, :]],
            replica_groups=PAIR_RG), inc="sC")
        AG_M = em.v("sC")
        em.wait("sync", "sC", AG_M)
        for r in range(2):
            em.op("sync", lambda e, r=r: e.dma_start(
                out=dfull[:, :].rearrange("p (g c) -> p g c", c=128)[:, r::2, :],
                in_=m_o[r, 0:16, :].rearrange("p (g c) -> p g c", c=128)),
                inc="dW", n=16)
            em.op("sync", lambda e, r=r: e.dma_start(
                out=btfull[:, :].rearrange("p (g c) -> p g c", c=128)[:, r::2, :],
                in_=m_o[r, 32:48, :].rearrange("p (g c) -> p g c", c=128)),
                inc="dW", n=16)
        MB_READY = em.v("dW")
        em.wait("vector", "dW", MB_READY)
        em.op("vector", lambda e: e.memset(hall[:, 0:1], 0.0), inc="sV")
        HALL0 = em.v("sV")
        em.wait("scalar", "sV", HALL0)
        em.wait("scalar", "dW", MB_READY)
        em.wait("tensor", "dW", MB_READY)
        scan_state["ready"] = True
        if not SCAN_INTERLEAVE:
            _scan_emit(NSCAN)

        em.wait("sync", "sS")
        em.wait("sync", "sV")
        em.op("sync", lambda e: e.dma_start(out=srow(ST_BV, 0, 1024),
              in_=rp_in[0:1, 0:1024]), inc="dW", n=16)
        RP_LD = em.v("dW")
        em.wait("tensor", "dW", RP_LD)
        em.wait("tensor", "sV")
        em.wait("tensor", "sS")
        em.op("tensor", lambda e: e.matmul(
            pb[6][:, 0:512], ones[64:65, :], srow(ST_BV, 0, 512), start=True,
            stop=True), inc="sT")
        em.op("tensor", lambda e: e.matmul(
            pb[7][:, 0:512], ones[64:65, :], srow(ST_BV, 512, 512), start=True,
            stop=True), inc="sT")
        BVB_MM = em.v("sT")
        em.wait("vector", "sT", BVB_MM)
        em.op("vector", lambda e: e.tensor_copy(
            work[:, C_BVB:C_BVB + 512], pb[6][:, 0:512]), inc="sV")
        em.op("vector", lambda e: e.tensor_copy(
            work[:, C_BVB + 512:C_BVB + 1024], pb[7][:, 0:512]), inc="sV")
        BVB_DONE = em.v("sV")

        qbdmas, kbdmas = [], []
        for tt in range(NTT):
            tw = tt * TT
            ln_stats(strm, tw, T, TT, ST_MUT, ST_RT, C_SQ)
            ln_apply(strm, tw, T, TT, ST_MUT, ST_RT, work, C_XN, TT,
                     gam="ln1g", bet="ln1b")
            em.wait("tensor", "sV")

            def evqb(oc, bank, mmv):
                po = C_QTMP + (oc % 2) * TT
                em.wait("vector", "sT", mmv)
                if len(qbdmas) >= 2:
                    em.wait("vector", "dG", qbdmas[-2])
                else:
                    em.wait("vector", "dG")
                em.op("vector", lambda e, po=po, oc=oc, bank=bank:
                      e.tensor_scalar(
                    work[:, po:po + TT], bank[:, 0:TT], bpc("bq", oc), QSC_B,
                    op0=OP.add, op1=OP.mult), inc="sV")
                sv = em.v("sV")
                em.wait("gpsimd", "sV", sv)
                em.op("gpsimd", lambda e, po=po, oc=oc, tt=tt: e.dma_start(
                    out=qb_scr[oc * 128:(oc + 1) * 128, tt * TT:(tt + 1) * TT],
                    in_=work[:, po:po + TT]), inc="dG", n=16)
                qbdmas.append(em.v("dG"))
                return ("sV", sv)
            proj(Wq, NCH, C_XN, TT, C_SLAB, NCH, evqb, scan_steps=6)

            def evkb(oc, bank, mmv):
                po = C_QTMP + (oc % 2) * TT
                em.wait("vector", "sT", mmv)
                if len(kbdmas) >= 2:
                    em.wait("vector", "dG", kbdmas[-2])
                else:
                    em.wait("vector", "dG")
                em.op("vector", lambda e, po=po, oc=oc, bank=bank:
                      e.tensor_scalar_add(
                    work[:, po:po + TT], bank[:, 0:TT], bpc("bk", oc)),
                    inc="sV")
                sv = em.v("sV")
                em.wait("gpsimd", "sV", sv)
                em.op("gpsimd", lambda e, po=po, oc=oc, tt=tt: e.dma_start(
                    out=kb_i[oc * 128:(oc + 1) * 128, tt * TT:(tt + 1) * TT],
                    in_=work[:, po:po + TT]), inc="dG", n=16)
                kbdmas.append(em.v("dG"))
                return ("sV", sv)
            proj(Wk, NCH, C_XN, TT, C_SLAB, NCH, evkb, scan_steps=6)

            em.wait("vector", "sV", BVB_DONE)
            vs_ready = []
            for dvg in range(2):
                so = C_VSL + dvg * 4096
                em.wait("sync", "sT")
                em.op("sync", lambda e, so=so, dvg=dvg: e.dma_start(
                    out=work[:, so:so + 4096].rearrange(
                        "p (k c) -> p k c", c=512),
                    in_=Wv[:, dvg * 512:(dvg + 1) * 512].rearrange(
                        "(k p) c -> p k c", p=128)), inc="dW", n=16)
                vs_ready.append(em.v("dW"))
            vdma = []
            vev = {}
            em.wait("tensor", "sV")
            for sub in range(4):
                vo = C_VTMP + (sub % 2) * 1032
                if sub >= 2:
                    em.wait("vector", "dG", vdma[sub - 2])
                else:
                    em.wait("vector", "dG")
                for dvg in range(2):
                    so = C_VSL + dvg * 4096
                    bank = pb[2 + dvg]
                    em.wait("tensor", "dW", vs_ready[dvg])
                    if (sub - 1, dvg) in vev:
                        em.wait("tensor", "sV", vev[(sub - 1, dvg)])
                    for k in range(NCH):
                        em.op("tensor", lambda e, so=so, k=k, sub=sub, bank=bank:
                              e.matmul(
                            bank[:, 0:512],
                            work[:, C_XN + k * TT + sub * 128:
                                  C_XN + k * TT + sub * 128 + 128],
                            work[:, so + k * 512:so + (k + 1) * 512],
                            start=(k == 0), stop=(k == NCH - 1)),
                            inc=("sT" if k == NCH - 1 else None))
                    mmv = em.v("sT")
                    em.wait("vector", "sT", mmv)
                    em.op("vector", lambda e, vo=vo, dvg=dvg, bank=bank:
                          e.tensor_add(
                        work[:, vo + dvg * 516:vo + dvg * 516 + 516].rearrange(
                            "p (h c) -> p h c", c=129)[:, :, 0:128],
                        bank[:, 0:512].rearrange("p (h c) -> p h c", c=128),
                        work[:, C_BVB + dvg * 512:C_BVB + (dvg + 1) * 512]
                        .rearrange("p (h c) -> p h c", c=128)), inc="sV")
                    vev[(sub, dvg)] = em.v("sV")
                em.op("vector", lambda e, vo=vo: e.memset(
                    work[:, vo:vo + 1032].rearrange("p (h c) -> p h c", c=129)
                    [:, :, 128:129], 1.0), inc="sV")
                sv = em.v("sV")
                em.wait("gpsimd", "sV", sv)
                em.op("gpsimd", lambda e, vo=vo, tt=tt, sub=sub: e.dma_start(
                    out=vb_i[tt * TT + sub * 128:tt * TT + sub * 128 + 128, :],
                    in_=work[:, vo:vo + 1032]), inc="dG", n=16)
                vdma.append(em.v("dG"))
                scan_pump(8)

        em.wait("gpsimd", "dG")
        em.op("gpsimd", lambda e: e.collective_compute(
            "AllGather", OP.bypass, ins=[kb_i[:, :]], outs=[kb_o[:, :, :]],
            replica_groups=PAIR_RG), inc="sC")
        em.op("gpsimd", lambda e: e.collective_compute(
            "AllGather", OP.bypass, ins=[vb_i[:, :]], outs=[vb_o[:, :, :]],
            replica_groups=PAIR_RG), inc="sC")
        AG_B = em.v("sC")

        WB_K, WB_V, WB_QB, WB_ES, WB_RCP, WB_CTH = 0, 4096, 8224, 9248, 11296, 11392
        em.wait("sync", "sC", AG_B)
        qbt_dma, qbt_used = [], []
        avb_evac = {}
        ctxb_dmas = []
        for h in range(8):
            em.wait("sync", "sT")
            for r in range(2):
                em.op("sync", lambda e, r=r, h=h: e.dma_start(
                    out=work[:, WB_K + r * 2048:WB_K + r * 2048 + 2048],
                    in_=kb_o[r, h * 128:(h + 1) * 128, :]), inc="dW", n=16)
                em.op("sync", lambda e, r=r, h=h: e.dma_start(
                    out=work[:, WB_V + r * 2064:WB_V + r * 2064 + 2064]
                    .rearrange("p (g c) -> p g c", c=129),
                    in_=vb_o[r, :, h * 129:(h + 1) * 129].rearrange(
                        "(g p) c -> p g c", p=128)), inc="dW", n=16)
            kv_ready = em.v("dW")
            em.wait("tensor", "dW", kv_ready)
            for qt in range(NTT):
                qo = WB_QB + (qt % 2) * TT
                if len(qbt_dma) >= 2:
                    em.wait("sync", "sT", qbt_used[-2])
                em.op("sync", lambda e, qo=qo, h=h, qt=qt: e.dma_start(
                    out=work[:, qo:qo + TT],
                    in_=qb_scr[h * 128:(h + 1) * 128, qt * TT:(qt + 1) * TT]),
                    inc="dW", n=16)
                qbt_dma.append(em.v("dW"))
                em.wait("tensor", "dW", qbt_dma[-1])
                em.wait("tensor", "sS")
                if avb_evac:
                    em.wait("tensor", "sV", max(avb_evac.values()))
                av0, av1 = pb[4], pb[6]
                exp_done = []
                for g in range(32):
                    par = g % 2
                    sb = pb[par]
                    if g >= 2:
                        em.wait("tensor", "sS", exp_done[g - 2])
                    em.op("tensor", lambda e, g=g, sb=sb, qo=qo: e.matmul(
                        sb[:, 0:TT],
                        work[:, WB_K + g * 128:WB_K + (g + 1) * 128],
                        work[:, qo:qo + TT], start=True, stop=True), inc="sT")
                    smm = em.v("sT")
                    em.wait("scalar", "sT", smm)
                    eo = WB_ES + par * TT
                    em.op("scalar", lambda e, sb=sb, eo=eo: e.activation(
                        out=work[:, eo:eo + TT], in_=sb[:, 0:TT], func=AF.Exp,
                        bias=-SHIFT_B), inc="sS")
                    exp_done.append(em.v("sS"))
                    em.wait("tensor", "sS", exp_done[-1])
                    for sq in range(4):
                        avb = av0 if sq < 2 else av1
                        em.op("tensor", lambda e, eo=eo, g=g, sq=sq, avb=avb:
                              e.matmul(
                            avb[:, (sq % 2) * 129:(sq % 2) * 129 + 129],
                            work[:, eo + sq * 128:eo + (sq + 1) * 128],
                            work[:, WB_V + g * 129:WB_V + (g + 1) * 129],
                            start=(g == 0 and sq % 2 == 0),
                            stop=(g == 31 and sq % 2 == 1)),
                            inc=("sT" if (g == 31 and sq == 3) else None))
                    scan_pump(1)
                av_mm = em.v("sT")
                em.wait("vector", "sT", av_mm)
                if qt == 0 and ctxb_dmas:
                    em.wait("vector", "dG", ctxb_dmas[-1])
                rb = WB_RCP + (qt % 2) * 16
                em.wait("scalar", "sT", av_mm)
                if (qt % 2) in avb_evac:
                    em.wait("scalar", "sV", avb_evac[qt % 2])
                for sq in range(4):
                    avb = av0 if sq < 2 else av1
                    em.op("scalar", lambda e, sq=sq, avb=avb, rb=rb:
                          e.activation(
                        out=work[:, rb + 8 + sq:rb + 9 + sq],
                        in_=avb[:, (sq % 2) * 129 + 128:(sq % 2) * 129 + 129],
                        func=AF.Ln), inc="sS")
                    em.op("scalar", lambda e, sq=sq, rb=rb: e.activation(
                        out=work[:, rb + sq:rb + sq + 1],
                        in_=work[:, rb + 8 + sq:rb + 9 + sq],
                        func=AF.Exp, scale=-1.0), inc="sS")
                em.wait("vector", "sS")
                for sq in range(4):
                    avb = av0 if sq < 2 else av1
                    em.op("vector", lambda e, sq=sq, avb=avb, qt=qt, rb=rb:
                          e.tensor_scalar_mul(
                        work[:, WB_CTH + (qt * 4 + sq) * 128:
                              WB_CTH + (qt * 4 + sq) * 128 + 128],
                        avb[:, (sq % 2) * 129:(sq % 2) * 129 + 128],
                        work[:, rb + sq:rb + sq + 1]), inc="sV")
                avb_evac[qt % 2] = em.v("sV")
                qbt_used.append(em.v("sT"))
                bar()
            em.wait("gpsimd", "sV")
            em.op("gpsimd", lambda e, h=h: e.dma_start(
                out=ctxb_scr[:, h * 128:(h + 1) * 128].rearrange(
                    "(i p) c -> p i c", p=128),
                in_=work[:, WB_CTH:WB_CTH + 2048].rearrange(
                    "p (i c) -> p i c", c=128)), inc="dG", n=16)
            ctxb_dmas.append(em.v("dG"))
        CTXB_SAVED = em.v("dG")

        WB_WO, WB_CTI, WB_FM, WB_X2P, WB_SQ2 = 0, 8192, 12288, 16384, 20480
        em.wait("sync", "sT")
        for oc in range(NCH):
            em.op("sync", lambda e, oc=oc: e.dma_start(
                out=work[:, WB_WO + oc * 1024:WB_WO + (oc + 1) * 1024]
                .rearrange("p (k c) -> p k c", c=128),
                in_=Wo[:, oc * 128:(oc + 1) * 128].rearrange(
                    "(k p) c -> p k c", p=128)), inc="dW", n=16)
        WOB_READY = em.v("dW")
        em.wait("sync", "dG", CTXB_SAVED)
        em.wait("tensor", "dW", WOB_READY)
        x2p_dmas = []
        for qt in range(NTT):
            em.wait("sync", "sT")
            em.op("sync", lambda e, qt=qt: e.dma_start(
                out=work[:, WB_CTI:WB_CTI + 4096].rearrange(
                    "p (s c) -> p s c", c=1024),
                in_=ctxb_scr[qt * 512:(qt + 1) * 512, :].rearrange(
                    "(s p) c -> p s c", p=128)), inc="dW", n=16)
            cti_ld = em.v("dW")
            em.wait("tensor", "dW", cti_ld)
            em.wait("vector", "sS")
            tr_done = []
            for blk in range(4 * NCH):
                sq, ec = blk // NCH, blk % NCH
                bank = pb[blk % 2]
                if blk >= 2:
                    em.wait("tensor", "sV", tr_done[blk - 2])
                em.op("tensor", lambda e, sq=sq, ec=ec, bank=bank:
                      e.transpose(
                    bank[:, 0:128],
                    work[:, WB_CTI + sq * 1024 + ec * 128:
                          WB_CTI + sq * 1024 + (ec + 1) * 128],
                    ident[:, :]), inc="sT")
                em.wait("vector", "sT")
                em.op("vector", lambda e, sq=sq, ec=ec, bank=bank:
                      e.tensor_copy(
                    work[:, WB_FM + ec * TT + sq * 128:
                          WB_FM + ec * TT + (sq + 1) * 128], bank[:, 0:128]),
                    inc="sV")
                tr_done.append(em.v("sV"))
            em.wait("tensor", "sV", tr_done[-1])
            x2p_off = WB_X2P
            opv = []
            for oc in range(NCH):
                bank = pb[2 + oc % 2]
                if oc >= 2:
                    em.wait("tensor", "sV", opv[oc - 2])
                for k in range(NCH):
                    em.op("tensor", lambda e, oc=oc, k=k, bank=bank: e.matmul(
                        bank[:, 0:TT],
                        work[:, WB_WO + oc * 1024 + k * 128:
                              WB_WO + oc * 1024 + (k + 1) * 128],
                        work[:, WB_FM + k * TT:WB_FM + (k + 1) * TT],
                        start=(k == 0), stop=(k == NCH - 1)),
                        inc=("sT" if k == NCH - 1 else None))
                mmv = em.v("sT")
                em.wait("vector", "sT", mmv)
                em.op("vector", lambda e, oc=oc, qt=qt, bank=bank,
                      x2p_off=x2p_off: e.scalar_tensor_tensor(
                    work[:, x2p_off + oc * TT:x2p_off + (oc + 1) * TT],
                    bank[:, 0:TT], bpc("bo", oc),
                    strm[:, oc * T + qt * TT:oc * T + (qt + 1) * TT],
                    op0=OP.add, op1=OP.add), inc="sV")
                opv.append(em.v("sV"))
            scan_pump(4)
            ln_stats(work, x2p_off, TT, TT, sl(ST_MU2X, qt * TT),
                     sl(ST_R2X, qt * TT), WB_SQ2)
            em.wait("gpsimd", "sV")
            em.wait("gpsimd", "sS")
            em.op("gpsimd", lambda e, x2p_off=x2p_off, qt=qt: e.dma_start(
                out=x2p_scr[:, qt * TT:(qt + 1) * TT].rearrange(
                    "(ch p) t -> p ch t", p=128),
                in_=work[:, x2p_off:x2p_off + 4096].rearrange(
                    "p (ch t) -> p ch t", t=TT)), inc="dG", n=16)
            x2p_dmas.append(em.v("dG"))
            bar()
        X2P_SAVED = em.v("dG")


        H_X2P, H_XN2, H_HB, H_W1S = 0, 2048, 4096, 12288
        H_W2S, H_TOUT = 14336, 18432
        H_MB, H_GT, H_WFS, H_FP, H_SQF = 4096, 6144, 8192, 12288, 14336
        TB = 256
        for t2 in range(8):
            tw2 = t2 * TB
            qt_of = t2 // 2
            em.wait("sync", "dG", X2P_SAVED)
            em.wait("sync", "sV")
            em.op("sync", lambda e, t2=t2: e.dma_start(
                out=work[:, H_X2P:H_X2P + 2048].rearrange(
                    "p (ch t) -> p ch t", t=TB),
                in_=x2p_scr[:, t2 * TB:(t2 + 1) * TB].rearrange(
                    "(ch p) t -> p ch t", p=128)), inc="dW", n=16)
            x2p_ld = em.v("dW")
            em.wait("vector", "dW", x2p_ld)
            ln_apply(work, H_X2P, TB, TB, sl(ST_MU2X, tw2), sl(ST_R2X, tw2),
                     work, H_XN2, TB, gam="ln2g", bet="ln2b")
            em.wait("tensor", "sV")

            def evw1(hc, bank, mmv):
                em.wait("scalar", "sT", mmv)
                em.op("scalar", lambda e, hc=hc, bank=bank: e.activation(
                    out=work[:, H_HB + hc * TB:H_HB + (hc + 1) * TB],
                    in_=bank[:, 0:TB], func=AF.Relu, bias=bpc("b1", hc)),
                    inc="sS")
                return ("sS", em.v("sS"))
            proj(W1, 32, H_XN2, TB, H_W1S, NCH, evw1, scan_steps=6)
            em.wait("tensor", "sS")

            w2_ready, w2mm, toutv = [], [], []

            def dma_w2(idx):
                so = H_W2S + (idx % 2) * 2048
                oc, hh = idx // 2, idx % 2
                if idx >= 2:
                    em.wait("sync", "sT", w2mm[idx - 2])
                em.op("sync", lambda e, so=so, oc=oc, hh=hh: e.dma_start(
                    out=work[:, so:so + 2048].rearrange(
                        "p (k c) -> p k c", c=128),
                    in_=W2[hh * 2048:(hh + 1) * 2048,
                           oc * 128:(oc + 1) * 128].rearrange(
                        "(k p) c -> p k c", p=128)), inc="dW", n=16)
                w2_ready.append(em.v("dW"))

            em.wait("sync", "sT")
            dma_w2(0); dma_w2(1)
            for oc in range(NCH):
                bank = pb[4 if oc % 2 == 0 else 6]
                if oc >= 2:
                    em.wait("tensor", "sV", toutv[oc - 2])
                for hh in range(2):
                    idx = oc * 2 + hh
                    so = H_W2S + (idx % 2) * 2048
                    em.wait("tensor", "dW", w2_ready[idx])
                    for k in range(16):
                        kk = hh * 16 + k
                        em.op("tensor", lambda e, so=so, k=k, kk=kk, bank=bank:
                              e.matmul(
                            bank[:, 0:TB],
                            work[:, so + k * 128:so + (k + 1) * 128],
                            work[:, H_HB + kk * TB:H_HB + kk * TB + TB],
                            start=(kk == 0), stop=(kk == 31)),
                            inc=("sT" if k == 15 else None))
                    w2mm.append(em.v("sT"))
                    if idx + 2 < 16:
                        dma_w2(idx + 2)
                mmv = em.cnt["sT"]
                em.wait("vector", "sT", mmv)
                em.op("vector", lambda e, oc=oc, bank=bank:
                      e.scalar_tensor_tensor(
                    work[:, H_TOUT + oc * TB:H_TOUT + (oc + 1) * TB],
                    bank[:, 0:TB], bpc("b2", oc),
                    work[:, H_X2P + oc * TB:H_X2P + (oc + 1) * TB],
                    op0=OP.add, op1=OP.add), inc="sV")
                toutv.append(em.v("sV"))
                scan_pump(6)
            em.wait("sync", "sT")
            em.op("sync", lambda e, t2=t2: e.dma_start(
                out=work[:, H_GT:H_GT + 2048].rearrange(
                    "p (ch t) -> p ch t", t=TB),
                in_=gate_scr[:, t2 * TB:(t2 + 1) * TB].rearrange(
                    "(ch p) t -> p ch t", p=128)), inc="dW", n=16)
            em.op("sync", lambda e: e.dma_start(
                out=work[0:16, H_SQF + 512:H_SQF + 1536], in_=Cmat[:, :]),
                inc="dW", n=16)
            gt_ld = em.v("dW")
            need = (4 * t2 + 4) * 128
            if scan_state["next"] < need:
                _scan_emit(need - scan_state["next"])
            em.wait("tensor", "zS", 2 * need)
            em.wait("tensor", "dW", gt_ld)
            em.wait("tensor", "sV")
            mbv = []
            for oc in range(NCH):
                bE, bO = pb[7], pb[4]
                if oc >= 1:
                    em.wait("tensor", "sV", mbv[-1])
                for par, bank in ((0, bE), (1, bO)):
                    for blk in range(2):
                        g = 2 * (2 * t2 + blk) + par
                        em.op("tensor", lambda e, oc=oc, g=g, blk=blk,
                              bank=bank: e.matmul(
                            bank[:, blk * 128:(blk + 1) * 128],
                            work[0:16, H_SQF + 512 + oc * 128:
                                 H_SQF + 512 + (oc + 1) * 128],
                            hall[:, 1 + g * 128:1 + (g + 1) * 128],
                            start=(blk == 0), stop=(blk == 1)),
                            inc=("sT" if (par == 1 and blk == 1) else None))
                mmv = em.v("sT")
                em.wait("vector", "sT", mmv)
                em.op("vector", lambda e, oc=oc: e.tensor_scalar_mul(
                    work[:, H_MB + oc * TB:H_MB + (oc + 1) * TB],
                    pb[7][:, 0:TB], bpc("selE")), inc="sV")
                em.op("vector", lambda e, oc=oc: e.scalar_tensor_tensor(
                    work[:, H_MB + oc * TB:H_MB + (oc + 1) * TB],
                    pb[4][:, 0:TB], bpc("selO"),
                    work[:, H_MB + oc * TB:H_MB + (oc + 1) * TB],
                    op0=OP.mult, op1=OP.add), inc="sV")
                em.op("vector", lambda e, oc=oc: e.tensor_mul(
                    work[:, H_MB + oc * TB:H_MB + (oc + 1) * TB],
                    work[:, H_MB + oc * TB:H_MB + (oc + 1) * TB],
                    work[:, H_GT + oc * TB:H_GT + (oc + 1) * TB]), inc="sV")
                em.op("vector", lambda e, oc=oc, t2=t2: e.tensor_add(
                    work[:, H_MB + oc * TB:H_MB + (oc + 1) * TB],
                    work[:, H_MB + oc * TB:H_MB + (oc + 1) * TB],
                    strm[:, oc * T + t2 * TB:oc * T + (t2 + 1) * TB]),
                    inc="sV")
                mbv.append(em.v("sV"))
            MB_DONE = em.v("sV")

            wf_ready, wfmm, fpv = [], [], []

            def dma_wf(oc):
                so = H_WFS + (oc % 2) * 2048
                if oc >= 2:
                    em.wait("sync", "sT", wfmm[oc - 2])
                em.op("sync", lambda e, so=so, oc=oc: e.dma_start(
                    out=work[:, so:so + 2048].rearrange(
                        "p (k c) -> p k c", c=128),
                    in_=Wf[:, oc * 128:(oc + 1) * 128].rearrange(
                        "(k p) c -> p k c", p=128)), inc="dW", n=16)
                wf_ready.append(em.v("dW"))

            em.wait("sync", "sT")
            dma_wf(0); dma_wf(1)
            em.wait("tensor", "sV", MB_DONE)
            for oc in range(NCH):
                so = H_WFS + (oc % 2) * 2048
                bank = pb[oc % 2]
                em.wait("tensor", "dW", wf_ready[oc])
                if oc >= 2:
                    em.wait("tensor", "sV", fpv[oc - 2])
                for k in range(16):
                    src = H_MB if k < 8 else H_TOUT
                    kk = k if k < 8 else k - 8
                    em.op("tensor", lambda e, so=so, k=k, kk=kk, src=src,
                          bank=bank: e.matmul(
                        bank[:, 0:TB],
                        work[:, so + k * 128:so + (k + 1) * 128],
                        work[:, src + kk * TB:src + kk * TB + TB],
                        start=(k == 0), stop=(k == 15)),
                        inc=("sT" if k == 15 else None))
                mmv = em.v("sT")
                wfmm.append(mmv)
                if oc + 2 < NCH:
                    dma_wf(oc + 2)
                em.wait("vector", "sT", mmv)
                em.op("vector", lambda e, oc=oc, bank=bank: e.tensor_scalar_add(
                    work[:, H_FP + oc * TB:H_FP + (oc + 1) * TB],
                    bank[:, 0:TB], bpc("bf", oc)), inc="sV")
                fpv.append(em.v("sV"))
            ln_stats(work, H_FP, TB, TB, ST_MUT, ST_RT, H_SQF)
            ln_apply(work, H_FP, TB, TB, ST_MUT, ST_RT, work, H_FP, TB,
                     gam="lnfg", bet="lnfb")
            FUSED_DONE = em.v("sV")
            em.wait("sync", "sV", FUSED_DONE)
            em.op("sync", lambda e, t2=t2: e.dma_start(
                out=work[:, H_XN2:H_XN2 + 2048].rearrange(
                    "p (ch t) -> p ch t", t=TB),
                in_=x2_scr[:, t2 * TB:(t2 + 1) * TB].rearrange(
                    "(ch p) t -> p ch t", p=128)), inc="dW", n=16)
            x2_ld = em.v("dW")
            em.wait("vector", "dW", x2_ld)
            for oc in range(NCH):
                em.op("vector", lambda e, oc=oc, t2=t2: e.tensor_add(
                    strm[:, oc * T + t2 * TB:oc * T + (t2 + 1) * TB],
                    work[:, H_FP + oc * TB:H_FP + (oc + 1) * TB],
                    work[:, H_XN2 + oc * TB:H_XN2 + (oc + 1) * TB]), inc="sV")
            bar()

        tap_stream("x3")
        if STOP_AFTER == "s3":
            finish()
            return nc

        ffn_phase()
        finish()
    return nc


def _pack_col(vec):
    return np.ascontiguousarray(vec.reshape(-1, 128).T.astype(np.float32))


def kernel(L0_attn_a_x, L0_attn_a_ln_gamma, L0_attn_a_ln_beta, L0_attn_a_w_qkv,
           L0_attn_a_w_out, L0_attn_a_b_out, L0_attn_b_ln_g, L0_attn_b_ln_b,
           L0_attn_b_Wdelta, L0_attn_b_bdelta, L0_attn_b_Win, L0_attn_b_b_in,
           L0_attn_b_Wgate, L0_attn_b_bgate, L0_attn_b_A, L0_attn_b_C,
           L0_attn_b_ln1_g, L0_attn_b_ln1_b, L0_attn_b_Wq, L0_attn_b_bq,
           L0_attn_b_Wk, L0_attn_b_bk, L0_attn_b_Wv, L0_attn_b_bv,
           L0_attn_b_Wo, L0_attn_b_bo, L0_attn_b_ln2_g, L0_attn_b_ln2_b,
           L0_attn_b_W1, L0_attn_b_b1, L0_attn_b_W2, L0_attn_b_b2,
           L0_attn_b_Wf, L0_attn_b_bf, L0_attn_b_lnf_g, L0_attn_b_lnf_b,
           L0_ffn_w_gate, L0_ffn_w_up, L0_ffn_w_down):
    x = np.asarray(L0_attn_a_x)
    w_qkv = np.asarray(L0_attn_a_w_qkv)
    wqT = np.ascontiguousarray(w_qkv[0:1024].T)
    wkT = np.ascontiguousarray(w_qkv[1024:2048].T)
    wvT = np.ascontiguousarray(w_qkv[2048:3072].T)
    woT = np.ascontiguousarray(np.asarray(L0_attn_a_w_out).T)
    wgT = np.ascontiguousarray(np.asarray(L0_ffn_w_gate).T)
    wuT = np.ascontiguousarray(np.asarray(L0_ffn_w_up).T)
    wdT = np.ascontiguousarray(np.asarray(L0_ffn_w_down).T)

    shared = {
        "wqT": wqT, "wkT": wkT, "wvT": wvT, "woT": woT,
        "wgT": wgT, "wuT": wuT, "wdT": wdT,
        "Wdl": np.ascontiguousarray(L0_attn_b_Wdelta),
        "Win": np.ascontiguousarray(L0_attn_b_Win),
        "Wgt": np.ascontiguousarray(L0_attn_b_Wgate),
        "Amat": np.ascontiguousarray(L0_attn_b_A),
        "Cmat": np.ascontiguousarray(L0_attn_b_C),
        "Wq": np.ascontiguousarray(L0_attn_b_Wq),
        "Wk": np.ascontiguousarray(L0_attn_b_Wk),
        "Wv": np.ascontiguousarray(L0_attn_b_Wv),
        "Wo": np.ascontiguousarray(L0_attn_b_Wo),
        "W1": np.ascontiguousarray(L0_attn_b_W1),
        "W2": np.ascontiguousarray(L0_attn_b_W2),
        "Wf": np.ascontiguousarray(L0_attn_b_Wf),
        "rpv": np.ascontiguousarray(np.asarray(L0_attn_b_bv)[None, :]),
        "identv": np.eye(128, dtype=np.float32),
        "onesv": np.ones((128, 128), np.float32),
    }

    diag = np.where(np.arange(128)[:, None] > np.arange(128)[None, :],
                    np.float32(NEG), np.float32(0.0)).astype(np.float32)
    full = np.full((128, 128), np.float32(NEG), np.float32)
    zero = np.zeros((128, 128), np.float32)

    in_maps = []
    for c in range(8):
        b, p = c // 2, c % 2
        bp_arr = np.zeros((128, NBP), np.float32)

        def put(name, vec):
            v = np.asarray(vec).astype(np.float32)
            n = v.size // 128
            bp_arr[:, BP[name]:BP[name] + n] = v.reshape(n, 128).T

        put("bout_a", L0_attn_a_b_out)
        put("gam_a", L0_attn_a_ln_gamma)
        put("bet_a", L0_attn_a_ln_beta)
        put("lng", L0_attn_b_ln_g); put("lnb", L0_attn_b_ln_b)
        put("ln1g", L0_attn_b_ln1_g); put("ln1b", L0_attn_b_ln1_b)
        put("ln2g", L0_attn_b_ln2_g); put("ln2b", L0_attn_b_ln2_b)
        put("lnfg", L0_attn_b_lnf_g); put("lnfb", L0_attn_b_lnf_b)
        put("bgate", L0_attn_b_bgate)
        put("bq", L0_attn_b_bq); put("bk", L0_attn_b_bk)
        put("bo", L0_attn_b_bo)
        put("b1", L0_attn_b_b1); put("b2", L0_attn_b_b2)
        put("bf", L0_attn_b_bf)
        bp_arr[0:16, BP["bdelta"]] = np.asarray(L0_attn_b_bdelta)
        bp_arr[0:16, BP["b_in"]] = np.asarray(L0_attn_b_b_in)
        bp_arr[:, BP["selE"]] = 1.0 if p == 0 else 0.0
        bp_arr[:, BP["selO"]] = 1.0 if p == 1 else 0.0
        bp_arr[:, BP["c_eps"]] = EPS
        bp_arr[:, BP["c_nsha"]] = -SHIFT_A
        bp_arr[:, BP["c_nshb"]] = -SHIFT_B
        bp_arr[:, BP["c_inv127"]] = INV127
        bp_arr[:, BP["c_i1024"]] = 1.0 / 1024.0
        bp_arr[:, BP["c_neg1"]] = -1.0

        xb = x[b].reshape(32, 128, D)[p::2].reshape(T, D)
        m = dict(shared)
        m["x0"] = np.ascontiguousarray(xb.T)
        m["bpv"] = bp_arr
        m["m0v"] = diag if p == 0 else zero
        m["m1v"] = full if p == 0 else diag
        in_maps.append(m)

    nc = build_nc()
    res = run_bass_kernel_spmd(nc, in_maps, core_ids=list(range(8)))

    out_full = np.empty((B, S, D), np.float32)
    for c in range(8):
        b, p = c // 2, c % 2
        sh = res.results[c]["out"]
        out_full[b].reshape(32, 128, D)[p::2] = sh.T.reshape(16, 128, D)
    if TAPS:
        kernel.tap_results = []
        for c in range(8):
            kernel.tap_results.append(
                {t: res.results[c]["tap_" + t] for t in TAPS})
    return out_full


# version 08
# speedup vs baseline: 1.1226x; 1.0948x over previous
import contextlib
import numpy as np
import concourse.bass as bass
import concourse.mybir as mybir
from concourse.bass_utils import run_bass_kernel_spmd

F32 = mybir.dt.float32
I32 = mybir.dt.int32
AF = mybir.ActivationFunctionType
OP = mybir.AluOpType

B, S, D = 4, 4096, 1024
T = 2048
NCH = 8
TT = 512
NTT = T // TT
EPS = 1e-5
SHIFT_A = 24.0
SHIFT_B = 10.0
QSC_A = 0.125
QSC_B = float(1.0 / np.sqrt(np.float32(128.0)))
INV127 = float(np.float32(1.0) / np.float32(127.0))
PAIR_RG = [[0, 1], [2, 3], [4, 5], [6, 7]]
NEG = -1.0e30
NSCAN = S

SCAN_INTERLEAVE = True
DBG_BARRIER = False
DBG_ABAR = False
DBG_NONORM = False
TAPS = []
STOP_AFTER = None

BP = {}
_c = 0
for _name, _n in [("bout_a", 8), ("gam_a", 8), ("bet_a", 8), ("lng", 8),
                  ("lnb", 8), ("ln1g", 8), ("ln1b", 8), ("ln2g", 8),
                  ("ln2b", 8), ("lnfg", 8), ("lnfb", 8), ("bgate", 8),
                  ("bq", 8), ("bk", 8), ("bo", 8), ("b1", 32), ("b2", 8),
                  ("bf", 8), ("bdelta", 1), ("b_in", 1), ("selE", 1),
                  ("selO", 1), ("c_eps", 1), ("c_nsha", 1), ("c_nshb", 1),
                  ("c_inv127", 1), ("c_i1024", 1), ("c_neg1", 1)]:
    BP[_name] = _c
    _c += _n
NBP = _c

ENGS = ["sync", "tensor", "vector", "scalar", "gpsimd"]


class Em:
    def __init__(self, nc, stack):
        self.nc = nc
        self.q = {e: [] for e in ENGS}
        self.sem = {}
        self.cnt = {}
        for s in ["dW", "dG", "sT", "sV", "sS", "sC", "zS", "zT", "st"]:
            self.sem[s] = stack.enter_context(nc.semaphore(s))
            self.cnt[s] = 0

    def v(self, s):
        return self.cnt[s]

    def wait(self, eng, s, val=None):
        v = self.cnt[s] if val is None else val
        if v <= 0:
            return
        h = self.sem[s]
        self.q[eng].append(lambda e: e.wait_ge(h, v))

    def op(self, eng, builder, inc=None, n=1):
        if inc is None:
            self.q[eng].append(lambda e: builder(e))
            return None
        h = self.sem[inc]
        self.q[eng].append(lambda e: builder(e).then_inc(h, n))
        self.cnt[inc] += n
        return self.cnt[inc]

    def flush(self, block):
        for name in ENGS:
            ops = self.q[name]
            if not ops:
                continue

            def runner(e, ops=ops):
                for f in ops:
                    f(e)
            getattr(block, name)(runner)


def build_nc():
    nc = bass.Bass("TRN2", target_bir_lowering=False)
    di = lambda n, s, dt=F32: nc.dram_tensor(n, s, dt, kind="ExternalInput").ap()
    do = lambda n, s, dt=F32: nc.dram_tensor(n, s, dt, kind="ExternalOutput").ap()
    dd = lambda n, s, dt=F32: nc.dram_tensor(n, s, dt).ap()

    x0 = di("x0", [D, T])
    wqT = di("wqT", [D, D]); wkT = di("wkT", [D, D]); wvT = di("wvT", [D, D])
    woT = di("woT", [D, D])
    wgT = di("wgT", [D, 3072], I32); wuT = di("wuT", [D, 3072], I32)
    wdT = di("wdT", [3072, D], I32)
    Wdl = di("Wdl", [D, 16]); Win = di("Win", [D, 16])
    Wgt = di("Wgt", [D, D])
    Amat = di("Amat", [16, 16]); Cmat = di("Cmat", [16, D])
    Wq = di("Wq", [D, D]); Wk = di("Wk", [D, D]); Wv = di("Wv", [D, D])
    Wo = di("Wo", [D, D])
    W1 = di("W1", [D, 4096]); W2 = di("W2", [4096, D]); Wf = di("Wf", [2 * D, D])
    bp_in = di("bpv", [128, NBP])
    rp_in = di("rpv", [1, D])
    ident_in = di("identv", [128, 128])
    m0_in = di("m0v", [128, 128])
    m1_in = di("m1v", [128, 128])
    ones_in = di("onesv", [128, 128])

    out = do("out", [D, T])
    _tap_shapes = {"q_scr": [D, T], "ka_i": [D, T], "va_i": [T, 1040],
                   "ctxa": [T, D], "qb_scr": [D, T], "kb_i": [D, T],
                   "vb_i": [T, 1032], "ctxb": [T, D], "gate": [D, T],
                   "m_i": [48, T], "x2p": [D, T], "hall": [16, S + 1],
                   "kdbg": [128, 4096], "vdbg": [128, 4160]}
    taps = {t: do("tap_" + t, _tap_shapes.get(t, [D, T])) for t in TAPS}

    q_scr = dd("q_scr", [D, T])
    qb_scr = dd("qb_scr", [D, T])
    gate_scr = dd("gate_scr", [D, T])
    x2_scr = dd("x2_scr", [D, T])
    x2p_scr = dd("x2p_scr", [D, T])
    ctxa_scr = dd("ctxa_scr", [T, D])
    ctxb_scr = dd("ctxb_scr", [T, D])
    ka_i = dd("ka_i", [D, T]); ka_o = dd("ka_o", [2, D, T])
    va_i = dd("va_i", [T, 1040]); va_o = dd("va_o", [2, T, 1040])
    m_i = dd("m_i", [48, T]); m_o = dd("m_o", [2, 48, T])
    kb_i = dd("kb_i", [D, T]); kb_o = dd("kb_o", [2, D, T])
    vb_i = dd("vb_i", [T, 1032]); vb_o = dd("vb_o", [2, T, 1032])

    stack = contextlib.ExitStack()
    with stack:
        work = stack.enter_context(nc.sbuf_tensor("work", [128, 21536], F32))
        strm = stack.enter_context(nc.sbuf_tensor("strm", [128, NCH * T], F32))
        bp = stack.enter_context(nc.sbuf_tensor("bps", [128, NBP], F32))
        ident = stack.enter_context(nc.sbuf_tensor("identt", [128, 128], F32))
        m0 = stack.enter_context(nc.sbuf_tensor("m0t", [128, 128], F32))
        m1 = stack.enter_context(nc.sbuf_tensor("m1t", [128, 128], F32))
        ones = stack.enter_context(nc.sbuf_tensor("ones_t", [128, 128], F32))
        strow = stack.enter_context(nc.sbuf_tensor("strow", [128, 2048], F32))
        A_sb = stack.enter_context(nc.sbuf_tensor("A_sb", [16, 16], F32))
        dfull = stack.enter_context(nc.sbuf_tensor("dfull", [16, S], F32))
        btfull = stack.enter_context(nc.sbuf_tensor("btfull", [16, S], F32))
        hall = stack.enter_context(nc.sbuf_tensor("hallt", [16, S + 1], F32))
        hd = stack.enter_context(nc.sbuf_tensor("hdt", [16, 1], F32))

        pb = [stack.enter_context(nc.psum_tensor(f"pb{k}", [128, 512], F32))
              for k in range(8)]

        for cname, cval in [("c_eps", EPS), ("c_nsha", -SHIFT_A),
                            ("c_nshb", -SHIFT_B), ("c_inv127", INV127),
                            ("c_i1024", 1.0 / 1024.0), ("c_neg1", -1.0)]:
            nc.const_aps.register(F32, cval, bp[:, BP[cname]:BP[cname] + 1])

        em = Em(nc, stack)
        block = stack.enter_context(nc.Block())

        ST_MU2X, ST_R2X = (0, 0), (32, 0)
        ST_MUT, ST_RT = (64, 0), (64, 512)
        ST_BV = (64, 1024)
        ST_A, ST_B = (64, 1024), (64, 1536)

        def srow(slot, off, n):
            r, c = slot
            return strow[r:r + 1, c + off:c + off + n]

        def sl(slot, off):
            return (slot[0], slot[1] + off)

        def bpc(name, k=0):
            return bp[:, BP[name] + k:BP[name] + k + 1]

        def bar():
            for _e in ENGS:
                for _s in ["sT", "sV", "sS", "dW", "dG"]:
                    em.wait(_e, _s)

        scan_state = {"next": 0, "ready": False}

        def scan_pump(nsteps):
            if not scan_state["ready"] or not SCAN_INTERLEAVE:
                return
            _scan_emit(nsteps)

        def _scan_emit(nsteps):
            t0 = scan_state["next"]
            t1 = min(t0 + nsteps, NSCAN)
            for t in range(t0, t1):
                em.op("scalar", lambda e, t=t: e.activation(
                    out=hd[:, :], in_=hall[:, t:t + 1], func=AF.Identity,
                    scale=dfull[:, t:t + 1]), inc="zS")
                em.wait("tensor", "zS", 2 * t + 1)
                em.op("tensor", lambda e: e.matmul(
                    pb[5][0:16, 0:1], A_sb[:, :], hd[:, :], start=True,
                    stop=True), inc="zT")
                em.wait("scalar", "zT", t + 1)
                em.op("scalar", lambda e, t=t: e.activation(
                    out=hall[:, t + 1:t + 2], in_=pb[5][0:16, 0:1],
                    func=AF.Tanh, bias=btfull[:, t:t + 1]), inc="zS")
            scan_state["next"] = t1

        def ln_stats(sbuf, soff, sstride, n, mu_slot, r_slot, sq_off,
                     bank_s=6, bank_q=7):
            em.wait("tensor", "sS")
            em.wait("tensor", "sV")
            em.wait("scalar", "sT")
            em.wait("scalar", "sV")
            em.wait("scalar", "dG")
            for k in range(NCH):
                cw = soff + k * sstride
                last = (k == NCH - 1)
                em.op("tensor", lambda e, cw=cw, k=k, last=last: e.matmul(
                    pb[bank_s][0:1, 0:n], ones[:, 0:1], sbuf[:, cw:cw + n],
                    start=(k == 0), stop=last), inc=("sT" if last else None))
            mm_sum = em.v("sT")
            sq_mm = []
            for k in range(NCH):
                cw = soff + k * sstride
                po = sq_off + (k % 2) * n
                if k >= 2:
                    em.wait("scalar", "sT", sq_mm[k - 2])
                em.op("scalar", lambda e, cw=cw, po=po: e.activation(
                    out=work[:, po:po + n], in_=sbuf[:, cw:cw + n],
                    func=AF.Square), inc="sS")
                em.wait("tensor", "sS")
                last = (k == NCH - 1)
                em.op("tensor", lambda e, po=po, k=k, last=last: e.matmul(
                    pb[bank_q][0:1, 0:n], ones[:, 0:1], work[:, po:po + n],
                    start=(k == 0), stop=last), inc="sT")
                sq_mm.append(em.v("sT"))
            mm_sq = em.v("sT")
            em.wait("scalar", "sT", mm_sum)
            em.op("scalar", lambda e: e.mul(
                out=srow(mu_slot, 0, n), in_=pb[bank_s][0:1, 0:n],
                mul=1.0 / 1024.0), inc="sS")
            em.wait("scalar", "sT", mm_sq)
            em.op("scalar", lambda e: e.mul(
                out=srow(ST_A, 0, n), in_=pb[bank_q][0:1, 0:n],
                mul=1.0 / 1024.0), inc="sS")
            em.op("scalar", lambda e: e.activation(
                out=srow(ST_B, 0, n),
                in_=srow(mu_slot, 0, n), func=AF.Square), inc="sS")
            em.wait("vector", "sS")
            em.op("vector", lambda e: e.tensor_sub(
                srow(ST_A, 0, n), srow(ST_A, 0, n), srow(ST_B, 0, n)),
                inc="sV")
            em.wait("scalar", "sV")
            em.op("scalar", lambda e: e.activation(
                out=srow(ST_B, 0, n), in_=srow(ST_A, 0, n),
                func=AF.Sqrt, bias=EPS), inc="sS")
            em.wait("vector", "sS")
            em.op("vector", lambda e: e.reciprocal(
                srow(r_slot, 0, n), srow(ST_B, 0, n)), inc="sV")

        def ln_apply(sbuf, soff, sstride, n, mu_slot, r_slot,
                     dbuf, doff, dstride, gam=None, bet=None,
                     bank_mu=4, bank_r=6):
            em.wait("tensor", "sS")
            em.wait("tensor", "sV")
            rmu, rr = mu_slot[0], r_slot[0]
            em.op("tensor", lambda e: e.matmul(
                pb[bank_mu][:, 0:n], ones[rmu:rmu + 1, :],
                srow(mu_slot, 0, n), start=True, stop=True), inc="sT")
            em.op("tensor", lambda e: e.matmul(
                pb[bank_r][:, 0:n], ones[rr:rr + 1, :],
                srow(r_slot, 0, n), start=True, stop=True), inc="sT")
            bc = em.v("sT")
            em.wait("vector", "sT", bc)
            for k in range(NCH):
                cw = soff + k * sstride
                dw = doff + k * dstride
                em.op("vector", lambda e, cw=cw, dw=dw: e.tensor_sub(
                    dbuf[:, dw:dw + n], sbuf[:, cw:cw + n],
                    pb[bank_mu][:, 0:n]), inc="sV")
                em.op("vector", lambda e, dw=dw: e.tensor_mul(
                    dbuf[:, dw:dw + n], dbuf[:, dw:dw + n], pb[bank_r][:, 0:n]),
                    inc="sV")
                if gam is not None:
                    em.op("vector", lambda e, dw=dw, k=k: e.tensor_scalar(
                        dbuf[:, dw:dw + n], dbuf[:, dw:dw + n], bpc(gam, k),
                        bpc(bet, k), op0=OP.mult, op1=OP.add), inc="sV")

        def proj(wsrc, n_out, xn_off, n, slab_off, kch, evac, cast=False,
                 scan_steps=0, xn_stride=None):
            xst = xn_stride if xn_stride is not None else n
            slab_cols = kch * 128
            queue = "gpsimd" if cast else "sync"
            dsem = "dG" if cast else "dW"
            slab_ready, mm_done, evac_done = [], [], []

            def dma_slab(oc):
                so = slab_off + (oc % 2) * slab_cols
                if oc >= 2:
                    em.wait(queue, "sT", mm_done[oc - 2])
                em.op(queue, lambda e, so=so, oc=oc: e.dma_start(
                    out=work[:, so:so + slab_cols].rearrange(
                        "p (k c) -> p k c", c=128),
                    in_=wsrc[:, oc * 128:(oc + 1) * 128].rearrange(
                        "(k p) c -> p k c", p=128)), inc=dsem, n=16)
                slab_ready.append(em.v(dsem))

            em.wait(queue, "sT")
            em.wait("tensor", "sV")
            em.wait("tensor", "sS")
            dma_slab(0)
            if n_out > 1:
                dma_slab(1)
            for oc in range(n_out):
                so = slab_off + (oc % 2) * slab_cols
                bank = pb[oc % 2]
                em.wait("tensor", dsem, slab_ready[oc])
                if oc >= 2:
                    sem_n, sem_v = evac_done[oc - 2]
                    em.wait("tensor", sem_n, sem_v)
                for k in range(kch):
                    last = (k == kch - 1)
                    em.op("tensor", lambda e, so=so, k=k, last=last, bank=bank:
                          e.matmul(
                        bank[:, 0:n],
                        work[:, so + k * 128:so + (k + 1) * 128],
                        work[:, xn_off + k * xst:xn_off + k * xst + n],
                        start=(k == 0), stop=last),
                        inc=("sT" if last else None))
                mm_done.append(em.v("sT"))
                if oc + 2 < n_out:
                    dma_slab(oc + 2)
                evac_done.append(evac(oc, bank, mm_done[oc]))
                if scan_steps:
                    scan_pump(scan_steps)

        for dst, src in [(bp, bp_in), (ident, ident_in), (m0, m0_in),
                         (m1, m1_in), (ones, ones_in), (A_sb, Amat)]:
            em.op("sync", lambda e, dst=dst, src=src: e.dma_start(
                out=dst[:, :], in_=src[:, :]), inc="dW", n=16)
        LD_CONST = em.v("dW")
        em.op("gpsimd", lambda e: e.dma_start(
            out=strm[:, :].rearrange("p (ch t) -> p ch t", t=T),
            in_=x0[:, :].rearrange("(ch p) t -> p ch t", p=128)), inc="dG", n=16)
        LD_X0 = em.v("dG")
        for eng in ["tensor", "vector", "scalar"]:
            em.wait(eng, "dW", LD_CONST)
            em.wait(eng, "dG", LD_X0)

        def tap_stream(name):
            if name in TAPS:
                em.wait("gpsimd", "sV")
                em.op("gpsimd", lambda e, name=name: e.dma_start(
                    out=taps[name][:, :].rearrange("(ch p) t -> p ch t", p=128),
                    in_=strm[:, :].rearrange("p (ch t) -> p ch t", t=T)),
                    inc="dG", n=16)

        def finish():
            _scr = {"q_scr": q_scr, "ka_i": ka_i, "va_i": va_i,
                    "ctxa": ctxa_scr, "qb_scr": qb_scr, "kb_i": kb_i,
                    "vb_i": vb_i, "ctxb": ctxb_scr, "gate": gate_scr,
                    "m_i": m_i, "x2p": x2p_scr}
            em.wait("gpsimd", "sV")
            em.wait("gpsimd", "sS")
            em.wait("gpsimd", "dW")
            for tname in TAPS:
                if tname in _scr:
                    em.op("gpsimd", lambda e, tname=tname: e.dma_start(
                        out=taps[tname][:, :], in_=_scr[tname][:, :]),
                        inc="dG", n=16)
                elif tname == "hall":
                    em.op("gpsimd", lambda e: e.dma_start(
                        out=taps["hall"][:, :], in_=hall[:, :]), inc="dG", n=16)
            em.op("gpsimd", lambda e: e.dma_start(
                out=out[:, :].rearrange("(ch p) t -> p ch t", p=128),
                in_=strm[:, :].rearrange("p (ch t) -> p ch t", t=T)),
                inc="st", n=16)
            em.wait("gpsimd", "st")
            em.flush(block)

        W_N0, W_XN, W_SLAB = 0, 4096, 8192
        W_QTMP, W_SQ = 10240, 11264
        W_VSLAB, W_VTMP = 11264, 19456

        for tt in range(NTT):
            tw = tt * TT
            ln_stats(strm, tw, T, TT, ST_MUT, ST_RT, W_SQ)
            ln_apply(strm, tw, T, TT, ST_MUT, ST_RT, work, W_N0, TT)
            ln_stats(work, W_N0, TT, TT, ST_MUT, ST_RT, W_SQ)
            ln_apply(work, W_N0, TT, TT, ST_MUT, ST_RT, work, W_XN, TT,
                     gam="gam_a", bet="bet_a")
            if "xn_dbg" in TAPS:
                em.wait("gpsimd", "sV")
                em.op("gpsimd", lambda e, tt=tt: e.dma_start(
                    out=taps["xn_dbg"][:, tt * TT:(tt + 1) * TT].rearrange(
                        "(ch p) t -> p ch t", p=128),
                    in_=work[:, W_XN:W_XN + 4096].rearrange(
                        "p (ch t) -> p ch t", t=TT)), inc="dG", n=16)
                em.wait("tensor", "dG")
            em.wait("tensor", "sV")

            def make_ev(scr, scale=None, copy=False, tt=tt):
                dmas = []

                def ev(oc, bank, mmv, tt=tt):
                    po = W_QTMP + (oc % 2) * TT
                    em.wait("vector", "sT", mmv)
                    if len(dmas) >= 2:
                        em.wait("vector", "dG", dmas[-2])
                    else:
                        em.wait("vector", "dG")
                    if copy:
                        em.op("vector", lambda e, po=po, bank=bank: e.tensor_copy(
                            work[:, po:po + TT], bank[:, 0:TT]), inc="sV")
                    else:
                        em.op("vector", lambda e, po=po, bank=bank:
                              e.tensor_scalar_mul(
                            work[:, po:po + TT], bank[:, 0:TT], scale), inc="sV")
                    sv = em.v("sV")
                    em.wait("gpsimd", "sV", sv)
                    em.op("gpsimd", lambda e, po=po, oc=oc, tt=tt: e.dma_start(
                        out=scr[oc * 128:(oc + 1) * 128, tt * TT:(tt + 1) * TT],
                        in_=work[:, po:po + TT]), inc="dG", n=16)
                    dmas.append(em.v("dG"))
                    return ("sV", sv)
                return ev

            proj(wqT, NCH, W_XN, TT, W_SLAB, NCH, make_ev(q_scr, scale=QSC_A))
            if DBG_BARRIER:
                for _e in ["tensor", "vector", "scalar", "sync", "gpsimd"]:
                    for _s in ["sT", "sV", "sS", "dW", "dG"]:
                        em.wait(_e, _s)
            if STOP_AFTER == "qonly":
                continue
            proj(wkT, NCH, W_XN, TT, W_SLAB, NCH, make_ev(ka_i, copy=True))

            vs_ready = []
            for dvg in range(2):
                so = W_VSLAB + dvg * 4096
                em.wait("sync", "sT")
                em.op("sync", lambda e, so=so, dvg=dvg: e.dma_start(
                    out=work[:, so:so + 4096].rearrange(
                        "p (k c) -> p k c", c=512),
                    in_=wvT[:, dvg * 512:(dvg + 1) * 512].rearrange(
                        "(k p) c -> p k c", p=128)), inc="dW", n=16)
                vs_ready.append(em.v("dW"))
            vdma = []
            vev = {}
            em.wait("tensor", "sV")
            for sub in range(4):
                vo = W_VTMP + (sub % 2) * 1040
                if sub >= 2:
                    em.wait("vector", "dG", vdma[sub - 2])
                else:
                    em.wait("vector", "dG")
                for dvg in range(2):
                    so = W_VSLAB + dvg * 4096
                    bank = pb[2 + dvg]
                    em.wait("tensor", "dW", vs_ready[dvg])
                    if (sub - 1, dvg) in vev:
                        em.wait("tensor", "sV", vev[(sub - 1, dvg)])
                    for k in range(NCH):
                        em.op("tensor", lambda e, so=so, k=k, sub=sub, bank=bank:
                              e.matmul(
                            bank[:, 0:512],
                            work[:, W_XN + k * TT + sub * 128:
                                  W_XN + k * TT + sub * 128 + 128],
                            work[:, so + k * 512:so + (k + 1) * 512],
                            start=(k == 0), stop=(k == NCH - 1)),
                            inc=("sT" if k == NCH - 1 else None))
                    mmv = em.v("sT")
                    em.wait("vector", "sT", mmv)
                    em.op("vector", lambda e, vo=vo, dvg=dvg, bank=bank:
                          e.tensor_copy(
                        work[:, vo + dvg * 520:vo + dvg * 520 + 520].rearrange(
                            "p (h c) -> p h c", c=65)[:, :, 0:64],
                        bank[:, 0:512].rearrange("p (h c) -> p h c", c=64)),
                        inc="sV")
                    vev[(sub, dvg)] = em.v("sV")
                em.op("vector", lambda e, vo=vo: e.memset(
                    work[:, vo:vo + 1040].rearrange("p (h c) -> p h c", c=65)
                    [:, :, 64:65], 1.0), inc="sV")
                sv = em.v("sV")
                em.wait("gpsimd", "sV", sv)
                em.op("gpsimd", lambda e, vo=vo, tt=tt, sub=sub: e.dma_start(
                    out=va_i[tt * TT + sub * 128:tt * TT + sub * 128 + 128, :],
                    in_=work[:, vo:vo + 1040]), inc="dG", n=16)
                vdma.append(em.v("dG"))

        if STOP_AFTER == "qonly":
            finish()
            return nc
        em.wait("gpsimd", "dG")
        em.op("gpsimd", lambda e: e.collective_compute(
            "AllGather", OP.bypass, ins=[ka_i[:, :]], outs=[ka_o[:, :, :]],
            replica_groups=PAIR_RG), inc="sC")
        em.op("gpsimd", lambda e: e.collective_compute(
            "AllGather", OP.bypass, ins=[va_i[:, :]], outs=[va_o[:, :, :]],
            replica_groups=PAIR_RG), inc="sC")
        AG_A = em.v("sC")

        WA_K, WA_V, WA_QB, WA_ES, WA_RCP, WA_CTH = 0, 4096, 8256, 8512, 9024, 9216
        em.wait("sync", "sC", AG_A)
        qdma, qused = [], []
        av_evac = {}
        ctx_dmas = []
        for hg in range(NCH):
            em.wait("sync", "sT")
            for r in range(2):
                em.op("sync", lambda e, r=r, hg=hg: e.dma_start(
                    out=work[:, WA_K:WA_K + 4096].rearrange(
                        "p (g c) -> p g c", c=128)[:, r::2, :],
                    in_=ka_o[r, hg * 128:(hg + 1) * 128, :].rearrange(
                        "p (g c) -> p g c", c=128)), inc="dW", n=16)
                em.op("sync", lambda e, r=r, hg=hg: e.dma_start(
                    out=work[:, WA_V:WA_V + 4160].rearrange(
                        "p (g c) -> p g c", c=130)[:, r::2, :],
                    in_=va_o[r, :, hg * 130:(hg + 1) * 130].rearrange(
                        "(g p) c -> p g c", p=128)), inc="dW", n=16)
            kv_ready = em.v("dW")
            if hg == 0 and "kdbg" in TAPS:
                em.wait("gpsimd", "dW", kv_ready)
                em.op("gpsimd", lambda e: e.dma_start(
                    out=taps["kdbg"][:, :], in_=work[:, WA_K:WA_K + 4096]),
                    inc="dG", n=16)
                em.op("gpsimd", lambda e: e.dma_start(
                    out=taps["vdbg"][:, :], in_=work[:, WA_V:WA_V + 4160]),
                    inc="dG", n=16)
                em.wait("tensor", "dG")
            em.wait("tensor", "dW", kv_ready)
            for i in range(16):
                qo = WA_QB + (i % 2) * 128
                if len(qdma) >= 2:
                    em.wait("sync", "sT", qused[-2])
                em.op("sync", lambda e, qo=qo, hg=hg, i=i: e.dma_start(
                    out=work[:, qo:qo + 128],
                    in_=q_scr[hg * 128:(hg + 1) * 128, i * 128:(i + 1) * 128]),
                    inc="dW", n=16)
                qdma.append(em.v("dW"))
                em.wait("tensor", "dW", qdma[-1])
                em.wait("tensor", "sS")
                avb = pb[4 if i % 2 == 0 else 6]
                if (i % 2) in av_evac:
                    em.wait("tensor", "sV", av_evac[i % 2])
                ng = 2 * i + 2
                exp_done = []
                for g in range(ng):
                    par = g % 2
                    sb0, sb1 = pb[0 + par], pb[2 + par]
                    if g >= 2:
                        em.wait("tensor", "sS", exp_done[g - 2])
                    for h, sb in ((0, sb0), (1, sb1)):
                        em.op("tensor", lambda e, g=g, h=h, sb=sb, qo=qo:
                              e.matmul(
                            sb[:, 0:128],
                            work[:, WA_K + g * 128:WA_K + (g + 1) * 128]
                            [h * 64:(h + 1) * 64, :],
                            work[:, qo:qo + 128][h * 64:(h + 1) * 64, :],
                            start=True, stop=True, tile_position=(h * 64, 0)),
                            inc="sT")
                    smm = em.v("sT")
                    if g >= 2 * i:
                        mt = m0 if g == 2 * i else m1
                        em.wait("vector", "sT", smm)
                        for sb in (sb0, sb1):
                            em.op("vector", lambda e, sb=sb, mt=mt: e.tensor_add(
                                sb[:, 0:128], sb[:, 0:128], mt[:, :]), inc="sV")
                        em.wait("scalar", "sV")
                    em.wait("scalar", "sT", smm)
                    for h, sb in ((0, sb0), (1, sb1)):
                        eo = WA_ES + (par * 2 + h) * 128
                        em.op("scalar", lambda e, sb=sb, eo=eo: e.activation(
                            out=work[:, eo:eo + 128], in_=sb[:, 0:128],
                            func=AF.Exp, bias=-SHIFT_A), inc="sS")
                    exp_done.append(em.v("sS"))
                    em.wait("tensor", "sS", exp_done[-1])
                    for h in range(2):
                        eo = WA_ES + (par * 2 + h) * 128
                        em.op("tensor", lambda e, eo=eo, g=g, h=h, avb=avb,
                              ng=ng: e.matmul(
                            avb[:, h * 65:h * 65 + 65], work[:, eo:eo + 128],
                            work[:, WA_V + g * 130 + h * 65:
                                  WA_V + g * 130 + h * 65 + 65],
                            start=(g == 0 and h == 0),
                            stop=(g == ng - 1 and h == 1)),
                            inc=("sT" if (g == ng - 1 and h == 1) else None))
                av_mm = em.v("sT")
                scan_pump(2)
                em.wait("vector", "sT", av_mm)
                if i == 0 and ctx_dmas:
                    em.wait("vector", "dG", ctx_dmas[-1])
                for h in range(2):
                    if DBG_NONORM:
                        em.op("vector", lambda e, h=h, avb=avb, i=i:
                              e.tensor_copy(
                            work[:, WA_CTH + i * 128 + h * 64:
                                  WA_CTH + i * 128 + h * 64 + 63],
                            avb[:, h * 65:h * 65 + 63]), inc="sV")
                        em.op("vector", lambda e, h=h, avb=avb, i=i:
                              e.tensor_copy(
                            work[:, WA_CTH + i * 128 + h * 64 + 63:
                                  WA_CTH + i * 128 + h * 64 + 64],
                            avb[:, h * 65 + 64:h * 65 + 65]), inc="sV")
                        continue
                    rb = WA_RCP + (i % 2) * 8
                    em.wait("scalar", "sT", av_mm)
                    if (i % 2) in av_evac:
                        em.wait("scalar", "sV", av_evac[i % 2])
                    em.op("scalar", lambda e, h=h, avb=avb, rb=rb: e.activation(
                        out=work[:, rb + 4 + h:rb + 5 + h],
                        in_=avb[:, h * 65 + 64:h * 65 + 65], func=AF.Ln),
                        inc="sS")
                    em.op("scalar", lambda e, h=h, rb=rb: e.activation(
                        out=work[:, rb + h:rb + h + 1],
                        in_=work[:, rb + 4 + h:rb + 5 + h],
                        func=AF.Exp, scale=-1.0), inc="sS")
                    em.wait("vector", "sS")
                    em.op("vector", lambda e, h=h, avb=avb, i=i, rb=rb:
                          e.tensor_scalar_mul(
                        work[:, WA_CTH + i * 128 + h * 64:
                              WA_CTH + i * 128 + h * 64 + 64],
                        avb[:, h * 65:h * 65 + 64],
                        work[:, rb + h:rb + h + 1]), inc="sV")
                av_evac[i % 2] = em.v("sV")
                qused.append(em.v("sT"))
                bar()
            em.wait("gpsimd", "sV")
            em.op("gpsimd", lambda e, hg=hg: e.dma_start(
                out=ctxa_scr[:, hg * 128:(hg + 1) * 128].rearrange(
                    "(i p) c -> p i c", p=128),
                in_=work[:, WA_CTH:WA_CTH + 2048].rearrange(
                    "p (i c) -> p i c", c=128)), inc="dG", n=16)
            ctx_dmas.append(em.v("dG"))
        CTXA_SAVED = em.v("dG")

        WD_WO, WD_CTI, WD_FM = 0, 8192, 10240
        em.wait("sync", "sT")
        for oc in range(NCH):
            em.op("sync", lambda e, oc=oc: e.dma_start(
                out=work[:, WD_WO + oc * 1024:WD_WO + (oc + 1) * 1024].rearrange(
                    "p (k c) -> p k c", c=128),
                in_=woT[:, oc * 128:(oc + 1) * 128].rearrange(
                    "(k p) c -> p k c", p=128)), inc="dW", n=16)
        WO_READY = em.v("dW")
        em.wait("sync", "dG", CTXA_SAVED)
        cti_dma, cti_used = [], []
        em.wait("tensor", "dW", WO_READY)
        for i in range(16):
            co = WD_CTI + (i % 2) * 1024
            if len(cti_dma) >= 2:
                em.wait("sync", "sT", cti_used[-2])
            em.op("sync", lambda e, i=i, co=co: e.dma_start(
                out=work[:, co:co + 1024],
                in_=ctxa_scr[i * 128:(i + 1) * 128, :]), inc="dW", n=16)
            cti_dma.append(em.v("dW"))
            em.wait("tensor", "dW", cti_dma[-1])
            tr_done = []
            for ec in range(NCH):
                bank = pb[ec % 2]
                if ec >= 2:
                    em.wait("tensor", "sV", tr_done[ec - 2])
                em.op("tensor", lambda e, co=co, ec=ec, bank=bank: e.transpose(
                    bank[:, 0:128],
                    work[:, co + ec * 128:co + (ec + 1) * 128], ident[:, :]),
                    inc="sT")
                em.wait("vector", "sT")
                em.op("vector", lambda e, ec=ec, bank=bank: e.tensor_copy(
                    work[:, WD_FM + ec * 128:WD_FM + (ec + 1) * 128],
                    bank[:, 0:128]), inc="sV")
                tr_done.append(em.v("sV"))
            em.wait("tensor", "sV", tr_done[-1])
            opv = []
            for oc in range(NCH):
                bank = pb[2 + oc % 2]
                if oc >= 2:
                    em.wait("tensor", "sV", opv[oc - 2])
                for k in range(NCH):
                    em.op("tensor", lambda e, oc=oc, k=k, bank=bank: e.matmul(
                        bank[:, 0:128],
                        work[:, WD_WO + oc * 1024 + k * 128:
                              WD_WO + oc * 1024 + (k + 1) * 128],
                        work[:, WD_FM + k * 128:WD_FM + (k + 1) * 128],
                        start=(k == 0), stop=(k == NCH - 1)),
                        inc=("sT" if k == NCH - 1 else None))
                mmv = em.v("sT")
                em.wait("vector", "sT", mmv)
                em.op("vector", lambda e, oc=oc, i=i, bank=bank:
                      e.scalar_tensor_tensor(
                    strm[:, oc * T + i * 128:oc * T + (i + 1) * 128],
                    bank[:, 0:128], bpc("bout_a", oc),
                    strm[:, oc * T + i * 128:oc * T + (i + 1) * 128],
                    op0=OP.add, op1=OP.add), inc="sV")
                opv.append(em.v("sV"))
            cti_used.append(em.v("sT"))
            for _e in ["tensor", "vector", "scalar", "sync", "gpsimd"]:
                for _s in ["sT", "sV", "sS", "dW", "dG"]:
                    em.wait(_e, _s)
            scan_pump(2)
        tap_stream("x1")
        if STOP_AFTER == "s1":
            finish()
            return nc

        BF16 = mybir.dt.bfloat16

        def ffn_phase():
            F_XNT, F_XNT2, F_XH, F_XL = 0, 512, 1024, 3072
            F_SQ, F_GS, F_H2T = 5120, 5120, 6144
            F_GSL, F_USL = 6656, 7680
            F_H2H, F_H2L, F_DSL = 8704, 14848, 0
            xhv = work[:, F_XH:F_XH + 2048].bitcast(BF16)
            xlv = work[:, F_XL:F_XL + 2048].bitcast(BF16)
            h2hv = work[:, F_H2H:F_H2H + 6144].bitcast(BF16)
            h2lv = work[:, F_H2L:F_H2L + 6144].bitcast(BF16)
            for tt in range(NTT):
                tw = tt * TT
                ln_stats(strm, tw, T, TT, ST_MUT, ST_RT, F_SQ)
                em.wait("tensor", "sS")
                em.wait("tensor", "sV")
                rmu, rr = ST_MUT[0], ST_RT[0]
                em.op("tensor", lambda e: e.matmul(
                    pb[4][:, 0:TT], ones[rmu:rmu + 1, :], srow(ST_MUT, 0, TT),
                    start=True, stop=True), inc="sT")
                em.op("tensor", lambda e: e.matmul(
                    pb[6][:, 0:TT], ones[rr:rr + 1, :], srow(ST_RT, 0, TT),
                    start=True, stop=True), inc="sT")
                bc = em.v("sT")
                em.wait("vector", "sT", bc)
                for k in range(NCH):
                    cw = tw + k * T
                    em.op("vector", lambda e, cw=cw: e.tensor_sub(
                        work[:, F_XNT:F_XNT + TT], strm[:, cw:cw + TT],
                        pb[4][:, 0:TT]), inc="sV")
                    em.op("vector", lambda e: e.tensor_mul(
                        work[:, F_XNT:F_XNT + TT], work[:, F_XNT:F_XNT + TT],
                        pb[6][:, 0:TT]), inc="sV")
                    em.op("vector", lambda e, k=k: e.tensor_copy(
                        xhv[:, k * TT:(k + 1) * TT],
                        work[:, F_XNT:F_XNT + TT]), inc="sV")
                    em.op("vector", lambda e, k=k: e.tensor_sub(
                        work[:, F_XNT2:F_XNT2 + TT], work[:, F_XNT:F_XNT + TT],
                        xhv[:, k * TT:(k + 1) * TT]), inc="sV")
                    em.op("vector", lambda e, k=k: e.tensor_copy(
                        xlv[:, k * TT:(k + 1) * TT],
                        work[:, F_XNT2:F_XNT2 + TT]), inc="sV")
                em.wait("tensor", "sV")
                gsl_ready, usl_ready, gmm, umm, h2v = [], [], [], [], []

                def dma_g(hc):
                    so = F_GSL + (hc % 2) * 512
                    uo = F_USL + (hc % 2) * 512
                    if hc >= 2:
                        em.wait("gpsimd", "sT", umm[hc - 2])
                    em.op("gpsimd", lambda e, so=so, hc=hc: e.dma_start(
                        out=work[:, so:so + 512].bitcast(BF16).rearrange(
                            "p (k c) -> p k c", c=128),
                        in_=wgT[:, hc * 128:(hc + 1) * 128].rearrange(
                            "(k p) c -> p k c", p=128)), inc="dG", n=16)
                    gsl_ready.append(em.v("dG"))
                    em.op("gpsimd", lambda e, uo=uo, hc=hc: e.dma_start(
                        out=work[:, uo:uo + 512].bitcast(BF16).rearrange(
                            "p (k c) -> p k c", c=128),
                        in_=wuT[:, hc * 128:(hc + 1) * 128].rearrange(
                            "(k p) c -> p k c", p=128)), inc="dG", n=16)
                    usl_ready.append(em.v("dG"))

                em.wait("gpsimd", "sT")
                dma_g(0); dma_g(1)
                for hc in range(24):
                    so = F_GSL + (hc % 2) * 512
                    uo = F_USL + (hc % 2) * 512
                    gb, ub = pb[hc % 2], pb[2 + hc % 2]
                    gslv = work[:, so:so + 512].bitcast(BF16)
                    uslv = work[:, uo:uo + 512].bitcast(BF16)
                    em.wait("tensor", "dG", usl_ready[hc])
                    if hc >= 2:
                        em.wait("tensor", "sV", h2v[hc - 2])
                        em.wait("tensor", "sS")
                    for k in range(NCH):
                        for ph, xv in ((0, xhv), (1, xlv)):
                            em.op("tensor", lambda e, k=k, gb=gb, gslv=gslv,
                                  xv=xv, ph=ph: e.matmul(
                                gb[:, 0:TT], gslv[:, k * 128:(k + 1) * 128],
                                xv[:, k * TT:k * TT + TT],
                                start=(k == 0 and ph == 0),
                                stop=(k == NCH - 1 and ph == 1)),
                                inc=("sT" if (k == NCH - 1 and ph == 1)
                                     else None))
                    gmm.append(em.v("sT"))
                    for k in range(NCH):
                        for ph, xv in ((0, xhv), (1, xlv)):
                            em.op("tensor", lambda e, k=k, ub=ub, uslv=uslv,
                                  xv=xv, ph=ph: e.matmul(
                                ub[:, 0:TT], uslv[:, k * 128:(k + 1) * 128],
                                xv[:, k * TT:k * TT + TT],
                                start=(k == 0 and ph == 0),
                                stop=(k == NCH - 1 and ph == 1)),
                                inc=("sT" if (k == NCH - 1 and ph == 1)
                                     else None))
                    umm.append(em.v("sT"))
                    if hc + 2 < 24:
                        dma_g(hc + 2)
                    go = F_GS + (hc % 2) * TT
                    em.wait("scalar", "sT", gmm[-1])
                    em.op("scalar", lambda e, go=go, gb=gb: e.activation(
                        out=work[:, go:go + TT], in_=gb[:, 0:TT], func=AF.Silu,
                        scale=INV127), inc="sS")
                    ssv = em.v("sS")
                    em.wait("vector", "sT", umm[-1])
                    em.wait("vector", "sS", ssv)
                    em.op("vector", lambda e, go=go, ub=ub:
                          e.scalar_tensor_tensor(
                        work[:, F_H2T:F_H2T + TT], ub[:, 0:TT], INV127,
                        work[:, go:go + TT], op0=OP.mult, op1=OP.mult),
                        inc="sV")
                    em.op("vector", lambda e, hc=hc: e.tensor_copy(
                        h2hv[:, hc * TT:(hc + 1) * TT],
                        work[:, F_H2T:F_H2T + TT]), inc="sV")
                    em.op("vector", lambda e, hc=hc: e.tensor_sub(
                        work[:, F_XNT2:F_XNT2 + TT], work[:, F_H2T:F_H2T + TT],
                        h2hv[:, hc * TT:(hc + 1) * TT]), inc="sV")
                    em.op("vector", lambda e, hc=hc: e.tensor_copy(
                        h2lv[:, hc * TT:(hc + 1) * TT],
                        work[:, F_XNT2:F_XNT2 + TT]), inc="sV")
                    h2v.append(em.v("sV"))
                    scan_pump(5)
                dsl_ready, dmm, resv = [], [], []

                def dma_d(oc):
                    so = F_DSL + (oc % 2) * 1536
                    if oc >= 2:
                        em.wait("gpsimd", "sT", dmm[oc - 2])
                    em.op("gpsimd", lambda e, so=so, oc=oc: e.dma_start(
                        out=work[:, so:so + 1536].bitcast(BF16).rearrange(
                            "p (k c) -> p k c", c=128),
                        in_=wdT[:, oc * 128:(oc + 1) * 128].rearrange(
                            "(k p) c -> p k c", p=128)), inc="dG", n=16)
                    dsl_ready.append(em.v("dG"))

                em.wait("gpsimd", "sT")
                em.wait("gpsimd", "sV")
                dma_d(0); dma_d(1)
                em.wait("tensor", "sV", h2v[-1])
                for oc in range(NCH):
                    so = F_DSL + (oc % 2) * 1536
                    bank = pb[4 if oc % 2 == 0 else 6]
                    dslv = work[:, so:so + 1536].bitcast(BF16)
                    em.wait("tensor", "dG", dsl_ready[oc])
                    if oc >= 2:
                        em.wait("tensor", "sV", resv[oc - 2])
                    for k in range(24):
                        for ph, hv in ((0, h2hv), (1, h2lv)):
                            em.op("tensor", lambda e, k=k, bank=bank, hv=hv,
                                  ph=ph, dslv=dslv: e.matmul(
                                bank[:, 0:TT], dslv[:, k * 128:(k + 1) * 128],
                                hv[:, k * TT:k * TT + TT],
                                start=(k == 0 and ph == 0),
                                stop=(k == 23 and ph == 1)),
                                inc=("sT" if (k == 23 and ph == 1) else None))
                    dmm.append(em.v("sT"))
                    if oc + 2 < NCH:
                        dma_d(oc + 2)
                    em.wait("vector", "sT", dmm[-1])
                    em.op("vector", lambda e, oc=oc, tw=tw, bank=bank:
                          e.scalar_tensor_tensor(
                        strm[:, oc * T + tw:oc * T + tw + TT], bank[:, 0:TT],
                        INV127, strm[:, oc * T + tw:oc * T + tw + TT],
                        op0=OP.mult, op1=OP.add), inc="sV")
                    resv.append(em.v("sV"))
                    scan_pump(5)

        ffn_phase()
        tap_stream("x2")
        if STOP_AFTER == "s2":
            finish()
            return nc

        em.wait("gpsimd", "sV")
        em.op("gpsimd", lambda e: e.dma_start(
            out=x2_scr[:, :].rearrange("(ch p) t -> p ch t", p=128),
            in_=strm[:, :].rearrange("p (ch t) -> p ch t", t=T)), inc="dG", n=16)
        X2_SAVED = em.v("dG")

        C_XN, C_SQ, C_SLAB, C_QTMP = 0, 4096, 5120, 4096
        C_VSL, C_VTMP = 7168, 15360
        C_WDL, C_WIN, C_BVB = C_VTMP + 512, C_VTMP + 640, 17424
        em.wait("sync", "sT")
        em.wait("sync", "sV")
        em.op("sync", lambda e: e.dma_start(
            out=work[:, C_WDL:C_WDL + 128].rearrange("p (k c) -> p k c", c=16),
            in_=Wdl[:, :].rearrange("(k p) c -> p k c", p=128)), inc="dW", n=16)
        em.op("sync", lambda e: e.dma_start(
            out=work[:, C_WIN:C_WIN + 128].rearrange("p (k c) -> p k c", c=16),
            in_=Win[:, :].rearrange("(k p) c -> p k c", p=128)), inc="dW", n=16)
        WDLWIN = em.v("dW")
        em.wait("tensor", "dW", WDLWIN)
        gate_dmas = []
        for tt in range(NTT):
            tw = tt * TT
            ln_stats(strm, tw, T, TT, ST_MUT, ST_RT, C_SQ)
            if tt == 0:
                em.wait("vector", "dG", X2_SAVED)
            ln_apply(strm, tw, T, TT, ST_MUT, ST_RT, strm, tw, T)
            ln_stats(strm, tw, T, TT, ST_MUT, ST_RT, C_SQ)
            ln_apply(strm, tw, T, TT, ST_MUT, ST_RT, work, C_XN, TT,
                     gam="lng", bet="lnb")
            em.wait("tensor", "sV")
            for woff, bname, func, r0, r1 in [
                    (C_WDL, "bdelta", AF.Sigmoid, 0, 16),
                    (C_WIN, "b_in", AF.Identity, 32, 48)]:
                em.wait("tensor", "sS")
                for k in range(NCH):
                    em.op("tensor", lambda e, woff=woff, k=k: e.matmul(
                        pb[6][0:16, 0:TT],
                        work[:, woff + k * 16:woff + (k + 1) * 16],
                        work[:, C_XN + k * TT:C_XN + k * TT + TT],
                        start=(k == 0), stop=(k == NCH - 1)),
                        inc=("sT" if k == NCH - 1 else None))
                mmv = em.v("sT")
                em.wait("scalar", "sT", mmv)
                em.op("scalar", lambda e, bname=bname, func=func, r0=r0, r1=r1:
                      e.activation(
                    out=work[r0:r1, C_VTMP:C_VTMP + TT],
                    in_=pb[6][0:16, 0:TT], func=func,
                    bias=bp[0:16, BP[bname]:BP[bname] + 1]), inc="sS")
            dm_done = em.v("sS")
            em.wait("gpsimd", "sS", dm_done)
            em.op("gpsimd", lambda e, tt=tt: e.dma_start(
                out=m_i[:, tt * TT:(tt + 1) * TT],
                in_=work[0:48, C_VTMP:C_VTMP + TT]), inc="dG", n=16)

            def evg(oc, bank, mmv):
                po = C_QTMP + (oc % 2) * TT
                em.wait("scalar", "sT", mmv)
                if len(gate_dmas) >= 2:
                    em.wait("scalar", "dG", gate_dmas[-2])
                else:
                    em.wait("scalar", "dG")
                em.op("scalar", lambda e, po=po, oc=oc, bank=bank: e.activation(
                    out=work[:, po:po + TT], in_=bank[:, 0:TT], func=AF.Sigmoid,
                    bias=bpc("bgate", oc)), inc="sS")
                sv = em.v("sS")
                em.wait("gpsimd", "sS", sv)
                em.op("gpsimd", lambda e, po=po, oc=oc, tt=tt: e.dma_start(
                    out=gate_scr[oc * 128:(oc + 1) * 128, tt * TT:(tt + 1) * TT],
                    in_=work[:, po:po + TT]), inc="dG", n=16)
                gate_dmas.append(em.v("dG"))
                return ("sS", sv)
            proj(Wgt, NCH, C_XN, TT, C_SLAB, NCH, evg)

        em.wait("gpsimd", "dG")
        em.op("gpsimd", lambda e: e.collective_compute(
            "AllGather", OP.bypass, ins=[m_i[:, :]], outs=[m_o[:, :, :]],
            replica_groups=PAIR_RG), inc="sC")
        AG_M = em.v("sC")
        em.wait("sync", "sC", AG_M)
        for r in range(2):
            em.op("sync", lambda e, r=r: e.dma_start(
                out=dfull[:, :].rearrange("p (g c) -> p g c", c=128)[:, r::2, :],
                in_=m_o[r, 0:16, :].rearrange("p (g c) -> p g c", c=128)),
                inc="dW", n=16)
            em.op("sync", lambda e, r=r: e.dma_start(
                out=btfull[:, :].rearrange("p (g c) -> p g c", c=128)[:, r::2, :],
                in_=m_o[r, 32:48, :].rearrange("p (g c) -> p g c", c=128)),
                inc="dW", n=16)
        MB_READY = em.v("dW")
        em.wait("vector", "dW", MB_READY)
        em.op("vector", lambda e: e.memset(hall[:, 0:1], 0.0), inc="sV")
        HALL0 = em.v("sV")
        em.wait("scalar", "sV", HALL0)
        em.wait("scalar", "dW", MB_READY)
        em.wait("tensor", "dW", MB_READY)
        scan_state["ready"] = True
        if not SCAN_INTERLEAVE:
            _scan_emit(NSCAN)

        em.wait("sync", "sS")
        em.wait("sync", "sV")
        em.op("sync", lambda e: e.dma_start(out=srow(ST_BV, 0, 1024),
              in_=rp_in[0:1, 0:1024]), inc="dW", n=16)
        RP_LD = em.v("dW")
        em.wait("tensor", "dW", RP_LD)
        em.wait("tensor", "sV")
        em.wait("tensor", "sS")
        em.op("tensor", lambda e: e.matmul(
            pb[6][:, 0:512], ones[64:65, :], srow(ST_BV, 0, 512), start=True,
            stop=True), inc="sT")
        em.op("tensor", lambda e: e.matmul(
            pb[7][:, 0:512], ones[64:65, :], srow(ST_BV, 512, 512), start=True,
            stop=True), inc="sT")
        BVB_MM = em.v("sT")
        em.wait("vector", "sT", BVB_MM)
        em.op("vector", lambda e: e.tensor_copy(
            work[:, C_BVB:C_BVB + 512], pb[6][:, 0:512]), inc="sV")
        em.op("vector", lambda e: e.tensor_copy(
            work[:, C_BVB + 512:C_BVB + 1024], pb[7][:, 0:512]), inc="sV")
        BVB_DONE = em.v("sV")

        qbdmas, kbdmas = [], []
        for tt in range(NTT):
            tw = tt * TT
            ln_stats(strm, tw, T, TT, ST_MUT, ST_RT, C_SQ)
            ln_apply(strm, tw, T, TT, ST_MUT, ST_RT, work, C_XN, TT,
                     gam="ln1g", bet="ln1b")
            em.wait("tensor", "sV")

            def evqb(oc, bank, mmv):
                po = C_QTMP + (oc % 2) * TT
                em.wait("vector", "sT", mmv)
                if len(qbdmas) >= 2:
                    em.wait("vector", "dG", qbdmas[-2])
                else:
                    em.wait("vector", "dG")
                em.op("vector", lambda e, po=po, oc=oc, bank=bank:
                      e.tensor_scalar(
                    work[:, po:po + TT], bank[:, 0:TT], bpc("bq", oc), QSC_B,
                    op0=OP.add, op1=OP.mult), inc="sV")
                sv = em.v("sV")
                em.wait("gpsimd", "sV", sv)
                em.op("gpsimd", lambda e, po=po, oc=oc, tt=tt: e.dma_start(
                    out=qb_scr[oc * 128:(oc + 1) * 128, tt * TT:(tt + 1) * TT],
                    in_=work[:, po:po + TT]), inc="dG", n=16)
                qbdmas.append(em.v("dG"))
                return ("sV", sv)
            proj(Wq, NCH, C_XN, TT, C_SLAB, NCH, evqb, scan_steps=6)

            def evkb(oc, bank, mmv):
                po = C_QTMP + (oc % 2) * TT
                em.wait("vector", "sT", mmv)
                if len(kbdmas) >= 2:
                    em.wait("vector", "dG", kbdmas[-2])
                else:
                    em.wait("vector", "dG")
                em.op("vector", lambda e, po=po, oc=oc, bank=bank:
                      e.tensor_scalar_add(
                    work[:, po:po + TT], bank[:, 0:TT], bpc("bk", oc)),
                    inc="sV")
                sv = em.v("sV")
                em.wait("gpsimd", "sV", sv)
                em.op("gpsimd", lambda e, po=po, oc=oc, tt=tt: e.dma_start(
                    out=kb_i[oc * 128:(oc + 1) * 128, tt * TT:(tt + 1) * TT],
                    in_=work[:, po:po + TT]), inc="dG", n=16)
                kbdmas.append(em.v("dG"))
                return ("sV", sv)
            proj(Wk, NCH, C_XN, TT, C_SLAB, NCH, evkb, scan_steps=6)

            em.wait("vector", "sV", BVB_DONE)
            vs_ready = []
            for dvg in range(2):
                so = C_VSL + dvg * 4096
                em.wait("sync", "sT")
                em.op("sync", lambda e, so=so, dvg=dvg: e.dma_start(
                    out=work[:, so:so + 4096].rearrange(
                        "p (k c) -> p k c", c=512),
                    in_=Wv[:, dvg * 512:(dvg + 1) * 512].rearrange(
                        "(k p) c -> p k c", p=128)), inc="dW", n=16)
                vs_ready.append(em.v("dW"))
            vdma = []
            vev = {}
            em.wait("tensor", "sV")
            for sub in range(4):
                vo = C_VTMP + (sub % 2) * 1032
                if sub >= 2:
                    em.wait("vector", "dG", vdma[sub - 2])
                else:
                    em.wait("vector", "dG")
                for dvg in range(2):
                    so = C_VSL + dvg * 4096
                    bank = pb[2 + dvg]
                    em.wait("tensor", "dW", vs_ready[dvg])
                    if (sub - 1, dvg) in vev:
                        em.wait("tensor", "sV", vev[(sub - 1, dvg)])
                    for k in range(NCH):
                        em.op("tensor", lambda e, so=so, k=k, sub=sub, bank=bank:
                              e.matmul(
                            bank[:, 0:512],
                            work[:, C_XN + k * TT + sub * 128:
                                  C_XN + k * TT + sub * 128 + 128],
                            work[:, so + k * 512:so + (k + 1) * 512],
                            start=(k == 0), stop=(k == NCH - 1)),
                            inc=("sT" if k == NCH - 1 else None))
                    mmv = em.v("sT")
                    em.wait("vector", "sT", mmv)
                    em.op("vector", lambda e, vo=vo, dvg=dvg, bank=bank:
                          e.tensor_add(
                        work[:, vo + dvg * 516:vo + dvg * 516 + 516].rearrange(
                            "p (h c) -> p h c", c=129)[:, :, 0:128],
                        bank[:, 0:512].rearrange("p (h c) -> p h c", c=128),
                        work[:, C_BVB + dvg * 512:C_BVB + (dvg + 1) * 512]
                        .rearrange("p (h c) -> p h c", c=128)), inc="sV")
                    vev[(sub, dvg)] = em.v("sV")
                em.op("vector", lambda e, vo=vo: e.memset(
                    work[:, vo:vo + 1032].rearrange("p (h c) -> p h c", c=129)
                    [:, :, 128:129], 1.0), inc="sV")
                sv = em.v("sV")
                em.wait("gpsimd", "sV", sv)
                em.op("gpsimd", lambda e, vo=vo, tt=tt, sub=sub: e.dma_start(
                    out=vb_i[tt * TT + sub * 128:tt * TT + sub * 128 + 128, :],
                    in_=work[:, vo:vo + 1032]), inc="dG", n=16)
                vdma.append(em.v("dG"))
                scan_pump(8)

        em.wait("gpsimd", "dG")
        em.op("gpsimd", lambda e: e.collective_compute(
            "AllGather", OP.bypass, ins=[kb_i[:, :]], outs=[kb_o[:, :, :]],
            replica_groups=PAIR_RG), inc="sC")
        em.op("gpsimd", lambda e: e.collective_compute(
            "AllGather", OP.bypass, ins=[vb_i[:, :]], outs=[vb_o[:, :, :]],
            replica_groups=PAIR_RG), inc="sC")
        AG_B = em.v("sC")

        WB_K, WB_V, WB_QB, WB_ES, WB_RCP, WB_CTH = 0, 4096, 8224, 9248, 11296, 11392
        em.wait("sync", "sC", AG_B)
        qbt_dma, qbt_used = [], []
        avb_evac = {}
        ctxb_dmas = []
        for h in range(8):
            em.wait("sync", "sT")
            for r in range(2):
                em.op("sync", lambda e, r=r, h=h: e.dma_start(
                    out=work[:, WB_K + r * 2048:WB_K + r * 2048 + 2048],
                    in_=kb_o[r, h * 128:(h + 1) * 128, :]), inc="dW", n=16)
                em.op("sync", lambda e, r=r, h=h: e.dma_start(
                    out=work[:, WB_V + r * 2064:WB_V + r * 2064 + 2064]
                    .rearrange("p (g c) -> p g c", c=129),
                    in_=vb_o[r, :, h * 129:(h + 1) * 129].rearrange(
                        "(g p) c -> p g c", p=128)), inc="dW", n=16)
            kv_ready = em.v("dW")
            em.wait("tensor", "dW", kv_ready)
            for qt in range(NTT):
                qo = WB_QB + (qt % 2) * TT
                if len(qbt_dma) >= 2:
                    em.wait("sync", "sT", qbt_used[-2])
                em.op("sync", lambda e, qo=qo, h=h, qt=qt: e.dma_start(
                    out=work[:, qo:qo + TT],
                    in_=qb_scr[h * 128:(h + 1) * 128, qt * TT:(qt + 1) * TT]),
                    inc="dW", n=16)
                qbt_dma.append(em.v("dW"))
                em.wait("tensor", "dW", qbt_dma[-1])
                em.wait("tensor", "sS")
                if avb_evac:
                    em.wait("tensor", "sV", max(avb_evac.values()))
                av0, av1 = pb[4], pb[6]
                exp_done = []
                for g in range(32):
                    par = g % 2
                    sb = pb[par]
                    if g >= 2:
                        em.wait("tensor", "sS", exp_done[g - 2])
                    em.op("tensor", lambda e, g=g, sb=sb, qo=qo: e.matmul(
                        sb[:, 0:TT],
                        work[:, WB_K + g * 128:WB_K + (g + 1) * 128],
                        work[:, qo:qo + TT], start=True, stop=True), inc="sT")
                    smm = em.v("sT")
                    em.wait("scalar", "sT", smm)
                    eo = WB_ES + par * TT
                    em.op("scalar", lambda e, sb=sb, eo=eo: e.activation(
                        out=work[:, eo:eo + TT], in_=sb[:, 0:TT], func=AF.Exp,
                        bias=-SHIFT_B), inc="sS")
                    exp_done.append(em.v("sS"))
                    em.wait("tensor", "sS", exp_done[-1])
                    for sq in range(4):
                        avb = av0 if sq < 2 else av1
                        em.op("tensor", lambda e, eo=eo, g=g, sq=sq, avb=avb:
                              e.matmul(
                            avb[:, (sq % 2) * 129:(sq % 2) * 129 + 129],
                            work[:, eo + sq * 128:eo + (sq + 1) * 128],
                            work[:, WB_V + g * 129:WB_V + (g + 1) * 129],
                            start=(g == 0 and sq % 2 == 0),
                            stop=(g == 31 and sq % 2 == 1)),
                            inc=("sT" if (g == 31 and sq == 3) else None))
                    scan_pump(1)
                av_mm = em.v("sT")
                em.wait("vector", "sT", av_mm)
                if qt == 0 and ctxb_dmas:
                    em.wait("vector", "dG", ctxb_dmas[-1])
                rb = WB_RCP + (qt % 2) * 16
                em.wait("scalar", "sT", av_mm)
                if (qt % 2) in avb_evac:
                    em.wait("scalar", "sV", avb_evac[qt % 2])
                for sq in range(4):
                    avb = av0 if sq < 2 else av1
                    em.op("scalar", lambda e, sq=sq, avb=avb, rb=rb:
                          e.activation(
                        out=work[:, rb + 8 + sq:rb + 9 + sq],
                        in_=avb[:, (sq % 2) * 129 + 128:(sq % 2) * 129 + 129],
                        func=AF.Ln), inc="sS")
                    em.op("scalar", lambda e, sq=sq, rb=rb: e.activation(
                        out=work[:, rb + sq:rb + sq + 1],
                        in_=work[:, rb + 8 + sq:rb + 9 + sq],
                        func=AF.Exp, scale=-1.0), inc="sS")
                em.wait("vector", "sS")
                for sq in range(4):
                    avb = av0 if sq < 2 else av1
                    em.op("vector", lambda e, sq=sq, avb=avb, qt=qt, rb=rb:
                          e.tensor_scalar_mul(
                        work[:, WB_CTH + (qt * 4 + sq) * 128:
                              WB_CTH + (qt * 4 + sq) * 128 + 128],
                        avb[:, (sq % 2) * 129:(sq % 2) * 129 + 128],
                        work[:, rb + sq:rb + sq + 1]), inc="sV")
                avb_evac[qt % 2] = em.v("sV")
                qbt_used.append(em.v("sT"))
                bar()
            em.wait("gpsimd", "sV")
            em.op("gpsimd", lambda e, h=h: e.dma_start(
                out=ctxb_scr[:, h * 128:(h + 1) * 128].rearrange(
                    "(i p) c -> p i c", p=128),
                in_=work[:, WB_CTH:WB_CTH + 2048].rearrange(
                    "p (i c) -> p i c", c=128)), inc="dG", n=16)
            ctxb_dmas.append(em.v("dG"))
        CTXB_SAVED = em.v("dG")

        WB_WO, WB_CTI, WB_FM, WB_X2P, WB_SQ2 = 0, 8192, 12288, 16384, 20480
        em.wait("sync", "sT")
        for oc in range(NCH):
            em.op("sync", lambda e, oc=oc: e.dma_start(
                out=work[:, WB_WO + oc * 1024:WB_WO + (oc + 1) * 1024]
                .rearrange("p (k c) -> p k c", c=128),
                in_=Wo[:, oc * 128:(oc + 1) * 128].rearrange(
                    "(k p) c -> p k c", p=128)), inc="dW", n=16)
        WOB_READY = em.v("dW")
        em.wait("sync", "dG", CTXB_SAVED)
        em.wait("tensor", "dW", WOB_READY)
        x2p_dmas = []
        for qt in range(NTT):
            em.wait("sync", "sT")
            em.op("sync", lambda e, qt=qt: e.dma_start(
                out=work[:, WB_CTI:WB_CTI + 4096].rearrange(
                    "p (s c) -> p s c", c=1024),
                in_=ctxb_scr[qt * 512:(qt + 1) * 512, :].rearrange(
                    "(s p) c -> p s c", p=128)), inc="dW", n=16)
            cti_ld = em.v("dW")
            em.wait("tensor", "dW", cti_ld)
            em.wait("vector", "sS")
            tr_done = []
            for blk in range(4 * NCH):
                sq, ec = blk // NCH, blk % NCH
                bank = pb[blk % 2]
                if blk >= 2:
                    em.wait("tensor", "sV", tr_done[blk - 2])
                em.op("tensor", lambda e, sq=sq, ec=ec, bank=bank:
                      e.transpose(
                    bank[:, 0:128],
                    work[:, WB_CTI + sq * 1024 + ec * 128:
                          WB_CTI + sq * 1024 + (ec + 1) * 128],
                    ident[:, :]), inc="sT")
                em.wait("vector", "sT")
                em.op("vector", lambda e, sq=sq, ec=ec, bank=bank:
                      e.tensor_copy(
                    work[:, WB_FM + ec * TT + sq * 128:
                          WB_FM + ec * TT + (sq + 1) * 128], bank[:, 0:128]),
                    inc="sV")
                tr_done.append(em.v("sV"))
            em.wait("tensor", "sV", tr_done[-1])
            x2p_off = WB_X2P
            opv = []
            for oc in range(NCH):
                bank = pb[2 + oc % 2]
                if oc >= 2:
                    em.wait("tensor", "sV", opv[oc - 2])
                for k in range(NCH):
                    em.op("tensor", lambda e, oc=oc, k=k, bank=bank: e.matmul(
                        bank[:, 0:TT],
                        work[:, WB_WO + oc * 1024 + k * 128:
                              WB_WO + oc * 1024 + (k + 1) * 128],
                        work[:, WB_FM + k * TT:WB_FM + (k + 1) * TT],
                        start=(k == 0), stop=(k == NCH - 1)),
                        inc=("sT" if k == NCH - 1 else None))
                mmv = em.v("sT")
                em.wait("vector", "sT", mmv)
                em.op("vector", lambda e, oc=oc, qt=qt, bank=bank,
                      x2p_off=x2p_off: e.scalar_tensor_tensor(
                    work[:, x2p_off + oc * TT:x2p_off + (oc + 1) * TT],
                    bank[:, 0:TT], bpc("bo", oc),
                    strm[:, oc * T + qt * TT:oc * T + (qt + 1) * TT],
                    op0=OP.add, op1=OP.add), inc="sV")
                opv.append(em.v("sV"))
            scan_pump(4)
            ln_stats(work, x2p_off, TT, TT, sl(ST_MU2X, qt * TT),
                     sl(ST_R2X, qt * TT), WB_SQ2)
            em.wait("gpsimd", "sV")
            em.wait("gpsimd", "sS")
            em.op("gpsimd", lambda e, x2p_off=x2p_off, qt=qt: e.dma_start(
                out=x2p_scr[:, qt * TT:(qt + 1) * TT].rearrange(
                    "(ch p) t -> p ch t", p=128),
                in_=work[:, x2p_off:x2p_off + 4096].rearrange(
                    "p (ch t) -> p ch t", t=TT)), inc="dG", n=16)
            x2p_dmas.append(em.v("dG"))
            bar()
        X2P_SAVED = em.v("dG")


        H_X2P, H_XN2, H_HB, H_W1S = 0, 2048, 4096, 12288
        H_W2S, H_TOUT = 14336, 18432
        H_MB, H_GT, H_WFS, H_FP, H_SQF = 4096, 6144, 8192, 12288, 14336
        TB = 256
        for t2 in range(8):
            tw2 = t2 * TB
            qt_of = t2 // 2
            em.wait("sync", "dG", X2P_SAVED)
            em.wait("sync", "sV")
            em.op("sync", lambda e, t2=t2: e.dma_start(
                out=work[:, H_X2P:H_X2P + 2048].rearrange(
                    "p (ch t) -> p ch t", t=TB),
                in_=x2p_scr[:, t2 * TB:(t2 + 1) * TB].rearrange(
                    "(ch p) t -> p ch t", p=128)), inc="dW", n=16)
            x2p_ld = em.v("dW")
            em.wait("vector", "dW", x2p_ld)
            ln_apply(work, H_X2P, TB, TB, sl(ST_MU2X, tw2), sl(ST_R2X, tw2),
                     work, H_XN2, TB, gam="ln2g", bet="ln2b")
            em.wait("tensor", "sV")

            def evw1(hc, bank, mmv):
                em.wait("scalar", "sT", mmv)
                em.op("scalar", lambda e, hc=hc, bank=bank: e.activation(
                    out=work[:, H_HB + hc * TB:H_HB + (hc + 1) * TB],
                    in_=bank[:, 0:TB], func=AF.Relu, bias=bpc("b1", hc)),
                    inc="sS")
                return ("sS", em.v("sS"))
            proj(W1, 32, H_XN2, TB, H_W1S, NCH, evw1, scan_steps=6)
            em.wait("tensor", "sS")

            w2_ready, w2mm, toutv = [], [], []

            def dma_w2(idx):
                so = H_W2S + (idx % 2) * 2048
                oc, hh = idx // 2, idx % 2
                if idx >= 2:
                    em.wait("sync", "sT", w2mm[idx - 2])
                em.op("sync", lambda e, so=so, oc=oc, hh=hh: e.dma_start(
                    out=work[:, so:so + 2048].rearrange(
                        "p (k c) -> p k c", c=128),
                    in_=W2[hh * 2048:(hh + 1) * 2048,
                           oc * 128:(oc + 1) * 128].rearrange(
                        "(k p) c -> p k c", p=128)), inc="dW", n=16)
                w2_ready.append(em.v("dW"))

            em.wait("sync", "sT")
            dma_w2(0); dma_w2(1)
            for oc in range(NCH):
                bank = pb[4 if oc % 2 == 0 else 6]
                if oc >= 2:
                    em.wait("tensor", "sV", toutv[oc - 2])
                for hh in range(2):
                    idx = oc * 2 + hh
                    so = H_W2S + (idx % 2) * 2048
                    em.wait("tensor", "dW", w2_ready[idx])
                    for k in range(16):
                        kk = hh * 16 + k
                        em.op("tensor", lambda e, so=so, k=k, kk=kk, bank=bank:
                              e.matmul(
                            bank[:, 0:TB],
                            work[:, so + k * 128:so + (k + 1) * 128],
                            work[:, H_HB + kk * TB:H_HB + kk * TB + TB],
                            start=(kk == 0), stop=(kk == 31)),
                            inc=("sT" if k == 15 else None))
                    w2mm.append(em.v("sT"))
                    if idx + 2 < 16:
                        dma_w2(idx + 2)
                mmv = em.cnt["sT"]
                em.wait("vector", "sT", mmv)
                em.op("vector", lambda e, oc=oc, bank=bank:
                      e.scalar_tensor_tensor(
                    work[:, H_TOUT + oc * TB:H_TOUT + (oc + 1) * TB],
                    bank[:, 0:TB], bpc("b2", oc),
                    work[:, H_X2P + oc * TB:H_X2P + (oc + 1) * TB],
                    op0=OP.add, op1=OP.add), inc="sV")
                toutv.append(em.v("sV"))
                scan_pump(6)
            em.wait("sync", "sT")
            em.op("sync", lambda e, t2=t2: e.dma_start(
                out=work[:, H_GT:H_GT + 2048].rearrange(
                    "p (ch t) -> p ch t", t=TB),
                in_=gate_scr[:, t2 * TB:(t2 + 1) * TB].rearrange(
                    "(ch p) t -> p ch t", p=128)), inc="dW", n=16)
            em.op("sync", lambda e: e.dma_start(
                out=work[0:16, H_SQF + 512:H_SQF + 1536], in_=Cmat[:, :]),
                inc="dW", n=16)
            gt_ld = em.v("dW")
            need = (4 * t2 + 4) * 128
            if scan_state["next"] < need:
                _scan_emit(need - scan_state["next"])
            em.wait("tensor", "zS", 2 * need)
            em.wait("tensor", "dW", gt_ld)
            em.wait("tensor", "sV")
            mbv = []
            for oc in range(NCH):
                bE, bO = pb[7], pb[4]
                if oc >= 1:
                    em.wait("tensor", "sV", mbv[-1])
                for par, bank in ((0, bE), (1, bO)):
                    for blk in range(2):
                        g = 2 * (2 * t2 + blk) + par
                        em.op("tensor", lambda e, oc=oc, g=g, blk=blk,
                              bank=bank: e.matmul(
                            bank[:, blk * 128:(blk + 1) * 128],
                            work[0:16, H_SQF + 512 + oc * 128:
                                 H_SQF + 512 + (oc + 1) * 128],
                            hall[:, 1 + g * 128:1 + (g + 1) * 128],
                            start=(blk == 0), stop=(blk == 1)),
                            inc=("sT" if (par == 1 and blk == 1) else None))
                mmv = em.v("sT")
                em.wait("vector", "sT", mmv)
                em.op("vector", lambda e, oc=oc: e.tensor_scalar_mul(
                    work[:, H_MB + oc * TB:H_MB + (oc + 1) * TB],
                    pb[7][:, 0:TB], bpc("selE")), inc="sV")
                em.op("vector", lambda e, oc=oc: e.scalar_tensor_tensor(
                    work[:, H_MB + oc * TB:H_MB + (oc + 1) * TB],
                    pb[4][:, 0:TB], bpc("selO"),
                    work[:, H_MB + oc * TB:H_MB + (oc + 1) * TB],
                    op0=OP.mult, op1=OP.add), inc="sV")
                em.op("vector", lambda e, oc=oc: e.tensor_mul(
                    work[:, H_MB + oc * TB:H_MB + (oc + 1) * TB],
                    work[:, H_MB + oc * TB:H_MB + (oc + 1) * TB],
                    work[:, H_GT + oc * TB:H_GT + (oc + 1) * TB]), inc="sV")
                em.op("vector", lambda e, oc=oc, t2=t2: e.tensor_add(
                    work[:, H_MB + oc * TB:H_MB + (oc + 1) * TB],
                    work[:, H_MB + oc * TB:H_MB + (oc + 1) * TB],
                    strm[:, oc * T + t2 * TB:oc * T + (t2 + 1) * TB]),
                    inc="sV")
                mbv.append(em.v("sV"))
            MB_DONE = em.v("sV")

            wf_ready, wfmm, fpv = [], [], []

            def dma_wf(oc):
                so = H_WFS + (oc % 2) * 2048
                if oc >= 2:
                    em.wait("sync", "sT", wfmm[oc - 2])
                em.op("sync", lambda e, so=so, oc=oc: e.dma_start(
                    out=work[:, so:so + 2048].rearrange(
                        "p (k c) -> p k c", c=128),
                    in_=Wf[:, oc * 128:(oc + 1) * 128].rearrange(
                        "(k p) c -> p k c", p=128)), inc="dW", n=16)
                wf_ready.append(em.v("dW"))

            em.wait("sync", "sT")
            dma_wf(0); dma_wf(1)
            em.wait("tensor", "sV", MB_DONE)
            for oc in range(NCH):
                so = H_WFS + (oc % 2) * 2048
                bank = pb[oc % 2]
                em.wait("tensor", "dW", wf_ready[oc])
                if oc >= 2:
                    em.wait("tensor", "sV", fpv[oc - 2])
                for k in range(16):
                    src = H_MB if k < 8 else H_TOUT
                    kk = k if k < 8 else k - 8
                    em.op("tensor", lambda e, so=so, k=k, kk=kk, src=src,
                          bank=bank: e.matmul(
                        bank[:, 0:TB],
                        work[:, so + k * 128:so + (k + 1) * 128],
                        work[:, src + kk * TB:src + kk * TB + TB],
                        start=(k == 0), stop=(k == 15)),
                        inc=("sT" if k == 15 else None))
                mmv = em.v("sT")
                wfmm.append(mmv)
                if oc + 2 < NCH:
                    dma_wf(oc + 2)
                em.wait("vector", "sT", mmv)
                em.op("vector", lambda e, oc=oc, bank=bank: e.tensor_scalar_add(
                    work[:, H_FP + oc * TB:H_FP + (oc + 1) * TB],
                    bank[:, 0:TB], bpc("bf", oc)), inc="sV")
                fpv.append(em.v("sV"))
            ln_stats(work, H_FP, TB, TB, ST_MUT, ST_RT, H_SQF)
            ln_apply(work, H_FP, TB, TB, ST_MUT, ST_RT, work, H_FP, TB,
                     gam="lnfg", bet="lnfb")
            FUSED_DONE = em.v("sV")
            em.wait("sync", "sV", FUSED_DONE)
            em.op("sync", lambda e, t2=t2: e.dma_start(
                out=work[:, H_XN2:H_XN2 + 2048].rearrange(
                    "p (ch t) -> p ch t", t=TB),
                in_=x2_scr[:, t2 * TB:(t2 + 1) * TB].rearrange(
                    "(ch p) t -> p ch t", p=128)), inc="dW", n=16)
            x2_ld = em.v("dW")
            em.wait("vector", "dW", x2_ld)
            for oc in range(NCH):
                em.op("vector", lambda e, oc=oc, t2=t2: e.tensor_add(
                    strm[:, oc * T + t2 * TB:oc * T + (t2 + 1) * TB],
                    work[:, H_FP + oc * TB:H_FP + (oc + 1) * TB],
                    work[:, H_XN2 + oc * TB:H_XN2 + (oc + 1) * TB]), inc="sV")
            bar()

        tap_stream("x3")
        if STOP_AFTER == "s3":
            finish()
            return nc

        ffn_phase()
        finish()
    return nc


def _pack_col(vec):
    return np.ascontiguousarray(vec.reshape(-1, 128).T.astype(np.float32))


def kernel(L0_attn_a_x, L0_attn_a_ln_gamma, L0_attn_a_ln_beta, L0_attn_a_w_qkv,
           L0_attn_a_w_out, L0_attn_a_b_out, L0_attn_b_ln_g, L0_attn_b_ln_b,
           L0_attn_b_Wdelta, L0_attn_b_bdelta, L0_attn_b_Win, L0_attn_b_b_in,
           L0_attn_b_Wgate, L0_attn_b_bgate, L0_attn_b_A, L0_attn_b_C,
           L0_attn_b_ln1_g, L0_attn_b_ln1_b, L0_attn_b_Wq, L0_attn_b_bq,
           L0_attn_b_Wk, L0_attn_b_bk, L0_attn_b_Wv, L0_attn_b_bv,
           L0_attn_b_Wo, L0_attn_b_bo, L0_attn_b_ln2_g, L0_attn_b_ln2_b,
           L0_attn_b_W1, L0_attn_b_b1, L0_attn_b_W2, L0_attn_b_b2,
           L0_attn_b_Wf, L0_attn_b_bf, L0_attn_b_lnf_g, L0_attn_b_lnf_b,
           L0_ffn_w_gate, L0_ffn_w_up, L0_ffn_w_down):
    x = np.asarray(L0_attn_a_x)
    w_qkv = np.asarray(L0_attn_a_w_qkv)
    wqT = np.ascontiguousarray(w_qkv[0:1024].T)
    wkT = np.ascontiguousarray(w_qkv[1024:2048].T)
    wvT = np.ascontiguousarray(w_qkv[2048:3072].T)
    woT = np.ascontiguousarray(np.asarray(L0_attn_a_w_out).T)
    wgT = np.ascontiguousarray(np.asarray(L0_ffn_w_gate).T)
    wuT = np.ascontiguousarray(np.asarray(L0_ffn_w_up).T)
    wdT = np.ascontiguousarray(np.asarray(L0_ffn_w_down).T)

    shared = {
        "wqT": wqT, "wkT": wkT, "wvT": wvT, "woT": woT,
        "wgT": wgT, "wuT": wuT, "wdT": wdT,
        "Wdl": np.ascontiguousarray(L0_attn_b_Wdelta),
        "Win": np.ascontiguousarray(L0_attn_b_Win),
        "Wgt": np.ascontiguousarray(L0_attn_b_Wgate),
        "Amat": np.ascontiguousarray(L0_attn_b_A),
        "Cmat": np.ascontiguousarray(L0_attn_b_C),
        "Wq": np.ascontiguousarray(L0_attn_b_Wq),
        "Wk": np.ascontiguousarray(L0_attn_b_Wk),
        "Wv": np.ascontiguousarray(L0_attn_b_Wv),
        "Wo": np.ascontiguousarray(L0_attn_b_Wo),
        "W1": np.ascontiguousarray(L0_attn_b_W1),
        "W2": np.ascontiguousarray(L0_attn_b_W2),
        "Wf": np.ascontiguousarray(L0_attn_b_Wf),
        "rpv": np.ascontiguousarray(np.asarray(L0_attn_b_bv)[None, :]),
        "identv": np.eye(128, dtype=np.float32),
        "onesv": np.ones((128, 128), np.float32),
    }

    diag = np.where(np.arange(128)[:, None] > np.arange(128)[None, :],
                    np.float32(NEG), np.float32(0.0)).astype(np.float32)
    full = np.full((128, 128), np.float32(NEG), np.float32)
    zero = np.zeros((128, 128), np.float32)

    in_maps = []
    for c in range(8):
        b, p = c // 2, c % 2
        bp_arr = np.zeros((128, NBP), np.float32)

        def put(name, vec):
            v = np.asarray(vec).astype(np.float32)
            n = v.size // 128
            bp_arr[:, BP[name]:BP[name] + n] = v.reshape(n, 128).T

        put("bout_a", L0_attn_a_b_out)
        put("gam_a", L0_attn_a_ln_gamma)
        put("bet_a", L0_attn_a_ln_beta)
        put("lng", L0_attn_b_ln_g); put("lnb", L0_attn_b_ln_b)
        put("ln1g", L0_attn_b_ln1_g); put("ln1b", L0_attn_b_ln1_b)
        put("ln2g", L0_attn_b_ln2_g); put("ln2b", L0_attn_b_ln2_b)
        put("lnfg", L0_attn_b_lnf_g); put("lnfb", L0_attn_b_lnf_b)
        put("bgate", L0_attn_b_bgate)
        put("bq", L0_attn_b_bq); put("bk", L0_attn_b_bk)
        put("bo", L0_attn_b_bo)
        put("b1", L0_attn_b_b1); put("b2", L0_attn_b_b2)
        put("bf", L0_attn_b_bf)
        bp_arr[0:16, BP["bdelta"]] = np.asarray(L0_attn_b_bdelta)
        bp_arr[0:16, BP["b_in"]] = np.asarray(L0_attn_b_b_in)
        bp_arr[:, BP["selE"]] = 1.0 if p == 0 else 0.0
        bp_arr[:, BP["selO"]] = 1.0 if p == 1 else 0.0
        bp_arr[:, BP["c_eps"]] = EPS
        bp_arr[:, BP["c_nsha"]] = -SHIFT_A
        bp_arr[:, BP["c_nshb"]] = -SHIFT_B
        bp_arr[:, BP["c_inv127"]] = INV127
        bp_arr[:, BP["c_i1024"]] = 1.0 / 1024.0
        bp_arr[:, BP["c_neg1"]] = -1.0

        xb = x[b].reshape(32, 128, D)[p::2].reshape(T, D)
        m = dict(shared)
        m["x0"] = np.ascontiguousarray(xb.T)
        m["bpv"] = bp_arr
        m["m0v"] = diag if p == 0 else zero
        m["m1v"] = full if p == 0 else diag
        in_maps.append(m)

    nc = build_nc()
    res = run_bass_kernel_spmd(nc, in_maps, core_ids=list(range(8)))

    out_full = np.empty((B, S, D), np.float32)
    for c in range(8):
        b, p = c // 2, c % 2
        sh = res.results[c]["out"]
        out_full[b].reshape(32, 128, D)[p::2] = sh.T.reshape(16, 128, D)
    if TAPS:
        kernel.tap_results = []
        for c in range(8):
            kernel.tap_results.append(
                {t: res.results[c]["tap_" + t] for t in TAPS})
    return out_full


# version 09 (final)
# speedup vs baseline: 1.1273x; 1.0042x over previous
import contextlib
import numpy as np
import concourse.bass as bass
import concourse.mybir as mybir
from concourse.bass_utils import run_bass_kernel_spmd

F32 = mybir.dt.float32
I32 = mybir.dt.int32
AF = mybir.ActivationFunctionType
OP = mybir.AluOpType

B, S, D = 4, 4096, 1024
T = 2048
NCH = 8
TT = 512
NTT = T // TT
EPS = 1e-5
SHIFT_A = 24.0
SHIFT_B = 10.0
QSC_A = 0.125
QSC_B = float(1.0 / np.sqrt(np.float32(128.0)))
INV127 = float(np.float32(1.0) / np.float32(127.0))
PAIR_RG = [[0, 1], [2, 3], [4, 5], [6, 7]]
NEG = -1.0e30
NSCAN = S

SCAN_INTERLEAVE = True
DBG_BARRIER = False
DBG_ABAR = False
DBG_NONORM = False
TAPS = []
STOP_AFTER = None

BP = {}
_c = 0
for _name, _n in [("bout_a", 8), ("gam_a", 8), ("bet_a", 8), ("lng", 8),
                  ("lnb", 8), ("ln1g", 8), ("ln1b", 8), ("ln2g", 8),
                  ("ln2b", 8), ("lnfg", 8), ("lnfb", 8), ("bgate", 8),
                  ("bq", 8), ("bk", 8), ("bo", 8), ("b1", 32), ("b2", 8),
                  ("bf", 8), ("bdelta", 1), ("b_in", 1), ("selE", 1),
                  ("selO", 1), ("c_eps", 1), ("c_nsha", 1), ("c_nshb", 1),
                  ("c_inv127", 1), ("c_i1024", 1), ("c_neg1", 1)]:
    BP[_name] = _c
    _c += _n
NBP = _c

ENGS = ["sync", "tensor", "vector", "scalar", "gpsimd"]


class Em:
    def __init__(self, nc, stack):
        self.nc = nc
        self.q = {e: [] for e in ENGS}
        self.sem = {}
        self.cnt = {}
        for s in ["dW", "dG", "sT", "sV", "sS", "sC", "zS", "zT", "st"]:
            self.sem[s] = stack.enter_context(nc.semaphore(s))
            self.cnt[s] = 0

    def v(self, s):
        return self.cnt[s]

    def wait(self, eng, s, val=None):
        v = self.cnt[s] if val is None else val
        if v <= 0:
            return
        h = self.sem[s]
        self.q[eng].append(lambda e: e.wait_ge(h, v))

    def op(self, eng, builder, inc=None, n=1):
        if inc is None:
            self.q[eng].append(lambda e: builder(e))
            return None
        h = self.sem[inc]
        self.q[eng].append(lambda e: builder(e).then_inc(h, n))
        self.cnt[inc] += n
        return self.cnt[inc]

    def flush(self, block):
        for name in ENGS:
            ops = self.q[name]
            if not ops:
                continue

            def runner(e, ops=ops):
                for f in ops:
                    f(e)
            getattr(block, name)(runner)


def build_nc():
    nc = bass.Bass("TRN2", target_bir_lowering=False,
                   detect_race_conditions=False)
    di = lambda n, s, dt=F32: nc.dram_tensor(n, s, dt, kind="ExternalInput").ap()
    do = lambda n, s, dt=F32: nc.dram_tensor(n, s, dt, kind="ExternalOutput").ap()
    dd = lambda n, s, dt=F32: nc.dram_tensor(n, s, dt).ap()

    x0 = di("x0", [D, T])
    wqT = di("wqT", [D, D]); wkT = di("wkT", [D, D]); wvT = di("wvT", [D, D])
    woT = di("woT", [D, D])
    wgT = di("wgT", [D, 3072], I32); wuT = di("wuT", [D, 3072], I32)
    wdT = di("wdT", [3072, D], I32)
    Wdl = di("Wdl", [D, 16]); Win = di("Win", [D, 16])
    Wgt = di("Wgt", [D, D])
    Amat = di("Amat", [16, 16]); Cmat = di("Cmat", [16, D])
    Wq = di("Wq", [D, D]); Wk = di("Wk", [D, D]); Wv = di("Wv", [D, D])
    Wo = di("Wo", [D, D])
    W1 = di("W1", [D, 4096]); W2 = di("W2", [4096, D]); Wf = di("Wf", [2 * D, D])
    bp_in = di("bpv", [128, NBP])
    rp_in = di("rpv", [1, D])
    ident_in = di("identv", [128, 128])
    m0_in = di("m0v", [128, 128])
    m1_in = di("m1v", [128, 128])
    ones_in = di("onesv", [128, 128])

    out = do("out", [D, T])
    _tap_shapes = {"q_scr": [D, T], "ka_i": [D, T], "va_i": [T, 1040],
                   "ctxa": [T, D], "qb_scr": [D, T], "kb_i": [D, T],
                   "vb_i": [T, 1032], "ctxb": [T, D], "gate": [D, T],
                   "m_i": [48, T], "x2p": [D, T], "hall": [16, S + 1],
                   "kdbg": [128, 4096], "vdbg": [128, 4160]}
    taps = {t: do("tap_" + t, _tap_shapes.get(t, [D, T])) for t in TAPS}

    q_scr = dd("q_scr", [D, T])
    qb_scr = dd("qb_scr", [D, T])
    gate_scr = dd("gate_scr", [D, T])
    x2_scr = dd("x2_scr", [D, T])
    x2p_scr = dd("x2p_scr", [D, T])
    ctxa_scr = dd("ctxa_scr", [T, D])
    ctxb_scr = dd("ctxb_scr", [T, D])
    ka_i = dd("ka_i", [D, T]); ka_o = dd("ka_o", [2, D, T])
    va_i = dd("va_i", [T, 1040]); va_o = dd("va_o", [2, T, 1040])
    m_i = dd("m_i", [48, T]); m_o = dd("m_o", [2, 48, T])
    kb_i = dd("kb_i", [D, T]); kb_o = dd("kb_o", [2, D, T])
    vb_i = dd("vb_i", [T, 1032]); vb_o = dd("vb_o", [2, T, 1032])

    stack = contextlib.ExitStack()
    with stack:
        work = stack.enter_context(nc.sbuf_tensor("work", [128, 21536], F32))
        strm = stack.enter_context(nc.sbuf_tensor("strm", [128, NCH * T], F32))
        bp = stack.enter_context(nc.sbuf_tensor("bps", [128, NBP], F32))
        ident = stack.enter_context(nc.sbuf_tensor("identt", [128, 128], F32))
        m0 = stack.enter_context(nc.sbuf_tensor("m0t", [128, 128], F32))
        m1 = stack.enter_context(nc.sbuf_tensor("m1t", [128, 128], F32))
        ones = stack.enter_context(nc.sbuf_tensor("ones_t", [128, 128], F32))
        strow = stack.enter_context(nc.sbuf_tensor("strow", [128, 2048], F32))
        A_sb = stack.enter_context(nc.sbuf_tensor("A_sb", [16, 16], F32))
        dfull = stack.enter_context(nc.sbuf_tensor("dfull", [16, S], F32))
        btfull = stack.enter_context(nc.sbuf_tensor("btfull", [16, S], F32))
        hall = stack.enter_context(nc.sbuf_tensor("hallt", [16, S + 1], F32))
        hd = stack.enter_context(nc.sbuf_tensor("hdt", [16, 1], F32))

        pb = [stack.enter_context(nc.psum_tensor(f"pb{k}", [128, 512], F32))
              for k in range(8)]

        for cname, cval in [("c_eps", EPS), ("c_nsha", -SHIFT_A),
                            ("c_nshb", -SHIFT_B), ("c_inv127", INV127),
                            ("c_i1024", 1.0 / 1024.0), ("c_neg1", -1.0)]:
            nc.const_aps.register(F32, cval, bp[:, BP[cname]:BP[cname] + 1])

        em = Em(nc, stack)
        block = stack.enter_context(nc.Block())

        ST_MU2X, ST_R2X = (0, 0), (32, 0)
        ST_MUT, ST_RT = (64, 0), (64, 512)
        ST_BV = (64, 1024)
        ST_A, ST_B = (64, 1024), (64, 1536)

        def srow(slot, off, n):
            r, c = slot
            return strow[r:r + 1, c + off:c + off + n]

        def sl(slot, off):
            return (slot[0], slot[1] + off)

        def bpc(name, k=0):
            return bp[:, BP[name] + k:BP[name] + k + 1]

        def bar():
            for _e in ENGS:
                for _s in ["sT", "sV", "sS", "dW", "dG"]:
                    em.wait(_e, _s)

        scan_state = {"next": 0, "ready": False}

        def scan_pump(nsteps):
            if not scan_state["ready"] or not SCAN_INTERLEAVE:
                return
            _scan_emit(nsteps)

        def _scan_emit(nsteps):
            t0 = scan_state["next"]
            t1 = min(t0 + nsteps, NSCAN)
            for t in range(t0, t1):
                em.op("scalar", lambda e, t=t: e.activation(
                    out=hd[:, :], in_=hall[:, t:t + 1], func=AF.Identity,
                    scale=dfull[:, t:t + 1]), inc="zS")
                em.wait("tensor", "zS", 2 * t + 1)
                em.op("tensor", lambda e: e.matmul(
                    pb[5][0:16, 0:1], A_sb[:, :], hd[:, :], start=True,
                    stop=True), inc="zT")
                em.wait("scalar", "zT", t + 1)
                em.op("scalar", lambda e, t=t: e.activation(
                    out=hall[:, t + 1:t + 2], in_=pb[5][0:16, 0:1],
                    func=AF.Tanh, bias=btfull[:, t:t + 1]), inc="zS")
            scan_state["next"] = t1

        def ln_stats(sbuf, soff, sstride, n, mu_slot, r_slot, sq_off,
                     bank_s=6, bank_q=7):
            em.wait("tensor", "sS")
            em.wait("tensor", "sV")
            em.wait("scalar", "sT")
            em.wait("scalar", "sV")
            em.wait("scalar", "dG")
            for k in range(NCH):
                cw = soff + k * sstride
                last = (k == NCH - 1)
                em.op("tensor", lambda e, cw=cw, k=k, last=last: e.matmul(
                    pb[bank_s][0:1, 0:n], ones[:, 0:1], sbuf[:, cw:cw + n],
                    start=(k == 0), stop=last), inc=("sT" if last else None))
            mm_sum = em.v("sT")
            sq_mm = []
            for k in range(NCH):
                cw = soff + k * sstride
                po = sq_off + (k % 2) * n
                if k >= 2:
                    em.wait("scalar", "sT", sq_mm[k - 2])
                em.op("scalar", lambda e, cw=cw, po=po: e.activation(
                    out=work[:, po:po + n], in_=sbuf[:, cw:cw + n],
                    func=AF.Square), inc="sS")
                em.wait("tensor", "sS")
                last = (k == NCH - 1)
                em.op("tensor", lambda e, po=po, k=k, last=last: e.matmul(
                    pb[bank_q][0:1, 0:n], ones[:, 0:1], work[:, po:po + n],
                    start=(k == 0), stop=last), inc="sT")
                sq_mm.append(em.v("sT"))
            mm_sq = em.v("sT")
            em.wait("scalar", "sT", mm_sum)
            em.op("scalar", lambda e: e.mul(
                out=srow(mu_slot, 0, n), in_=pb[bank_s][0:1, 0:n],
                mul=1.0 / 1024.0), inc="sS")
            em.wait("scalar", "sT", mm_sq)
            em.op("scalar", lambda e: e.mul(
                out=srow(ST_A, 0, n), in_=pb[bank_q][0:1, 0:n],
                mul=1.0 / 1024.0), inc="sS")
            em.op("scalar", lambda e: e.activation(
                out=srow(ST_B, 0, n),
                in_=srow(mu_slot, 0, n), func=AF.Square), inc="sS")
            em.wait("vector", "sS")
            em.op("vector", lambda e: e.tensor_sub(
                srow(ST_A, 0, n), srow(ST_A, 0, n), srow(ST_B, 0, n)),
                inc="sV")
            em.wait("scalar", "sV")
            em.op("scalar", lambda e: e.activation(
                out=srow(ST_B, 0, n), in_=srow(ST_A, 0, n),
                func=AF.Sqrt, bias=EPS), inc="sS")
            em.wait("vector", "sS")
            em.op("vector", lambda e: e.reciprocal(
                srow(r_slot, 0, n), srow(ST_B, 0, n)), inc="sV")

        def ln_apply(sbuf, soff, sstride, n, mu_slot, r_slot,
                     dbuf, doff, dstride, gam=None, bet=None,
                     bank_mu=4, bank_r=6):
            em.wait("tensor", "sS")
            em.wait("tensor", "sV")
            rmu, rr = mu_slot[0], r_slot[0]
            em.op("tensor", lambda e: e.matmul(
                pb[bank_mu][:, 0:n], ones[rmu:rmu + 1, :],
                srow(mu_slot, 0, n), start=True, stop=True), inc="sT")
            em.op("tensor", lambda e: e.matmul(
                pb[bank_r][:, 0:n], ones[rr:rr + 1, :],
                srow(r_slot, 0, n), start=True, stop=True), inc="sT")
            bc = em.v("sT")
            em.wait("vector", "sT", bc)
            for k in range(NCH):
                cw = soff + k * sstride
                dw = doff + k * dstride
                em.op("vector", lambda e, cw=cw, dw=dw: e.tensor_sub(
                    dbuf[:, dw:dw + n], sbuf[:, cw:cw + n],
                    pb[bank_mu][:, 0:n]), inc="sV")
                em.op("vector", lambda e, dw=dw: e.tensor_mul(
                    dbuf[:, dw:dw + n], dbuf[:, dw:dw + n], pb[bank_r][:, 0:n]),
                    inc="sV")
                if gam is not None:
                    em.op("vector", lambda e, dw=dw, k=k: e.tensor_scalar(
                        dbuf[:, dw:dw + n], dbuf[:, dw:dw + n], bpc(gam, k),
                        bpc(bet, k), op0=OP.mult, op1=OP.add), inc="sV")

        def proj(wsrc, n_out, xn_off, n, slab_off, kch, evac, cast=False,
                 scan_steps=0, xn_stride=None):
            xst = xn_stride if xn_stride is not None else n
            slab_cols = kch * 128
            queue = "gpsimd" if cast else "sync"
            dsem = "dG" if cast else "dW"
            slab_ready, mm_done, evac_done = [], [], []

            def dma_slab(oc):
                so = slab_off + (oc % 2) * slab_cols
                if oc >= 2:
                    em.wait(queue, "sT", mm_done[oc - 2])
                em.op(queue, lambda e, so=so, oc=oc: e.dma_start(
                    out=work[:, so:so + slab_cols].rearrange(
                        "p (k c) -> p k c", c=128),
                    in_=wsrc[:, oc * 128:(oc + 1) * 128].rearrange(
                        "(k p) c -> p k c", p=128)), inc=dsem, n=16)
                slab_ready.append(em.v(dsem))

            em.wait(queue, "sT")
            em.wait("tensor", "sV")
            em.wait("tensor", "sS")
            dma_slab(0)
            if n_out > 1:
                dma_slab(1)
            for oc in range(n_out):
                so = slab_off + (oc % 2) * slab_cols
                bank = pb[oc % 2]
                em.wait("tensor", dsem, slab_ready[oc])
                if oc >= 2:
                    sem_n, sem_v = evac_done[oc - 2]
                    em.wait("tensor", sem_n, sem_v)
                for k in range(kch):
                    last = (k == kch - 1)
                    em.op("tensor", lambda e, so=so, k=k, last=last, bank=bank:
                          e.matmul(
                        bank[:, 0:n],
                        work[:, so + k * 128:so + (k + 1) * 128],
                        work[:, xn_off + k * xst:xn_off + k * xst + n],
                        start=(k == 0), stop=last),
                        inc=("sT" if last else None))
                mm_done.append(em.v("sT"))
                if oc + 2 < n_out:
                    dma_slab(oc + 2)
                evac_done.append(evac(oc, bank, mm_done[oc]))
                if scan_steps:
                    scan_pump(scan_steps)

        for dst, src in [(bp, bp_in), (ident, ident_in), (m0, m0_in),
                         (m1, m1_in), (ones, ones_in), (A_sb, Amat)]:
            em.op("sync", lambda e, dst=dst, src=src: e.dma_start(
                out=dst[:, :], in_=src[:, :]), inc="dW", n=16)
        LD_CONST = em.v("dW")
        em.op("gpsimd", lambda e: e.dma_start(
            out=strm[:, :].rearrange("p (ch t) -> p ch t", t=T),
            in_=x0[:, :].rearrange("(ch p) t -> p ch t", p=128)), inc="dG", n=16)
        LD_X0 = em.v("dG")
        for eng in ["tensor", "vector", "scalar"]:
            em.wait(eng, "dW", LD_CONST)
            em.wait(eng, "dG", LD_X0)

        def tap_stream(name):
            if name in TAPS:
                em.wait("gpsimd", "sV")
                em.op("gpsimd", lambda e, name=name: e.dma_start(
                    out=taps[name][:, :].rearrange("(ch p) t -> p ch t", p=128),
                    in_=strm[:, :].rearrange("p (ch t) -> p ch t", t=T)),
                    inc="dG", n=16)

        def finish():
            _scr = {"q_scr": q_scr, "ka_i": ka_i, "va_i": va_i,
                    "ctxa": ctxa_scr, "qb_scr": qb_scr, "kb_i": kb_i,
                    "vb_i": vb_i, "ctxb": ctxb_scr, "gate": gate_scr,
                    "m_i": m_i, "x2p": x2p_scr}
            em.wait("gpsimd", "sV")
            em.wait("gpsimd", "sS")
            em.wait("gpsimd", "dW")
            for tname in TAPS:
                if tname in _scr:
                    em.op("gpsimd", lambda e, tname=tname: e.dma_start(
                        out=taps[tname][:, :], in_=_scr[tname][:, :]),
                        inc="dG", n=16)
                elif tname == "hall":
                    em.op("gpsimd", lambda e: e.dma_start(
                        out=taps["hall"][:, :], in_=hall[:, :]), inc="dG", n=16)
            em.op("gpsimd", lambda e: e.dma_start(
                out=out[:, :].rearrange("(ch p) t -> p ch t", p=128),
                in_=strm[:, :].rearrange("p (ch t) -> p ch t", t=T)),
                inc="st", n=16)
            em.wait("gpsimd", "st")
            em.flush(block)

        W_N0, W_XN, W_SLAB = 0, 4096, 8192
        W_QTMP, W_SQ = 10240, 11264
        W_VSLAB, W_VTMP = 11264, 19456

        for tt in range(NTT):
            tw = tt * TT
            ln_stats(strm, tw, T, TT, ST_MUT, ST_RT, W_SQ)
            ln_apply(strm, tw, T, TT, ST_MUT, ST_RT, work, W_N0, TT)
            ln_stats(work, W_N0, TT, TT, ST_MUT, ST_RT, W_SQ)
            ln_apply(work, W_N0, TT, TT, ST_MUT, ST_RT, work, W_XN, TT,
                     gam="gam_a", bet="bet_a")
            if "xn_dbg" in TAPS:
                em.wait("gpsimd", "sV")
                em.op("gpsimd", lambda e, tt=tt: e.dma_start(
                    out=taps["xn_dbg"][:, tt * TT:(tt + 1) * TT].rearrange(
                        "(ch p) t -> p ch t", p=128),
                    in_=work[:, W_XN:W_XN + 4096].rearrange(
                        "p (ch t) -> p ch t", t=TT)), inc="dG", n=16)
                em.wait("tensor", "dG")
            em.wait("tensor", "sV")

            def make_ev(scr, scale=None, copy=False, tt=tt):
                dmas = []

                def ev(oc, bank, mmv, tt=tt):
                    po = W_QTMP + (oc % 2) * TT
                    em.wait("vector", "sT", mmv)
                    if len(dmas) >= 2:
                        em.wait("vector", "dG", dmas[-2])
                    else:
                        em.wait("vector", "dG")
                    if copy:
                        em.op("vector", lambda e, po=po, bank=bank: e.tensor_copy(
                            work[:, po:po + TT], bank[:, 0:TT]), inc="sV")
                    else:
                        em.op("vector", lambda e, po=po, bank=bank:
                              e.tensor_scalar_mul(
                            work[:, po:po + TT], bank[:, 0:TT], scale), inc="sV")
                    sv = em.v("sV")
                    em.wait("gpsimd", "sV", sv)
                    em.op("gpsimd", lambda e, po=po, oc=oc, tt=tt: e.dma_start(
                        out=scr[oc * 128:(oc + 1) * 128, tt * TT:(tt + 1) * TT],
                        in_=work[:, po:po + TT]), inc="dG", n=16)
                    dmas.append(em.v("dG"))
                    return ("sV", sv)
                return ev

            proj(wqT, NCH, W_XN, TT, W_SLAB, NCH, make_ev(q_scr, scale=QSC_A))
            if DBG_BARRIER:
                for _e in ["tensor", "vector", "scalar", "sync", "gpsimd"]:
                    for _s in ["sT", "sV", "sS", "dW", "dG"]:
                        em.wait(_e, _s)
            if STOP_AFTER == "qonly":
                continue
            proj(wkT, NCH, W_XN, TT, W_SLAB, NCH, make_ev(ka_i, copy=True))

            vs_ready = []
            for dvg in range(2):
                so = W_VSLAB + dvg * 4096
                em.wait("sync", "sT")
                em.op("sync", lambda e, so=so, dvg=dvg: e.dma_start(
                    out=work[:, so:so + 4096].rearrange(
                        "p (k c) -> p k c", c=512),
                    in_=wvT[:, dvg * 512:(dvg + 1) * 512].rearrange(
                        "(k p) c -> p k c", p=128)), inc="dW", n=16)
                vs_ready.append(em.v("dW"))
            vdma = []
            vev = {}
            em.wait("tensor", "sV")
            for sub in range(4):
                vo = W_VTMP + (sub % 2) * 1040
                if sub >= 2:
                    em.wait("vector", "dG", vdma[sub - 2])
                else:
                    em.wait("vector", "dG")
                for dvg in range(2):
                    so = W_VSLAB + dvg * 4096
                    bank = pb[2 + dvg]
                    em.wait("tensor", "dW", vs_ready[dvg])
                    if (sub - 1, dvg) in vev:
                        em.wait("tensor", "sV", vev[(sub - 1, dvg)])
                    for k in range(NCH):
                        em.op("tensor", lambda e, so=so, k=k, sub=sub, bank=bank:
                              e.matmul(
                            bank[:, 0:512],
                            work[:, W_XN + k * TT + sub * 128:
                                  W_XN + k * TT + sub * 128 + 128],
                            work[:, so + k * 512:so + (k + 1) * 512],
                            start=(k == 0), stop=(k == NCH - 1)),
                            inc=("sT" if k == NCH - 1 else None))
                    mmv = em.v("sT")
                    em.wait("vector", "sT", mmv)
                    em.op("vector", lambda e, vo=vo, dvg=dvg, bank=bank:
                          e.tensor_copy(
                        work[:, vo + dvg * 520:vo + dvg * 520 + 520].rearrange(
                            "p (h c) -> p h c", c=65)[:, :, 0:64],
                        bank[:, 0:512].rearrange("p (h c) -> p h c", c=64)),
                        inc="sV")
                    vev[(sub, dvg)] = em.v("sV")
                em.op("vector", lambda e, vo=vo: e.memset(
                    work[:, vo:vo + 1040].rearrange("p (h c) -> p h c", c=65)
                    [:, :, 64:65], 1.0), inc="sV")
                sv = em.v("sV")
                em.wait("gpsimd", "sV", sv)
                em.op("gpsimd", lambda e, vo=vo, tt=tt, sub=sub: e.dma_start(
                    out=va_i[tt * TT + sub * 128:tt * TT + sub * 128 + 128, :],
                    in_=work[:, vo:vo + 1040]), inc="dG", n=16)
                vdma.append(em.v("dG"))

        if STOP_AFTER == "qonly":
            finish()
            return nc
        em.wait("gpsimd", "dG")
        em.op("gpsimd", lambda e: e.collective_compute(
            "AllGather", OP.bypass, ins=[ka_i[:, :]], outs=[ka_o[:, :, :]],
            replica_groups=PAIR_RG), inc="sC")
        em.op("gpsimd", lambda e: e.collective_compute(
            "AllGather", OP.bypass, ins=[va_i[:, :]], outs=[va_o[:, :, :]],
            replica_groups=PAIR_RG), inc="sC")
        AG_A = em.v("sC")

        WA_K, WA_V, WA_QB, WA_ES, WA_RCP, WA_CTH = 0, 4096, 8256, 8512, 9024, 9216
        em.wait("sync", "sC", AG_A)
        qdma, qused = [], []
        av_evac = {}
        ctx_dmas = []
        for hg in range(NCH):
            em.wait("sync", "sT")
            for r in range(2):
                em.op("sync", lambda e, r=r, hg=hg: e.dma_start(
                    out=work[:, WA_K:WA_K + 4096].rearrange(
                        "p (g c) -> p g c", c=128)[:, r::2, :],
                    in_=ka_o[r, hg * 128:(hg + 1) * 128, :].rearrange(
                        "p (g c) -> p g c", c=128)), inc="dW", n=16)
                em.op("sync", lambda e, r=r, hg=hg: e.dma_start(
                    out=work[:, WA_V:WA_V + 4160].rearrange(
                        "p (g c) -> p g c", c=130)[:, r::2, :],
                    in_=va_o[r, :, hg * 130:(hg + 1) * 130].rearrange(
                        "(g p) c -> p g c", p=128)), inc="dW", n=16)
            kv_ready = em.v("dW")
            if hg == 0 and "kdbg" in TAPS:
                em.wait("gpsimd", "dW", kv_ready)
                em.op("gpsimd", lambda e: e.dma_start(
                    out=taps["kdbg"][:, :], in_=work[:, WA_K:WA_K + 4096]),
                    inc="dG", n=16)
                em.op("gpsimd", lambda e: e.dma_start(
                    out=taps["vdbg"][:, :], in_=work[:, WA_V:WA_V + 4160]),
                    inc="dG", n=16)
                em.wait("tensor", "dG")
            em.wait("tensor", "dW", kv_ready)
            for i in range(16):
                qo = WA_QB + (i % 2) * 128
                if len(qdma) >= 2:
                    em.wait("sync", "sT", qused[-2])
                em.op("sync", lambda e, qo=qo, hg=hg, i=i: e.dma_start(
                    out=work[:, qo:qo + 128],
                    in_=q_scr[hg * 128:(hg + 1) * 128, i * 128:(i + 1) * 128]),
                    inc="dW", n=16)
                qdma.append(em.v("dW"))
                em.wait("tensor", "dW", qdma[-1])
                em.wait("tensor", "sS")
                avb = pb[4 if i % 2 == 0 else 6]
                if (i % 2) in av_evac:
                    em.wait("tensor", "sV", av_evac[i % 2])
                ng = 2 * i + 2
                exp_done = []
                for g in range(ng):
                    par = g % 2
                    sb0, sb1 = pb[0 + par], pb[2 + par]
                    if g >= 2:
                        em.wait("tensor", "sS", exp_done[g - 2])
                    for h, sb in ((0, sb0), (1, sb1)):
                        em.op("tensor", lambda e, g=g, h=h, sb=sb, qo=qo:
                              e.matmul(
                            sb[:, 0:128],
                            work[:, WA_K + g * 128:WA_K + (g + 1) * 128]
                            [h * 64:(h + 1) * 64, :],
                            work[:, qo:qo + 128][h * 64:(h + 1) * 64, :],
                            start=True, stop=True, tile_position=(h * 64, 0)),
                            inc="sT")
                    smm = em.v("sT")
                    if g >= 2 * i:
                        mt = m0 if g == 2 * i else m1
                        em.wait("vector", "sT", smm)
                        for sb in (sb0, sb1):
                            em.op("vector", lambda e, sb=sb, mt=mt: e.tensor_add(
                                sb[:, 0:128], sb[:, 0:128], mt[:, :]), inc="sV")
                        em.wait("scalar", "sV")
                    em.wait("scalar", "sT", smm)
                    for h, sb in ((0, sb0), (1, sb1)):
                        eo = WA_ES + (par * 2 + h) * 128
                        em.op("scalar", lambda e, sb=sb, eo=eo: e.activation(
                            out=work[:, eo:eo + 128], in_=sb[:, 0:128],
                            func=AF.Exp, bias=-SHIFT_A), inc="sS")
                    exp_done.append(em.v("sS"))
                    em.wait("tensor", "sS", exp_done[-1])
                    for h in range(2):
                        eo = WA_ES + (par * 2 + h) * 128
                        em.op("tensor", lambda e, eo=eo, g=g, h=h, avb=avb,
                              ng=ng: e.matmul(
                            avb[:, h * 65:h * 65 + 65], work[:, eo:eo + 128],
                            work[:, WA_V + g * 130 + h * 65:
                                  WA_V + g * 130 + h * 65 + 65],
                            start=(g == 0 and h == 0),
                            stop=(g == ng - 1 and h == 1)),
                            inc=("sT" if (g == ng - 1 and h == 1) else None))
                av_mm = em.v("sT")
                scan_pump(2)
                em.wait("vector", "sT", av_mm)
                if i == 0 and ctx_dmas:
                    em.wait("vector", "dG", ctx_dmas[-1])
                for h in range(2):
                    if DBG_NONORM:
                        em.op("vector", lambda e, h=h, avb=avb, i=i:
                              e.tensor_copy(
                            work[:, WA_CTH + i * 128 + h * 64:
                                  WA_CTH + i * 128 + h * 64 + 63],
                            avb[:, h * 65:h * 65 + 63]), inc="sV")
                        em.op("vector", lambda e, h=h, avb=avb, i=i:
                              e.tensor_copy(
                            work[:, WA_CTH + i * 128 + h * 64 + 63:
                                  WA_CTH + i * 128 + h * 64 + 64],
                            avb[:, h * 65 + 64:h * 65 + 65]), inc="sV")
                        continue
                    rb = WA_RCP + (i % 2) * 8
                    em.wait("scalar", "sT", av_mm)
                    if (i % 2) in av_evac:
                        em.wait("scalar", "sV", av_evac[i % 2])
                    em.op("scalar", lambda e, h=h, avb=avb, rb=rb: e.activation(
                        out=work[:, rb + 4 + h:rb + 5 + h],
                        in_=avb[:, h * 65 + 64:h * 65 + 65], func=AF.Ln),
                        inc="sS")
                    em.op("scalar", lambda e, h=h, rb=rb: e.activation(
                        out=work[:, rb + h:rb + h + 1],
                        in_=work[:, rb + 4 + h:rb + 5 + h],
                        func=AF.Exp, scale=-1.0), inc="sS")
                    em.wait("vector", "sS")
                    em.op("vector", lambda e, h=h, avb=avb, i=i, rb=rb:
                          e.tensor_scalar_mul(
                        work[:, WA_CTH + i * 128 + h * 64:
                              WA_CTH + i * 128 + h * 64 + 64],
                        avb[:, h * 65:h * 65 + 64],
                        work[:, rb + h:rb + h + 1]), inc="sV")
                av_evac[i % 2] = em.v("sV")
                qused.append(em.v("sT"))
                bar()
            em.wait("gpsimd", "sV")
            em.op("gpsimd", lambda e, hg=hg: e.dma_start(
                out=ctxa_scr[:, hg * 128:(hg + 1) * 128].rearrange(
                    "(i p) c -> p i c", p=128),
                in_=work[:, WA_CTH:WA_CTH + 2048].rearrange(
                    "p (i c) -> p i c", c=128)), inc="dG", n=16)
            ctx_dmas.append(em.v("dG"))
        CTXA_SAVED = em.v("dG")

        WD_WO, WD_CTI, WD_FM = 0, 8192, 10240
        em.wait("sync", "sT")
        for oc in range(NCH):
            em.op("sync", lambda e, oc=oc: e.dma_start(
                out=work[:, WD_WO + oc * 1024:WD_WO + (oc + 1) * 1024].rearrange(
                    "p (k c) -> p k c", c=128),
                in_=woT[:, oc * 128:(oc + 1) * 128].rearrange(
                    "(k p) c -> p k c", p=128)), inc="dW", n=16)
        WO_READY = em.v("dW")
        em.wait("sync", "dG", CTXA_SAVED)
        cti_dma, cti_used = [], []
        em.wait("tensor", "dW", WO_READY)
        for i in range(16):
            co = WD_CTI + (i % 2) * 1024
            if len(cti_dma) >= 2:
                em.wait("sync", "sT", cti_used[-2])
            em.op("sync", lambda e, i=i, co=co: e.dma_start(
                out=work[:, co:co + 1024],
                in_=ctxa_scr[i * 128:(i + 1) * 128, :]), inc="dW", n=16)
            cti_dma.append(em.v("dW"))
            em.wait("tensor", "dW", cti_dma[-1])
            tr_done = []
            for ec in range(NCH):
                bank = pb[ec % 2]
                if ec >= 2:
                    em.wait("tensor", "sV", tr_done[ec - 2])
                em.op("tensor", lambda e, co=co, ec=ec, bank=bank: e.transpose(
                    bank[:, 0:128],
                    work[:, co + ec * 128:co + (ec + 1) * 128], ident[:, :]),
                    inc="sT")
                em.wait("vector", "sT")
                em.op("vector", lambda e, ec=ec, bank=bank: e.tensor_copy(
                    work[:, WD_FM + ec * 128:WD_FM + (ec + 1) * 128],
                    bank[:, 0:128]), inc="sV")
                tr_done.append(em.v("sV"))
            em.wait("tensor", "sV", tr_done[-1])
            opv = []
            for oc in range(NCH):
                bank = pb[2 + oc % 2]
                if oc >= 2:
                    em.wait("tensor", "sV", opv[oc - 2])
                for k in range(NCH):
                    em.op("tensor", lambda e, oc=oc, k=k, bank=bank: e.matmul(
                        bank[:, 0:128],
                        work[:, WD_WO + oc * 1024 + k * 128:
                              WD_WO + oc * 1024 + (k + 1) * 128],
                        work[:, WD_FM + k * 128:WD_FM + (k + 1) * 128],
                        start=(k == 0), stop=(k == NCH - 1)),
                        inc=("sT" if k == NCH - 1 else None))
                mmv = em.v("sT")
                em.wait("vector", "sT", mmv)
                em.op("vector", lambda e, oc=oc, i=i, bank=bank:
                      e.scalar_tensor_tensor(
                    strm[:, oc * T + i * 128:oc * T + (i + 1) * 128],
                    bank[:, 0:128], bpc("bout_a", oc),
                    strm[:, oc * T + i * 128:oc * T + (i + 1) * 128],
                    op0=OP.add, op1=OP.add), inc="sV")
                opv.append(em.v("sV"))
            cti_used.append(em.v("sT"))
            for _e in ["tensor", "vector", "scalar", "sync", "gpsimd"]:
                for _s in ["sT", "sV", "sS", "dW", "dG"]:
                    em.wait(_e, _s)
            scan_pump(2)
        tap_stream("x1")
        if STOP_AFTER == "s1":
            finish()
            return nc

        BF16 = mybir.dt.bfloat16

        def ffn_phase():
            F_XNT, F_XNT2, F_XH, F_XL = 0, 512, 1024, 3072
            F_SQ, F_GS, F_H2T = 5120, 5120, 6144
            F_GSL, F_USL = 6656, 7680
            F_H2H, F_H2L, F_DSL = 8704, 14848, 0
            xhv = work[:, F_XH:F_XH + 2048].bitcast(BF16)
            xlv = work[:, F_XL:F_XL + 2048].bitcast(BF16)
            h2hv = work[:, F_H2H:F_H2H + 6144].bitcast(BF16)
            h2lv = work[:, F_H2L:F_H2L + 6144].bitcast(BF16)
            for tt in range(NTT):
                tw = tt * TT
                ln_stats(strm, tw, T, TT, ST_MUT, ST_RT, F_SQ)
                em.wait("tensor", "sS")
                em.wait("tensor", "sV")
                rmu, rr = ST_MUT[0], ST_RT[0]
                em.op("tensor", lambda e: e.matmul(
                    pb[4][:, 0:TT], ones[rmu:rmu + 1, :], srow(ST_MUT, 0, TT),
                    start=True, stop=True), inc="sT")
                em.op("tensor", lambda e: e.matmul(
                    pb[6][:, 0:TT], ones[rr:rr + 1, :], srow(ST_RT, 0, TT),
                    start=True, stop=True), inc="sT")
                bc = em.v("sT")
                em.wait("vector", "sT", bc)
                for k in range(NCH):
                    cw = tw + k * T
                    em.op("vector", lambda e, cw=cw: e.tensor_sub(
                        work[:, F_XNT:F_XNT + TT], strm[:, cw:cw + TT],
                        pb[4][:, 0:TT]), inc="sV")
                    em.op("vector", lambda e: e.tensor_mul(
                        work[:, F_XNT:F_XNT + TT], work[:, F_XNT:F_XNT + TT],
                        pb[6][:, 0:TT]), inc="sV")
                    em.op("vector", lambda e, k=k: e.tensor_copy(
                        xhv[:, k * TT:(k + 1) * TT],
                        work[:, F_XNT:F_XNT + TT]), inc="sV")
                    em.op("vector", lambda e, k=k: e.tensor_sub(
                        work[:, F_XNT2:F_XNT2 + TT], work[:, F_XNT:F_XNT + TT],
                        xhv[:, k * TT:(k + 1) * TT]), inc="sV")
                    em.op("vector", lambda e, k=k: e.tensor_copy(
                        xlv[:, k * TT:(k + 1) * TT],
                        work[:, F_XNT2:F_XNT2 + TT]), inc="sV")
                em.wait("tensor", "sV")
                gsl_ready, usl_ready, gmm, umm, h2v = [], [], [], [], []

                def dma_g(hc):
                    so = F_GSL + (hc % 2) * 512
                    uo = F_USL + (hc % 2) * 512
                    if hc >= 2:
                        em.wait("gpsimd", "sT", umm[hc - 2])
                    em.op("gpsimd", lambda e, so=so, hc=hc: e.dma_start(
                        out=work[:, so:so + 512].bitcast(BF16).rearrange(
                            "p (k c) -> p k c", c=128),
                        in_=wgT[:, hc * 128:(hc + 1) * 128].rearrange(
                            "(k p) c -> p k c", p=128)), inc="dG", n=16)
                    gsl_ready.append(em.v("dG"))
                    em.op("gpsimd", lambda e, uo=uo, hc=hc: e.dma_start(
                        out=work[:, uo:uo + 512].bitcast(BF16).rearrange(
                            "p (k c) -> p k c", c=128),
                        in_=wuT[:, hc * 128:(hc + 1) * 128].rearrange(
                            "(k p) c -> p k c", p=128)), inc="dG", n=16)
                    usl_ready.append(em.v("dG"))

                em.wait("gpsimd", "sT")
                dma_g(0); dma_g(1)
                for hc in range(24):
                    so = F_GSL + (hc % 2) * 512
                    uo = F_USL + (hc % 2) * 512
                    gb, ub = pb[hc % 2], pb[2 + hc % 2]
                    gslv = work[:, so:so + 512].bitcast(BF16)
                    uslv = work[:, uo:uo + 512].bitcast(BF16)
                    em.wait("tensor", "dG", usl_ready[hc])
                    if hc >= 2:
                        em.wait("tensor", "sV", h2v[hc - 2])
                        em.wait("tensor", "sS")
                    for k in range(NCH):
                        for ph, xv in ((0, xhv), (1, xlv)):
                            em.op("tensor", lambda e, k=k, gb=gb, gslv=gslv,
                                  xv=xv, ph=ph: e.matmul(
                                gb[:, 0:TT], gslv[:, k * 128:(k + 1) * 128],
                                xv[:, k * TT:k * TT + TT],
                                start=(k == 0 and ph == 0),
                                stop=(k == NCH - 1 and ph == 1)),
                                inc=("sT" if (k == NCH - 1 and ph == 1)
                                     else None))
                    gmm.append(em.v("sT"))
                    for k in range(NCH):
                        for ph, xv in ((0, xhv), (1, xlv)):
                            em.op("tensor", lambda e, k=k, ub=ub, uslv=uslv,
                                  xv=xv, ph=ph: e.matmul(
                                ub[:, 0:TT], uslv[:, k * 128:(k + 1) * 128],
                                xv[:, k * TT:k * TT + TT],
                                start=(k == 0 and ph == 0),
                                stop=(k == NCH - 1 and ph == 1)),
                                inc=("sT" if (k == NCH - 1 and ph == 1)
                                     else None))
                    umm.append(em.v("sT"))
                    if hc + 2 < 24:
                        dma_g(hc + 2)
                    go = F_GS + (hc % 2) * TT
                    em.wait("scalar", "sT", gmm[-1])
                    em.op("scalar", lambda e, go=go, gb=gb: e.activation(
                        out=work[:, go:go + TT], in_=gb[:, 0:TT], func=AF.Silu,
                        scale=INV127), inc="sS")
                    ssv = em.v("sS")
                    em.wait("vector", "sT", umm[-1])
                    em.wait("vector", "sS", ssv)
                    em.op("vector", lambda e, go=go, ub=ub:
                          e.scalar_tensor_tensor(
                        work[:, F_H2T:F_H2T + TT], ub[:, 0:TT], INV127,
                        work[:, go:go + TT], op0=OP.mult, op1=OP.mult),
                        inc="sV")
                    em.op("vector", lambda e, hc=hc: e.tensor_copy(
                        h2hv[:, hc * TT:(hc + 1) * TT],
                        work[:, F_H2T:F_H2T + TT]), inc="sV")
                    em.op("vector", lambda e, hc=hc: e.tensor_sub(
                        work[:, F_XNT2:F_XNT2 + TT], work[:, F_H2T:F_H2T + TT],
                        h2hv[:, hc * TT:(hc + 1) * TT]), inc="sV")
                    em.op("vector", lambda e, hc=hc: e.tensor_copy(
                        h2lv[:, hc * TT:(hc + 1) * TT],
                        work[:, F_XNT2:F_XNT2 + TT]), inc="sV")
                    h2v.append(em.v("sV"))
                    scan_pump(5)
                dsl_ready, dmm, resv = [], [], []

                def dma_d(oc):
                    so = F_DSL + (oc % 2) * 1536
                    if oc >= 2:
                        em.wait("gpsimd", "sT", dmm[oc - 2])
                    em.op("gpsimd", lambda e, so=so, oc=oc: e.dma_start(
                        out=work[:, so:so + 1536].bitcast(BF16).rearrange(
                            "p (k c) -> p k c", c=128),
                        in_=wdT[:, oc * 128:(oc + 1) * 128].rearrange(
                            "(k p) c -> p k c", p=128)), inc="dG", n=16)
                    dsl_ready.append(em.v("dG"))

                em.wait("gpsimd", "sT")
                em.wait("gpsimd", "sV")
                dma_d(0); dma_d(1)
                em.wait("tensor", "sV", h2v[-1])
                for oc in range(NCH):
                    so = F_DSL + (oc % 2) * 1536
                    bank = pb[4 if oc % 2 == 0 else 6]
                    dslv = work[:, so:so + 1536].bitcast(BF16)
                    em.wait("tensor", "dG", dsl_ready[oc])
                    if oc >= 2:
                        em.wait("tensor", "sV", resv[oc - 2])
                    for k in range(24):
                        for ph, hv in ((0, h2hv), (1, h2lv)):
                            em.op("tensor", lambda e, k=k, bank=bank, hv=hv,
                                  ph=ph, dslv=dslv: e.matmul(
                                bank[:, 0:TT], dslv[:, k * 128:(k + 1) * 128],
                                hv[:, k * TT:k * TT + TT],
                                start=(k == 0 and ph == 0),
                                stop=(k == 23 and ph == 1)),
                                inc=("sT" if (k == 23 and ph == 1) else None))
                    dmm.append(em.v("sT"))
                    if oc + 2 < NCH:
                        dma_d(oc + 2)
                    em.wait("vector", "sT", dmm[-1])
                    em.op("vector", lambda e, oc=oc, tw=tw, bank=bank:
                          e.scalar_tensor_tensor(
                        strm[:, oc * T + tw:oc * T + tw + TT], bank[:, 0:TT],
                        INV127, strm[:, oc * T + tw:oc * T + tw + TT],
                        op0=OP.mult, op1=OP.add), inc="sV")
                    resv.append(em.v("sV"))
                    scan_pump(5)

        ffn_phase()
        tap_stream("x2")
        if STOP_AFTER == "s2":
            finish()
            return nc

        em.wait("gpsimd", "sV")
        em.op("gpsimd", lambda e: e.dma_start(
            out=x2_scr[:, :].rearrange("(ch p) t -> p ch t", p=128),
            in_=strm[:, :].rearrange("p (ch t) -> p ch t", t=T)), inc="dG", n=16)
        X2_SAVED = em.v("dG")

        C_XN, C_SQ, C_SLAB, C_QTMP = 0, 4096, 5120, 4096
        C_VSL, C_VTMP = 7168, 15360
        C_WDL, C_WIN, C_BVB = C_VTMP + 512, C_VTMP + 640, 17424
        em.wait("sync", "sT")
        em.wait("sync", "sV")
        em.op("sync", lambda e: e.dma_start(
            out=work[:, C_WDL:C_WDL + 128].rearrange("p (k c) -> p k c", c=16),
            in_=Wdl[:, :].rearrange("(k p) c -> p k c", p=128)), inc="dW", n=16)
        em.op("sync", lambda e: e.dma_start(
            out=work[:, C_WIN:C_WIN + 128].rearrange("p (k c) -> p k c", c=16),
            in_=Win[:, :].rearrange("(k p) c -> p k c", p=128)), inc="dW", n=16)
        WDLWIN = em.v("dW")
        em.wait("tensor", "dW", WDLWIN)
        gate_dmas = []
        for tt in range(NTT):
            tw = tt * TT
            ln_stats(strm, tw, T, TT, ST_MUT, ST_RT, C_SQ)
            if tt == 0:
                em.wait("vector", "dG", X2_SAVED)
            ln_apply(strm, tw, T, TT, ST_MUT, ST_RT, strm, tw, T)
            ln_stats(strm, tw, T, TT, ST_MUT, ST_RT, C_SQ)
            ln_apply(strm, tw, T, TT, ST_MUT, ST_RT, work, C_XN, TT,
                     gam="lng", bet="lnb")
            em.wait("tensor", "sV")
            for woff, bname, func, r0, r1 in [
                    (C_WDL, "bdelta", AF.Sigmoid, 0, 16),
                    (C_WIN, "b_in", AF.Identity, 32, 48)]:
                em.wait("tensor", "sS")
                for k in range(NCH):
                    em.op("tensor", lambda e, woff=woff, k=k: e.matmul(
                        pb[6][0:16, 0:TT],
                        work[:, woff + k * 16:woff + (k + 1) * 16],
                        work[:, C_XN + k * TT:C_XN + k * TT + TT],
                        start=(k == 0), stop=(k == NCH - 1)),
                        inc=("sT" if k == NCH - 1 else None))
                mmv = em.v("sT")
                em.wait("scalar", "sT", mmv)
                em.op("scalar", lambda e, bname=bname, func=func, r0=r0, r1=r1:
                      e.activation(
                    out=work[r0:r1, C_VTMP:C_VTMP + TT],
                    in_=pb[6][0:16, 0:TT], func=func,
                    bias=bp[0:16, BP[bname]:BP[bname] + 1]), inc="sS")
            dm_done = em.v("sS")
            em.wait("gpsimd", "sS", dm_done)
            em.op("gpsimd", lambda e, tt=tt: e.dma_start(
                out=m_i[:, tt * TT:(tt + 1) * TT],
                in_=work[0:48, C_VTMP:C_VTMP + TT]), inc="dG", n=16)

            def evg(oc, bank, mmv):
                po = C_QTMP + (oc % 2) * TT
                em.wait("scalar", "sT", mmv)
                if len(gate_dmas) >= 2:
                    em.wait("scalar", "dG", gate_dmas[-2])
                else:
                    em.wait("scalar", "dG")
                em.op("scalar", lambda e, po=po, oc=oc, bank=bank: e.activation(
                    out=work[:, po:po + TT], in_=bank[:, 0:TT], func=AF.Sigmoid,
                    bias=bpc("bgate", oc)), inc="sS")
                sv = em.v("sS")
                em.wait("gpsimd", "sS", sv)
                em.op("gpsimd", lambda e, po=po, oc=oc, tt=tt: e.dma_start(
                    out=gate_scr[oc * 128:(oc + 1) * 128, tt * TT:(tt + 1) * TT],
                    in_=work[:, po:po + TT]), inc="dG", n=16)
                gate_dmas.append(em.v("dG"))
                return ("sS", sv)
            proj(Wgt, NCH, C_XN, TT, C_SLAB, NCH, evg)

        em.wait("gpsimd", "dG")
        em.op("gpsimd", lambda e: e.collective_compute(
            "AllGather", OP.bypass, ins=[m_i[:, :]], outs=[m_o[:, :, :]],
            replica_groups=PAIR_RG), inc="sC")
        AG_M = em.v("sC")
        em.wait("sync", "sC", AG_M)
        for r in range(2):
            em.op("sync", lambda e, r=r: e.dma_start(
                out=dfull[:, :].rearrange("p (g c) -> p g c", c=128)[:, r::2, :],
                in_=m_o[r, 0:16, :].rearrange("p (g c) -> p g c", c=128)),
                inc="dW", n=16)
            em.op("sync", lambda e, r=r: e.dma_start(
                out=btfull[:, :].rearrange("p (g c) -> p g c", c=128)[:, r::2, :],
                in_=m_o[r, 32:48, :].rearrange("p (g c) -> p g c", c=128)),
                inc="dW", n=16)
        MB_READY = em.v("dW")
        em.wait("vector", "dW", MB_READY)
        em.op("vector", lambda e: e.memset(hall[:, 0:1], 0.0), inc="sV")
        HALL0 = em.v("sV")
        em.wait("scalar", "sV", HALL0)
        em.wait("scalar", "dW", MB_READY)
        em.wait("tensor", "dW", MB_READY)
        scan_state["ready"] = True
        if not SCAN_INTERLEAVE:
            _scan_emit(NSCAN)

        em.wait("sync", "sS")
        em.wait("sync", "sV")
        em.op("sync", lambda e: e.dma_start(out=srow(ST_BV, 0, 1024),
              in_=rp_in[0:1, 0:1024]), inc="dW", n=16)
        RP_LD = em.v("dW")
        em.wait("tensor", "dW", RP_LD)
        em.wait("tensor", "sV")
        em.wait("tensor", "sS")
        em.op("tensor", lambda e: e.matmul(
            pb[6][:, 0:512], ones[64:65, :], srow(ST_BV, 0, 512), start=True,
            stop=True), inc="sT")
        em.op("tensor", lambda e: e.matmul(
            pb[7][:, 0:512], ones[64:65, :], srow(ST_BV, 512, 512), start=True,
            stop=True), inc="sT")
        BVB_MM = em.v("sT")
        em.wait("vector", "sT", BVB_MM)
        em.op("vector", lambda e: e.tensor_copy(
            work[:, C_BVB:C_BVB + 512], pb[6][:, 0:512]), inc="sV")
        em.op("vector", lambda e: e.tensor_copy(
            work[:, C_BVB + 512:C_BVB + 1024], pb[7][:, 0:512]), inc="sV")
        BVB_DONE = em.v("sV")

        qbdmas, kbdmas = [], []
        for tt in range(NTT):
            tw = tt * TT
            ln_stats(strm, tw, T, TT, ST_MUT, ST_RT, C_SQ)
            ln_apply(strm, tw, T, TT, ST_MUT, ST_RT, work, C_XN, TT,
                     gam="ln1g", bet="ln1b")
            em.wait("tensor", "sV")

            def evqb(oc, bank, mmv):
                po = C_QTMP + (oc % 2) * TT
                em.wait("vector", "sT", mmv)
                if len(qbdmas) >= 2:
                    em.wait("vector", "dG", qbdmas[-2])
                else:
                    em.wait("vector", "dG")
                em.op("vector", lambda e, po=po, oc=oc, bank=bank:
                      e.tensor_scalar(
                    work[:, po:po + TT], bank[:, 0:TT], bpc("bq", oc), QSC_B,
                    op0=OP.add, op1=OP.mult), inc="sV")
                sv = em.v("sV")
                em.wait("gpsimd", "sV", sv)
                em.op("gpsimd", lambda e, po=po, oc=oc, tt=tt: e.dma_start(
                    out=qb_scr[oc * 128:(oc + 1) * 128, tt * TT:(tt + 1) * TT],
                    in_=work[:, po:po + TT]), inc="dG", n=16)
                qbdmas.append(em.v("dG"))
                return ("sV", sv)
            proj(Wq, NCH, C_XN, TT, C_SLAB, NCH, evqb, scan_steps=6)

            def evkb(oc, bank, mmv):
                po = C_QTMP + (oc % 2) * TT
                em.wait("vector", "sT", mmv)
                if len(kbdmas) >= 2:
                    em.wait("vector", "dG", kbdmas[-2])
                else:
                    em.wait("vector", "dG")
                em.op("vector", lambda e, po=po, oc=oc, bank=bank:
                      e.tensor_scalar_add(
                    work[:, po:po + TT], bank[:, 0:TT], bpc("bk", oc)),
                    inc="sV")
                sv = em.v("sV")
                em.wait("gpsimd", "sV", sv)
                em.op("gpsimd", lambda e, po=po, oc=oc, tt=tt: e.dma_start(
                    out=kb_i[oc * 128:(oc + 1) * 128, tt * TT:(tt + 1) * TT],
                    in_=work[:, po:po + TT]), inc="dG", n=16)
                kbdmas.append(em.v("dG"))
                return ("sV", sv)
            proj(Wk, NCH, C_XN, TT, C_SLAB, NCH, evkb, scan_steps=6)

            em.wait("vector", "sV", BVB_DONE)
            vs_ready = []
            for dvg in range(2):
                so = C_VSL + dvg * 4096
                em.wait("sync", "sT")
                em.op("sync", lambda e, so=so, dvg=dvg: e.dma_start(
                    out=work[:, so:so + 4096].rearrange(
                        "p (k c) -> p k c", c=512),
                    in_=Wv[:, dvg * 512:(dvg + 1) * 512].rearrange(
                        "(k p) c -> p k c", p=128)), inc="dW", n=16)
                vs_ready.append(em.v("dW"))
            vdma = []
            vev = {}
            em.wait("tensor", "sV")
            for sub in range(4):
                vo = C_VTMP + (sub % 2) * 1032
                if sub >= 2:
                    em.wait("vector", "dG", vdma[sub - 2])
                else:
                    em.wait("vector", "dG")
                for dvg in range(2):
                    so = C_VSL + dvg * 4096
                    bank = pb[2 + dvg]
                    em.wait("tensor", "dW", vs_ready[dvg])
                    if (sub - 1, dvg) in vev:
                        em.wait("tensor", "sV", vev[(sub - 1, dvg)])
                    for k in range(NCH):
                        em.op("tensor", lambda e, so=so, k=k, sub=sub, bank=bank:
                              e.matmul(
                            bank[:, 0:512],
                            work[:, C_XN + k * TT + sub * 128:
                                  C_XN + k * TT + sub * 128 + 128],
                            work[:, so + k * 512:so + (k + 1) * 512],
                            start=(k == 0), stop=(k == NCH - 1)),
                            inc=("sT" if k == NCH - 1 else None))
                    mmv = em.v("sT")
                    em.wait("vector", "sT", mmv)
                    em.op("vector", lambda e, vo=vo, dvg=dvg, bank=bank:
                          e.tensor_add(
                        work[:, vo + dvg * 516:vo + dvg * 516 + 516].rearrange(
                            "p (h c) -> p h c", c=129)[:, :, 0:128],
                        bank[:, 0:512].rearrange("p (h c) -> p h c", c=128),
                        work[:, C_BVB + dvg * 512:C_BVB + (dvg + 1) * 512]
                        .rearrange("p (h c) -> p h c", c=128)), inc="sV")
                    vev[(sub, dvg)] = em.v("sV")
                em.op("vector", lambda e, vo=vo: e.memset(
                    work[:, vo:vo + 1032].rearrange("p (h c) -> p h c", c=129)
                    [:, :, 128:129], 1.0), inc="sV")
                sv = em.v("sV")
                em.wait("gpsimd", "sV", sv)
                em.op("gpsimd", lambda e, vo=vo, tt=tt, sub=sub: e.dma_start(
                    out=vb_i[tt * TT + sub * 128:tt * TT + sub * 128 + 128, :],
                    in_=work[:, vo:vo + 1032]), inc="dG", n=16)
                vdma.append(em.v("dG"))
                scan_pump(8)

        em.wait("gpsimd", "dG")
        em.op("gpsimd", lambda e: e.collective_compute(
            "AllGather", OP.bypass, ins=[kb_i[:, :]], outs=[kb_o[:, :, :]],
            replica_groups=PAIR_RG), inc="sC")
        em.op("gpsimd", lambda e: e.collective_compute(
            "AllGather", OP.bypass, ins=[vb_i[:, :]], outs=[vb_o[:, :, :]],
            replica_groups=PAIR_RG), inc="sC")
        AG_B = em.v("sC")

        WB_K, WB_V, WB_QB, WB_ES, WB_RCP, WB_CTH = 0, 4096, 8224, 9248, 11296, 11392
        em.wait("sync", "sC", AG_B)
        qbt_dma, qbt_used = [], []
        avb_evac = {}
        ctxb_dmas = []
        for h in range(8):
            em.wait("sync", "sT")
            for r in range(2):
                em.op("sync", lambda e, r=r, h=h: e.dma_start(
                    out=work[:, WB_K + r * 2048:WB_K + r * 2048 + 2048],
                    in_=kb_o[r, h * 128:(h + 1) * 128, :]), inc="dW", n=16)
                em.op("sync", lambda e, r=r, h=h: e.dma_start(
                    out=work[:, WB_V + r * 2064:WB_V + r * 2064 + 2064]
                    .rearrange("p (g c) -> p g c", c=129),
                    in_=vb_o[r, :, h * 129:(h + 1) * 129].rearrange(
                        "(g p) c -> p g c", p=128)), inc="dW", n=16)
            kv_ready = em.v("dW")
            em.wait("tensor", "dW", kv_ready)
            for qt in range(NTT):
                qo = WB_QB + (qt % 2) * TT
                if len(qbt_dma) >= 2:
                    em.wait("sync", "sT", qbt_used[-2])
                em.op("sync", lambda e, qo=qo, h=h, qt=qt: e.dma_start(
                    out=work[:, qo:qo + TT],
                    in_=qb_scr[h * 128:(h + 1) * 128, qt * TT:(qt + 1) * TT]),
                    inc="dW", n=16)
                qbt_dma.append(em.v("dW"))
                em.wait("tensor", "dW", qbt_dma[-1])
                em.wait("tensor", "sS")
                if avb_evac:
                    em.wait("tensor", "sV", max(avb_evac.values()))
                av0, av1 = pb[4], pb[6]
                exp_done = []
                for g in range(32):
                    par = g % 2
                    sb = pb[par]
                    if g >= 2:
                        em.wait("tensor", "sS", exp_done[g - 2])
                    em.op("tensor", lambda e, g=g, sb=sb, qo=qo: e.matmul(
                        sb[:, 0:TT],
                        work[:, WB_K + g * 128:WB_K + (g + 1) * 128],
                        work[:, qo:qo + TT], start=True, stop=True), inc="sT")
                    smm = em.v("sT")
                    em.wait("scalar", "sT", smm)
                    eo = WB_ES + par * TT
                    em.op("scalar", lambda e, sb=sb, eo=eo: e.activation(
                        out=work[:, eo:eo + TT], in_=sb[:, 0:TT], func=AF.Exp,
                        bias=-SHIFT_B), inc="sS")
                    exp_done.append(em.v("sS"))
                    em.wait("tensor", "sS", exp_done[-1])
                    for sq in range(4):
                        avb = av0 if sq < 2 else av1
                        em.op("tensor", lambda e, eo=eo, g=g, sq=sq, avb=avb:
                              e.matmul(
                            avb[:, (sq % 2) * 129:(sq % 2) * 129 + 129],
                            work[:, eo + sq * 128:eo + (sq + 1) * 128],
                            work[:, WB_V + g * 129:WB_V + (g + 1) * 129],
                            start=(g == 0 and sq % 2 == 0),
                            stop=(g == 31 and sq % 2 == 1)),
                            inc=("sT" if (g == 31 and sq == 3) else None))
                    scan_pump(1)
                av_mm = em.v("sT")
                em.wait("vector", "sT", av_mm)
                if qt == 0 and ctxb_dmas:
                    em.wait("vector", "dG", ctxb_dmas[-1])
                rb = WB_RCP + (qt % 2) * 16
                em.wait("scalar", "sT", av_mm)
                if (qt % 2) in avb_evac:
                    em.wait("scalar", "sV", avb_evac[qt % 2])
                for sq in range(4):
                    avb = av0 if sq < 2 else av1
                    em.op("scalar", lambda e, sq=sq, avb=avb, rb=rb:
                          e.activation(
                        out=work[:, rb + 8 + sq:rb + 9 + sq],
                        in_=avb[:, (sq % 2) * 129 + 128:(sq % 2) * 129 + 129],
                        func=AF.Ln), inc="sS")
                    em.op("scalar", lambda e, sq=sq, rb=rb: e.activation(
                        out=work[:, rb + sq:rb + sq + 1],
                        in_=work[:, rb + 8 + sq:rb + 9 + sq],
                        func=AF.Exp, scale=-1.0), inc="sS")
                em.wait("vector", "sS")
                for sq in range(4):
                    avb = av0 if sq < 2 else av1
                    em.op("vector", lambda e, sq=sq, avb=avb, qt=qt, rb=rb:
                          e.tensor_scalar_mul(
                        work[:, WB_CTH + (qt * 4 + sq) * 128:
                              WB_CTH + (qt * 4 + sq) * 128 + 128],
                        avb[:, (sq % 2) * 129:(sq % 2) * 129 + 128],
                        work[:, rb + sq:rb + sq + 1]), inc="sV")
                avb_evac[qt % 2] = em.v("sV")
                qbt_used.append(em.v("sT"))
                bar()
            em.wait("gpsimd", "sV")
            em.op("gpsimd", lambda e, h=h: e.dma_start(
                out=ctxb_scr[:, h * 128:(h + 1) * 128].rearrange(
                    "(i p) c -> p i c", p=128),
                in_=work[:, WB_CTH:WB_CTH + 2048].rearrange(
                    "p (i c) -> p i c", c=128)), inc="dG", n=16)
            ctxb_dmas.append(em.v("dG"))
        CTXB_SAVED = em.v("dG")

        WB_WO, WB_CTI, WB_FM, WB_X2P, WB_SQ2 = 0, 8192, 12288, 16384, 20480
        em.wait("sync", "sT")
        for oc in range(NCH):
            em.op("sync", lambda e, oc=oc: e.dma_start(
                out=work[:, WB_WO + oc * 1024:WB_WO + (oc + 1) * 1024]
                .rearrange("p (k c) -> p k c", c=128),
                in_=Wo[:, oc * 128:(oc + 1) * 128].rearrange(
                    "(k p) c -> p k c", p=128)), inc="dW", n=16)
        WOB_READY = em.v("dW")
        em.wait("sync", "dG", CTXB_SAVED)
        em.wait("tensor", "dW", WOB_READY)
        x2p_dmas = []
        for qt in range(NTT):
            em.wait("sync", "sT")
            em.op("sync", lambda e, qt=qt: e.dma_start(
                out=work[:, WB_CTI:WB_CTI + 4096].rearrange(
                    "p (s c) -> p s c", c=1024),
                in_=ctxb_scr[qt * 512:(qt + 1) * 512, :].rearrange(
                    "(s p) c -> p s c", p=128)), inc="dW", n=16)
            cti_ld = em.v("dW")
            em.wait("tensor", "dW", cti_ld)
            em.wait("vector", "sS")
            tr_done = []
            for blk in range(4 * NCH):
                sq, ec = blk // NCH, blk % NCH
                bank = pb[blk % 2]
                if blk >= 2:
                    em.wait("tensor", "sV", tr_done[blk - 2])
                em.op("tensor", lambda e, sq=sq, ec=ec, bank=bank:
                      e.transpose(
                    bank[:, 0:128],
                    work[:, WB_CTI + sq * 1024 + ec * 128:
                          WB_CTI + sq * 1024 + (ec + 1) * 128],
                    ident[:, :]), inc="sT")
                em.wait("vector", "sT")
                em.op("vector", lambda e, sq=sq, ec=ec, bank=bank:
                      e.tensor_copy(
                    work[:, WB_FM + ec * TT + sq * 128:
                          WB_FM + ec * TT + (sq + 1) * 128], bank[:, 0:128]),
                    inc="sV")
                tr_done.append(em.v("sV"))
            em.wait("tensor", "sV", tr_done[-1])
            x2p_off = WB_X2P
            opv = []
            for oc in range(NCH):
                bank = pb[2 + oc % 2]
                if oc >= 2:
                    em.wait("tensor", "sV", opv[oc - 2])
                for k in range(NCH):
                    em.op("tensor", lambda e, oc=oc, k=k, bank=bank: e.matmul(
                        bank[:, 0:TT],
                        work[:, WB_WO + oc * 1024 + k * 128:
                              WB_WO + oc * 1024 + (k + 1) * 128],
                        work[:, WB_FM + k * TT:WB_FM + (k + 1) * TT],
                        start=(k == 0), stop=(k == NCH - 1)),
                        inc=("sT" if k == NCH - 1 else None))
                mmv = em.v("sT")
                em.wait("vector", "sT", mmv)
                em.op("vector", lambda e, oc=oc, qt=qt, bank=bank,
                      x2p_off=x2p_off: e.scalar_tensor_tensor(
                    work[:, x2p_off + oc * TT:x2p_off + (oc + 1) * TT],
                    bank[:, 0:TT], bpc("bo", oc),
                    strm[:, oc * T + qt * TT:oc * T + (qt + 1) * TT],
                    op0=OP.add, op1=OP.add), inc="sV")
                opv.append(em.v("sV"))
            scan_pump(4)
            ln_stats(work, x2p_off, TT, TT, sl(ST_MU2X, qt * TT),
                     sl(ST_R2X, qt * TT), WB_SQ2)
            em.wait("gpsimd", "sV")
            em.wait("gpsimd", "sS")
            em.op("gpsimd", lambda e, x2p_off=x2p_off, qt=qt: e.dma_start(
                out=x2p_scr[:, qt * TT:(qt + 1) * TT].rearrange(
                    "(ch p) t -> p ch t", p=128),
                in_=work[:, x2p_off:x2p_off + 4096].rearrange(
                    "p (ch t) -> p ch t", t=TT)), inc="dG", n=16)
            x2p_dmas.append(em.v("dG"))
            bar()
        X2P_SAVED = em.v("dG")


        H_X2P, H_XN2, H_HB, H_W1S = 0, 2048, 4096, 12288
        H_W2S, H_TOUT = 14336, 18432
        H_MB, H_GT, H_WFS, H_FP, H_SQF = 4096, 6144, 8192, 12288, 14336
        TB = 256
        for t2 in range(8):
            tw2 = t2 * TB
            qt_of = t2 // 2
            em.wait("sync", "dG", X2P_SAVED)
            em.wait("sync", "sV")
            em.op("sync", lambda e, t2=t2: e.dma_start(
                out=work[:, H_X2P:H_X2P + 2048].rearrange(
                    "p (ch t) -> p ch t", t=TB),
                in_=x2p_scr[:, t2 * TB:(t2 + 1) * TB].rearrange(
                    "(ch p) t -> p ch t", p=128)), inc="dW", n=16)
            x2p_ld = em.v("dW")
            em.wait("vector", "dW", x2p_ld)
            ln_apply(work, H_X2P, TB, TB, sl(ST_MU2X, tw2), sl(ST_R2X, tw2),
                     work, H_XN2, TB, gam="ln2g", bet="ln2b")
            em.wait("tensor", "sV")

            def evw1(hc, bank, mmv):
                em.wait("scalar", "sT", mmv)
                em.op("scalar", lambda e, hc=hc, bank=bank: e.activation(
                    out=work[:, H_HB + hc * TB:H_HB + (hc + 1) * TB],
                    in_=bank[:, 0:TB], func=AF.Relu, bias=bpc("b1", hc)),
                    inc="sS")
                return ("sS", em.v("sS"))
            proj(W1, 32, H_XN2, TB, H_W1S, NCH, evw1, scan_steps=6)
            em.wait("tensor", "sS")

            w2_ready, w2mm, toutv = [], [], []

            def dma_w2(idx):
                so = H_W2S + (idx % 2) * 2048
                oc, hh = idx // 2, idx % 2
                if idx >= 2:
                    em.wait("sync", "sT", w2mm[idx - 2])
                em.op("sync", lambda e, so=so, oc=oc, hh=hh: e.dma_start(
                    out=work[:, so:so + 2048].rearrange(
                        "p (k c) -> p k c", c=128),
                    in_=W2[hh * 2048:(hh + 1) * 2048,
                           oc * 128:(oc + 1) * 128].rearrange(
                        "(k p) c -> p k c", p=128)), inc="dW", n=16)
                w2_ready.append(em.v("dW"))

            em.wait("sync", "sT")
            dma_w2(0); dma_w2(1)
            for oc in range(NCH):
                bank = pb[4 if oc % 2 == 0 else 6]
                if oc >= 2:
                    em.wait("tensor", "sV", toutv[oc - 2])
                for hh in range(2):
                    idx = oc * 2 + hh
                    so = H_W2S + (idx % 2) * 2048
                    em.wait("tensor", "dW", w2_ready[idx])
                    for k in range(16):
                        kk = hh * 16 + k
                        em.op("tensor", lambda e, so=so, k=k, kk=kk, bank=bank:
                              e.matmul(
                            bank[:, 0:TB],
                            work[:, so + k * 128:so + (k + 1) * 128],
                            work[:, H_HB + kk * TB:H_HB + kk * TB + TB],
                            start=(kk == 0), stop=(kk == 31)),
                            inc=("sT" if k == 15 else None))
                    w2mm.append(em.v("sT"))
                    if idx + 2 < 16:
                        dma_w2(idx + 2)
                mmv = em.cnt["sT"]
                em.wait("vector", "sT", mmv)
                em.op("vector", lambda e, oc=oc, bank=bank:
                      e.scalar_tensor_tensor(
                    work[:, H_TOUT + oc * TB:H_TOUT + (oc + 1) * TB],
                    bank[:, 0:TB], bpc("b2", oc),
                    work[:, H_X2P + oc * TB:H_X2P + (oc + 1) * TB],
                    op0=OP.add, op1=OP.add), inc="sV")
                toutv.append(em.v("sV"))
                scan_pump(6)
            em.wait("sync", "sT")
            em.op("sync", lambda e, t2=t2: e.dma_start(
                out=work[:, H_GT:H_GT + 2048].rearrange(
                    "p (ch t) -> p ch t", t=TB),
                in_=gate_scr[:, t2 * TB:(t2 + 1) * TB].rearrange(
                    "(ch p) t -> p ch t", p=128)), inc="dW", n=16)
            em.op("sync", lambda e: e.dma_start(
                out=work[0:16, H_SQF + 512:H_SQF + 1536], in_=Cmat[:, :]),
                inc="dW", n=16)
            gt_ld = em.v("dW")
            need = (4 * t2 + 4) * 128
            if scan_state["next"] < need:
                _scan_emit(need - scan_state["next"])
            em.wait("tensor", "zS", 2 * need)
            em.wait("tensor", "dW", gt_ld)
            em.wait("tensor", "sV")
            mbv = []
            for oc in range(NCH):
                bE, bO = pb[7], pb[4]
                if oc >= 1:
                    em.wait("tensor", "sV", mbv[-1])
                for par, bank in ((0, bE), (1, bO)):
                    for blk in range(2):
                        g = 2 * (2 * t2 + blk) + par
                        em.op("tensor", lambda e, oc=oc, g=g, blk=blk,
                              bank=bank: e.matmul(
                            bank[:, blk * 128:(blk + 1) * 128],
                            work[0:16, H_SQF + 512 + oc * 128:
                                 H_SQF + 512 + (oc + 1) * 128],
                            hall[:, 1 + g * 128:1 + (g + 1) * 128],
                            start=(blk == 0), stop=(blk == 1)),
                            inc=("sT" if (par == 1 and blk == 1) else None))
                mmv = em.v("sT")
                em.wait("vector", "sT", mmv)
                em.op("vector", lambda e, oc=oc: e.tensor_scalar_mul(
                    work[:, H_MB + oc * TB:H_MB + (oc + 1) * TB],
                    pb[7][:, 0:TB], bpc("selE")), inc="sV")
                em.op("vector", lambda e, oc=oc: e.scalar_tensor_tensor(
                    work[:, H_MB + oc * TB:H_MB + (oc + 1) * TB],
                    pb[4][:, 0:TB], bpc("selO"),
                    work[:, H_MB + oc * TB:H_MB + (oc + 1) * TB],
                    op0=OP.mult, op1=OP.add), inc="sV")
                em.op("vector", lambda e, oc=oc: e.tensor_mul(
                    work[:, H_MB + oc * TB:H_MB + (oc + 1) * TB],
                    work[:, H_MB + oc * TB:H_MB + (oc + 1) * TB],
                    work[:, H_GT + oc * TB:H_GT + (oc + 1) * TB]), inc="sV")
                em.op("vector", lambda e, oc=oc, t2=t2: e.tensor_add(
                    work[:, H_MB + oc * TB:H_MB + (oc + 1) * TB],
                    work[:, H_MB + oc * TB:H_MB + (oc + 1) * TB],
                    strm[:, oc * T + t2 * TB:oc * T + (t2 + 1) * TB]),
                    inc="sV")
                mbv.append(em.v("sV"))
            MB_DONE = em.v("sV")

            wf_ready, wfmm, fpv = [], [], []

            def dma_wf(oc):
                so = H_WFS + (oc % 2) * 2048
                if oc >= 2:
                    em.wait("sync", "sT", wfmm[oc - 2])
                em.op("sync", lambda e, so=so, oc=oc: e.dma_start(
                    out=work[:, so:so + 2048].rearrange(
                        "p (k c) -> p k c", c=128),
                    in_=Wf[:, oc * 128:(oc + 1) * 128].rearrange(
                        "(k p) c -> p k c", p=128)), inc="dW", n=16)
                wf_ready.append(em.v("dW"))

            em.wait("sync", "sT")
            dma_wf(0); dma_wf(1)
            em.wait("tensor", "sV", MB_DONE)
            for oc in range(NCH):
                so = H_WFS + (oc % 2) * 2048
                bank = pb[oc % 2]
                em.wait("tensor", "dW", wf_ready[oc])
                if oc >= 2:
                    em.wait("tensor", "sV", fpv[oc - 2])
                for k in range(16):
                    src = H_MB if k < 8 else H_TOUT
                    kk = k if k < 8 else k - 8
                    em.op("tensor", lambda e, so=so, k=k, kk=kk, src=src,
                          bank=bank: e.matmul(
                        bank[:, 0:TB],
                        work[:, so + k * 128:so + (k + 1) * 128],
                        work[:, src + kk * TB:src + kk * TB + TB],
                        start=(k == 0), stop=(k == 15)),
                        inc=("sT" if k == 15 else None))
                mmv = em.v("sT")
                wfmm.append(mmv)
                if oc + 2 < NCH:
                    dma_wf(oc + 2)
                em.wait("vector", "sT", mmv)
                em.op("vector", lambda e, oc=oc, bank=bank: e.tensor_scalar_add(
                    work[:, H_FP + oc * TB:H_FP + (oc + 1) * TB],
                    bank[:, 0:TB], bpc("bf", oc)), inc="sV")
                fpv.append(em.v("sV"))
            ln_stats(work, H_FP, TB, TB, ST_MUT, ST_RT, H_SQF)
            ln_apply(work, H_FP, TB, TB, ST_MUT, ST_RT, work, H_FP, TB,
                     gam="lnfg", bet="lnfb")
            FUSED_DONE = em.v("sV")
            em.wait("sync", "sV", FUSED_DONE)
            em.op("sync", lambda e, t2=t2: e.dma_start(
                out=work[:, H_XN2:H_XN2 + 2048].rearrange(
                    "p (ch t) -> p ch t", t=TB),
                in_=x2_scr[:, t2 * TB:(t2 + 1) * TB].rearrange(
                    "(ch p) t -> p ch t", p=128)), inc="dW", n=16)
            x2_ld = em.v("dW")
            em.wait("vector", "dW", x2_ld)
            for oc in range(NCH):
                em.op("vector", lambda e, oc=oc, t2=t2: e.tensor_add(
                    strm[:, oc * T + t2 * TB:oc * T + (t2 + 1) * TB],
                    work[:, H_FP + oc * TB:H_FP + (oc + 1) * TB],
                    work[:, H_XN2 + oc * TB:H_XN2 + (oc + 1) * TB]), inc="sV")
            bar()

        tap_stream("x3")
        if STOP_AFTER == "s3":
            finish()
            return nc

        ffn_phase()
        finish()
    return nc


def _pack_col(vec):
    return np.ascontiguousarray(vec.reshape(-1, 128).T.astype(np.float32))


def kernel(L0_attn_a_x, L0_attn_a_ln_gamma, L0_attn_a_ln_beta, L0_attn_a_w_qkv,
           L0_attn_a_w_out, L0_attn_a_b_out, L0_attn_b_ln_g, L0_attn_b_ln_b,
           L0_attn_b_Wdelta, L0_attn_b_bdelta, L0_attn_b_Win, L0_attn_b_b_in,
           L0_attn_b_Wgate, L0_attn_b_bgate, L0_attn_b_A, L0_attn_b_C,
           L0_attn_b_ln1_g, L0_attn_b_ln1_b, L0_attn_b_Wq, L0_attn_b_bq,
           L0_attn_b_Wk, L0_attn_b_bk, L0_attn_b_Wv, L0_attn_b_bv,
           L0_attn_b_Wo, L0_attn_b_bo, L0_attn_b_ln2_g, L0_attn_b_ln2_b,
           L0_attn_b_W1, L0_attn_b_b1, L0_attn_b_W2, L0_attn_b_b2,
           L0_attn_b_Wf, L0_attn_b_bf, L0_attn_b_lnf_g, L0_attn_b_lnf_b,
           L0_ffn_w_gate, L0_ffn_w_up, L0_ffn_w_down):
    x = np.asarray(L0_attn_a_x)
    w_qkv = np.asarray(L0_attn_a_w_qkv)
    wqT = np.ascontiguousarray(w_qkv[0:1024].T)
    wkT = np.ascontiguousarray(w_qkv[1024:2048].T)
    wvT = np.ascontiguousarray(w_qkv[2048:3072].T)
    woT = np.ascontiguousarray(np.asarray(L0_attn_a_w_out).T)
    wgT = np.ascontiguousarray(np.asarray(L0_ffn_w_gate).T)
    wuT = np.ascontiguousarray(np.asarray(L0_ffn_w_up).T)
    wdT = np.ascontiguousarray(np.asarray(L0_ffn_w_down).T)

    shared = {
        "wqT": wqT, "wkT": wkT, "wvT": wvT, "woT": woT,
        "wgT": wgT, "wuT": wuT, "wdT": wdT,
        "Wdl": np.ascontiguousarray(L0_attn_b_Wdelta),
        "Win": np.ascontiguousarray(L0_attn_b_Win),
        "Wgt": np.ascontiguousarray(L0_attn_b_Wgate),
        "Amat": np.ascontiguousarray(L0_attn_b_A),
        "Cmat": np.ascontiguousarray(L0_attn_b_C),
        "Wq": np.ascontiguousarray(L0_attn_b_Wq),
        "Wk": np.ascontiguousarray(L0_attn_b_Wk),
        "Wv": np.ascontiguousarray(L0_attn_b_Wv),
        "Wo": np.ascontiguousarray(L0_attn_b_Wo),
        "W1": np.ascontiguousarray(L0_attn_b_W1),
        "W2": np.ascontiguousarray(L0_attn_b_W2),
        "Wf": np.ascontiguousarray(L0_attn_b_Wf),
        "rpv": np.ascontiguousarray(np.asarray(L0_attn_b_bv)[None, :]),
        "identv": np.eye(128, dtype=np.float32),
        "onesv": np.ones((128, 128), np.float32),
    }

    diag = np.where(np.arange(128)[:, None] > np.arange(128)[None, :],
                    np.float32(NEG), np.float32(0.0)).astype(np.float32)
    full = np.full((128, 128), np.float32(NEG), np.float32)
    zero = np.zeros((128, 128), np.float32)

    in_maps = []
    for c in range(8):
        b, p = c // 2, c % 2
        bp_arr = np.zeros((128, NBP), np.float32)

        def put(name, vec):
            v = np.asarray(vec).astype(np.float32)
            n = v.size // 128
            bp_arr[:, BP[name]:BP[name] + n] = v.reshape(n, 128).T

        put("bout_a", L0_attn_a_b_out)
        put("gam_a", L0_attn_a_ln_gamma)
        put("bet_a", L0_attn_a_ln_beta)
        put("lng", L0_attn_b_ln_g); put("lnb", L0_attn_b_ln_b)
        put("ln1g", L0_attn_b_ln1_g); put("ln1b", L0_attn_b_ln1_b)
        put("ln2g", L0_attn_b_ln2_g); put("ln2b", L0_attn_b_ln2_b)
        put("lnfg", L0_attn_b_lnf_g); put("lnfb", L0_attn_b_lnf_b)
        put("bgate", L0_attn_b_bgate)
        put("bq", L0_attn_b_bq); put("bk", L0_attn_b_bk)
        put("bo", L0_attn_b_bo)
        put("b1", L0_attn_b_b1); put("b2", L0_attn_b_b2)
        put("bf", L0_attn_b_bf)
        bp_arr[0:16, BP["bdelta"]] = np.asarray(L0_attn_b_bdelta)
        bp_arr[0:16, BP["b_in"]] = np.asarray(L0_attn_b_b_in)
        bp_arr[:, BP["selE"]] = 1.0 if p == 0 else 0.0
        bp_arr[:, BP["selO"]] = 1.0 if p == 1 else 0.0
        bp_arr[:, BP["c_eps"]] = EPS
        bp_arr[:, BP["c_nsha"]] = -SHIFT_A
        bp_arr[:, BP["c_nshb"]] = -SHIFT_B
        bp_arr[:, BP["c_inv127"]] = INV127
        bp_arr[:, BP["c_i1024"]] = 1.0 / 1024.0
        bp_arr[:, BP["c_neg1"]] = -1.0

        xb = x[b].reshape(32, 128, D)[p::2].reshape(T, D)
        m = dict(shared)
        m["x0"] = np.ascontiguousarray(xb.T)
        m["bpv"] = bp_arr
        m["m0v"] = diag if p == 0 else zero
        m["m1v"] = full if p == 0 else diag
        in_maps.append(m)

    nc = build_nc()
    res = run_bass_kernel_spmd(nc, in_maps, core_ids=list(range(8)))

    out_full = np.empty((B, S, D), np.float32)
    for c in range(8):
        b, p = c // 2, c % 2
        sh = res.results[c]["out"]
        out_full[b].reshape(32, 128, D)[p::2] = sh.T.reshape(16, 128, D)
    if TAPS:
        kernel.tap_results = []
        for c in range(8):
            kernel.tap_results.append(
                {t: res.results[c]["tap_" + t] for t in TAPS})
    return out_full
